# version 05
# baseline (speedup 1.0000x reference)
import contextlib
import numpy as np
import ml_dtypes
import concourse.bass as bass
import concourse.mybir as mybir
from concourse.bass_utils import run_bass_kernel_spmd

F32 = mybir.dt.float32
BF16 = mybir.dt.bfloat16
I32 = mybir.dt.int32
AF = mybir.ActivationFunctionType
ALU = mybir.AluOpType
AX = mybir.AxisListType
NPBF16 = ml_dtypes.bfloat16

N = 4096
DM = 4096
NV = 4352
E0 = 1792
EXTN = 2560
OWN0 = 256
CH = 128
HALO = 256


ENGS = ("T", "V", "S", "P", "Q")
DMA_LANES = {"Q": 8, "P": 4}


class SemPool:

    def __init__(self, nc, stack):
        self.eng_sem = {e: stack.enter_context(nc.semaphore(f"sp_{e}"))
                        for e in ("T", "V", "S", "P")}
        self.eng_cnt = {e: 0 for e in ("T", "V", "S", "P")}
        self.lanes = {}
        self.lane_cnt = {}
        self.lane_next = {}
        for q, n in DMA_LANES.items():
            self.lanes[q] = [stack.enter_context(nc.semaphore(f"sp_{q}l{i}"))
                             for i in range(n)]
            self.lane_cnt[q] = [0] * n
            self.lane_next[q] = 0


class Sched:
    def __init__(self, nc, name, pool, serialize=False,
                 strict_engines=("V", "S", "P")):
        self.nc = nc
        self.name = name
        self.pool = pool
        self.serialize = serialize
        self.strict_engines = set(strict_engines)
        self.ops = []
        self.last_writer = {}
        self.readers = {}
        self.eng_pos = {e: [] for e in ENGS}

    def add(self, eng, fn, reads=(), writes=(), scalar_reads=(), dma=False):
        idx = len(self.ops)
        deps = set()
        strict = set()
        if self.serialize and self.ops:
            deps.add(idx - 1)
            strict.add(idx - 1)
        for k in tuple(reads) + tuple(scalar_reads):
            w = self.last_writer.get(k)
            if w is not None:
                deps.add(w)
                if k in scalar_reads:
                    strict.add(w)
        for k in writes:
            w = self.last_writer.get(k)
            if w is not None:
                deps.add(w)
            for rd in self.readers.get(k, ()):
                deps.add(rd)
        if eng in self.strict_engines:
            strict |= deps
        op = dict(eng=eng, fn=fn, dma=dma, deps=deps, strict=strict, idx=idx,
                  need_inc=False)
        self.ops.append(op)
        self.eng_pos[eng].append(idx)
        for k in tuple(reads) + tuple(scalar_reads):
            self.readers.setdefault(k, []).append(idx)
        for k in writes:
            self.last_writer[k] = idx
            self.readers[k] = []
        return idx

    def op(self, eng, meth, *args, reads=(), writes=(), scalar_reads=(),
           dma=False, **kwargs):
        reads = list(reads)
        writes = list(writes)
        seen_out = False
        for a in args:
            if isinstance(a, bass.AP):
                if type(a.tensor).__name__.startswith("PSum"):
                    k = ("BK", a.tensor.name)
                    if not seen_out:
                        writes.append(k)
                    elif k not in reads:
                        reads.append(k)
                seen_out = True
            elif a is not None and not isinstance(a, (int, float)):
                seen_out = True
        return self.add(eng, (meth, args, kwargs), reads=reads, writes=writes,
                        scalar_reads=scalar_reads, dma=dma)

    def dma(self, eng, out_ap, in_ap, reads=(), writes=(), **kw):
        return self.op(eng, "dma_start", out_ap, in_ap, reads=reads,
                       writes=writes, dma=True, **kw)

    def finalize(self, stack, final_waits=True, dump=None):
        pool = self.pool
        for op in self.ops:
            if op["dma"]:
                op["need_inc"] = True
            for d in op["deps"]:
                dep = self.ops[d]
                if (dep["eng"] != op["eng"]) or dep["dma"] or (d in op["strict"]):
                    dep["need_inc"] = True
        target = {}
        for op in self.ops:
            if not op["need_inc"]:
                continue
            e = op["eng"]
            if op["dma"]:
                li = pool.lane_next[e]
                pool.lane_next[e] = (li + 1) % len(pool.lanes[e])
                pool.lane_cnt[e][li] += 16
                op["lane"] = li
                target[op["idx"]] = (pool.lanes[e][li], pool.lane_cnt[e][li])
            else:
                pool.eng_cnt[e] += 1
                target[op["idx"]] = (pool.eng_sem[e], pool.eng_cnt[e])
        blk = stack.enter_context(nc_block(self.nc, self.name))

        def body(eng_key):
            def _(e):
                watermark = {}
                for i in self.eng_pos[eng_key]:
                    op = self.ops[i]
                    need = {}
                    for d in op["deps"]:
                        dep = self.ops[d]
                        if (dep["eng"] == eng_key and not dep["dma"]
                                and d not in op["strict"]):
                            continue
                        sem, v = target[d]
                        if v > watermark.get(id(sem), 0):
                            kk = id(sem)
                            if kk not in need or need[kk][1] < v:
                                need[kk] = (sem, v)
                    for kk in sorted(need):
                        sem, v = need[kk]
                        e.wait_ge(sem, v)
                        watermark[kk] = v
                    fn = op["fn"]
                    if callable(fn):
                        inst = fn(e)
                        desc = getattr(fn, "__name__", "fn")
                    else:
                        meth, args, kwargs = fn
                        inst = getattr(e, meth)(*args, **kwargs)
                        desc = meth
                    if dump is not None:
                        tg = target.get(op["idx"])
                        dump.write(f"{eng_key} op{op['idx']} {desc} "
                                   f"tgt={tg[1] if tg else '-'} "
                                   f"waits={[(s.name if hasattr(s,'name') else '?', v) for _, (s, v) in sorted(need.items())]}\n")
                    if op["need_inc"]:
                        sem, v = target[op["idx"]]
                        inst.then_inc(sem, 16 if op["dma"] else 1)
                if final_waits and eng_key == "Q":
                    for en in ("T", "V", "S", "P"):
                        if pool.eng_cnt[en] > 0:
                            e.wait_ge(pool.eng_sem[en], pool.eng_cnt[en])
                    for q in DMA_LANES:
                        for li, sem in enumerate(pool.lanes[q]):
                            if pool.lane_cnt[q][li] > 0:
                                e.wait_ge(sem, pool.lane_cnt[q][li])
            return _

        for ek, meth in (("Q", blk.sync), ("P", blk.gpsimd), ("T", blk.tensor),
                         ("V", blk.vector), ("S", blk.scalar)):
            if self.eng_pos[ek] or (final_waits and ek == "Q"):
                meth(body(ek))


def nc_block(nc, name):
    return nc.Block(name)



def _rope_l0_tables():
    half = np.arange(64, dtype=np.float64)
    inv = 1.0 / 10000.0 ** (2.0 * half / 128.0)
    fr = np.arange(N, dtype=np.float64)[:, None] * inv[None, :]
    emb = np.concatenate([fr, fr], axis=-1)
    return np.cos(emb).astype(np.float32), np.sin(emb).astype(np.float32)


def _rope_l1_tables():
    inv = 1.0 / 10000.0 ** (np.arange(0, 128, 2, dtype=np.float32) / 128.0)
    t = np.arange(N, dtype=np.float32)
    f = np.outer(t.astype(np.float64), inv.astype(np.float64))
    cos = np.zeros((N, 128), np.float32)
    sin = np.zeros((N, 128), np.float32)
    cos[:, 0::2] = np.cos(f); cos[:, 1::2] = np.cos(f)
    sin[:, 0::2] = np.sin(f); sin[:, 1::2] = np.sin(f)
    return cos, sin


def host_constants():
    c = {}
    ut = np.triu(np.ones((CH, CH), np.float32)) / 16.0
    c["cum_lhsT"] = ut
    c["ones_col"] = np.ones((128, 1), np.float32)
    c["gla_maskT"] = np.triu(np.ones((CH, CH), np.float32)).astype(NPBF16)
    a = np.arange(128)[:, None]
    cc = np.arange(256)[None, :]
    rel = a + 128 - cc
    gen = ((rel >= 0) & (rel <= 64)).astype(np.float32)
    c["swa_mask_gen"] = gen
    c["swa_mask_first"] = np.ascontiguousarray(gen * (cc >= 128))
    c["cos0"], c["sin0"] = _rope_l0_tables()
    c["cos1"], c["sin1"] = _rope_l1_tables()
    alpha = np.linspace(0.0, 1.0, 256, dtype=np.float32)
    c["alpha_a"] = np.ascontiguousarray(alpha[:128].reshape(128, 1))
    c["alpha_b"] = np.ascontiguousarray(alpha[128:].reshape(128, 1))
    c["alpha1m_a"] = np.ascontiguousarray((1.0 - alpha[:128]).reshape(128, 1))
    c["alpha1m_b"] = np.ascontiguousarray((1.0 - alpha[128:]).reshape(128, 1))
    return c



class Prog:
    def __init__(self, name):
        self.nc = bass.Bass("TRN2", target_bir_lowering=False)
        self.name = name
        self.inputs = {}
        self.stack = None

    def dram_in(self, name, arr_shape, dtype):
        return self.nc.dram_tensor(name, list(arr_shape), dtype,
                                   kind="ExternalInput").ap()

    def dram_out(self, name, arr_shape, dtype):
        return self.nc.dram_tensor(name, list(arr_shape), dtype,
                                   kind="ExternalOutput").ap()

    def dram_tmp(self, name, arr_shape, dtype):
        return self.nc.dram_tensor(name, list(arr_shape), dtype,
                                   kind="Internal").ap()


_CST_SEQ = [0]


def load_constants(nc, stack, S, cst_aps, names):
    out = {}
    _CST_SEQ[0] += 1
    for nm in names:
        ap = cst_aps[nm]
        shape = list(ap.shape)
        dt = ap.dtype
        t = stack.enter_context(nc.sbuf_tensor(f"cs{_CST_SEQ[0]}_" + nm, shape, dt))
        S.dma("Q", t[:], ap, writes=[("cst", nm)])
        out[nm] = t
    return out



def emit_ln(S, x_t, ln_t, cols3, xkey, lnkey, ckey):
    negmean, rstd, _ = cols3
    S.op("V", "reduce_sum", negmean, x_t, axis=AX.X, negate=True,
         reads=[xkey], writes=[(ckey, "nm")])
    S.op("V", "tensor_scalar_mul", negmean, negmean, 1.0 / DM,
         reads=[(ckey, "nm")], writes=[(ckey, "nm")])
    S.op("S", "activation", ln_t, x_t, AF.Square, bias=negmean, scale=1.0,
         accum_out=rstd,
         reads=[xkey], writes=[lnkey, (ckey, "var")], scalar_reads=[(ckey, "nm")])
    S.op("V", "tensor_scalar_add", rstd, rstd, DM * 1e-5,
         reads=[(ckey, "var")], writes=[(ckey, "var")])
    S.op("S", "activation", rstd, rstd, AF.Ln,
         reads=[(ckey, "var")], writes=[(ckey, "var")])
    S.op("V", "tensor_scalar", rstd, rstd, -0.5, 0.5 * float(np.log(DM)),
         ALU.mult, ALU.add,
         reads=[(ckey, "var")], writes=[(ckey, "var")])
    S.op("S", "activation", rstd, rstd, AF.Exp,
         reads=[(ckey, "var")], writes=[(ckey, "var")])
    S.op("V", "tensor_scalar", ln_t, x_t, negmean, rstd, ALU.add, ALU.mult,
         reads=[xkey], writes=[lnkey],
         scalar_reads=[(ckey, "nm"), (ckey, "var")])



PANW = 256


def emit_a1(nc, S, stack, half, aps, cst):
    KT = NV // 128
    e0 = E0
    ext_tiles = set(range(e0 // 128, e0 // 128 + EXTN // 128))
    GT = 14
    x_ap, wq_ap, wk_ap, wv_ap = aps["x"], aps["Wq"], aps["Wk"], aps["Wv"]
    k_sc, v_sc, q_sc = aps["k_sc"], aps["v_sc"], aps["q_sc"]
    ident_bf = cst["ident_bf"]

    sb = lambda nm, shape, dt: stack.enter_context(nc.sbuf_tensor(nm, shape, dt))
    ps = lambda nm, shape, dt: stack.enter_context(nc.psum_tensor(nm, shape, dt))
    lnT = sb("a1_lnT", [128, GT * 32 * 128], BF16)
    xa = [sb(f"a1_xa{i}", [128, DM], F32) for i in range(2)]
    lnb = [sb(f"a1_lnb{i}", [128, DM], BF16) for i in range(2)]
    wpan = [sb(f"a1_wpan{i}", [128, 32 * PANW], BF16) for i in range(2)]
    evf = [sb(f"a1_evf{i}", [128, PANW], F32) for i in range(2)]
    evb = [sb(f"a1_evb{i}", [128, PANW], BF16) for i in range(2)]
    colbuf = sb("a1_col", [128, 8], F32)
    ptr = [ps(f"a1_ptr{i}", [128, 1024], BF16) for i in range(2)]
    pmm = [ps(f"a1_pmm{i}", [128, 512], F32) for i in range(2)]

    def lnT_sl(kt, ji):
        c = kt * (GT * 128) + ji * 128
        return lnT[:, c:c + 128]

    panels = ([("k", wk_ap, k_sc, "f32", c0) for c0 in range(0, 1024, PANW)]
              + [("v", wv_ap, v_sc, "bf16", c0) for c0 in range(0, 1024, PANW)]
              + [("q", wq_ap, q_sc, "bf16", c0) for c0 in range(0, DM, PANW)])
    groups = [range(t, min(t + GT, KT)) for t in range(0, KT, GT)]
    pan_i = 0

    for gi, group in enumerate(groups):
        g_ext = [ti for ti in group if ti in ext_tiles]
        for j, ti in enumerate(group):
            xb, lb = xa[j % 2], lnb[j % 2]
            xkey, lnkey, ckey = ("xa", j % 2), ("lnb", j % 2), ("col", j % 2)
            cols3 = (colbuf[:, (j % 2) * 3:(j % 2) * 3 + 1],
                     colbuf[:, (j % 2) * 3 + 1:(j % 2) * 3 + 2],
                     colbuf[:, (j % 2) * 3 + 2:(j % 2) * 3 + 3])
            S.dma("Q", xb[:], x_ap[ti * 128:(ti + 1) * 128, :], writes=[xkey])
            emit_ln(S, xb[:], lb[:], cols3, xkey, lnkey, ckey)
            ji = ti - group[0]
            for kg in range(4):
                pt = ptr[kg % 2]
                for k8 in range(8):
                    kt = kg * 8 + k8
                    S.op("T", "transpose", pt[:, k8 * 128:(k8 + 1) * 128],
                         lb[:, kt * 128:(kt + 1) * 128], ident_bf[:],
                         reads=[lnkey, ("cst", "ident_bf")],
                         writes=[("ptr", kg % 2, k8)])
                dst = lnT[:].rearrange("p (k t) -> p k t", k=32)[
                    :, kg * 8:(kg + 1) * 8, ji * 128:(ji + 1) * 128]
                S.op("V", "tensor_copy", dst,
                     pt[:].rearrange("p (k t) -> p k t", k=8),
                     reads=[("ptr", kg % 2, k8) for k8 in range(8)],
                     writes=[("lnT", ji, kg)] + [("ptr", kg % 2, k8)
                                                 for k8 in range(8)])
        plist = [p for p in panels if not (p[0] == "q" and not g_ext)]
        for (nm, w_ap, sc_ap, dt, c0) in plist:
            wb = pan_i % 2
            rows = g_ext if nm == "q" else list(group)
            S.dma("P", wpan[wb][:].rearrange("p (k j) -> p k j", k=32),
                  w_ap[:, c0:c0 + PANW].rearrange("(k p) j -> p k j", p=128),
                  writes=[("wpan", wb)])
            for ri, ti in enumerate(rows):
                ji = ti - group[0]
                pb = pmm[ri % 2]
                for kt in range(32):
                    S.op("T", "matmul", pb[:, 0:PANW], lnT_sl(kt, ji),
                         wpan[wb][:, kt * PANW:(kt + 1) * PANW],
                         start=(kt == 0), stop=(kt == 31),
                         reads=[("lnT", ji, kt // 8), ("wpan", wb)],
                         writes=[("pmm", ri % 2)])
                ev = evf[ri % 2] if dt == "f32" else evb[ri % 2]
                S.op("V", "tensor_copy", ev[:, 0:PANW], pb[:, 0:PANW],
                     reads=[("pmm", ri % 2)],
                     writes=[("ev", dt, ri % 2), ("pmm", ri % 2)])
                r0 = ti * 128 - (e0 if nm == "q" else 0)
                S.dma("Q", sc_ap[r0:r0 + 128, c0:c0 + PANW], ev[:, 0:PANW],
                      reads=[("ev", dt, ri % 2)], writes=[("sc", nm, ti, c0)])
            pan_i += 1



def bcast_mid(t, p0, p1, nrep, c0, c1):
    ap = t[:]
    pstep = ap.ap[0][0]
    return bass.AP(t, p0 * pstep + c0, [[pstep, p1 - p0], [0, nrep], [1, c1 - c0]])


def emit_a2(nc, S, stack, half, aps, cst):
    e0 = E0
    chunks = list(range(0, e0 + EXTN, 128))
    n_prefix = e0 // 128
    k_sc, v_sc, q_sc, ocatT_sc = (aps["k_sc"], aps["v_sc"], aps["q_sc"],
                                  aps["ocatT_sc"])
    SC = 1.0 / float(np.sqrt(128.0))

    sb = lambda nm, shape, dt: stack.enter_context(nc.sbuf_tensor(nm, shape, dt))
    ps = lambda nm, shape, dt: stack.enter_context(nc.psum_tensor(nm, shape, dt))
    kc = [sb(f"a2_kc{i}", [128, 1024], F32) for i in range(2)]
    vc = [sb(f"a2_vc{i}", [128, 1024], BF16) for i in range(2)]
    qc = [sb(f"a2_qc{i}", [128, DM], BF16) for i in range(2)]
    gmin = sb("a2_gmin", [128, 1024], F32)
    gtmp = sb("a2_gtmp", [128, 1024], F32)
    gs = sb("a2_gs", [128, 1024], F32)
    eB = sb("a2_eB", [128, 1024], F32)
    eBn = sb("a2_eBn", [128, 1024], F32)
    eBC = sb("a2_eBC", [128, 8], F32)
    eks = sb("a2_eks", [128, 1024], F32)
    ktil = sb("a2_ktil", [128, 1024], BF16)
    ktilT = sb("a2_ktilT", [128, 1024], BF16)
    Sst = [sb(f"a2_S{i}", [128, 1024], F32) for i in range(2)]
    Sb = sb("a2_Sb", [128, 1024], BF16)
    kbf = sb("a2_kbf", [128, 1024], BF16)
    krn = sb("a2_krn", [128, 1024], BF16)
    krT = [sb(f"a2_krT{i}", [128, 1024], BF16) for i in range(2)]
    qrn = sb("a2_qrn", [128, DM], BF16)
    qrT = sb("a2_qrT", [128, DM], BF16)
    cosn = sb("a2_cosn", [128, 128], F32)
    sinn = sb("a2_sinn", [128, 128], F32)
    QT = sb("a2_QT", [128, DM], BF16)
    qs_all = sb("a2_qsall", [128, DM], BF16)
    eqa = sb("a2_eqa", [128, DM], BF16)
    qt_all = sb("a2_qtall", [128, DM], BF16)
    colq = sb("a2_colq", [128, 64], F32)
    ptb = [sb(f"a2_ptb{i}", [128, 128], BF16) for i in range(2)]
    esb = sb("a2_esb", [128, 256], F32)
    punb = sb("a2_punb", [128, 256], BF16)
    PT1 = sb("a2_PT1", [128, 128], BF16)
    PT2 = sb("a2_PT2", [128, 128], BF16)
    ogla = [sb(f"a2_ogla{i}", [128, 128], F32) for i in range(2)]
    o_ = sb("a2_o", [128, DM], BF16)
    ocat = sb("a2_ocat", [128, DM], BF16)
    colb = sb("a2_colb", [128, 24], F32)
    ropet = sb("a2_ropet", [128, DM], BF16)
    ropet2 = sb("a2_ropet2", [128, DM], BF16)

    PB = ps("a2_PB", [128, 512], F32)
    PBC = ps("a2_PBC", [128, 512], F32)
    TRB = [ps(f"a2_TRB{i}", [128, 1024], BF16) for i in range(2)]
    PX = ps("a2_PX", [128, 512], F32)
    PO = ps("a2_PO", [128, 512], F32)
    PSW = [ps(f"a2_PSW{i}", [128, 512], F32) for i in range(2)]

    cum, ones16 = cst["cum_lhsT"], cst["ones16_col"]
    gmask = cst["gla_maskT"]
    smasks = {"gen": cst["swa_maskB"], "first": cst["swa_maskA"]}
    cos_dram, sin_dram = cst["cos0_dram"], cst["sin0_dram"]
    identb = cst["ident_bf"]

    S.op("V", "memset", krT[1][:], 0.0, writes=[("krT", 1)])
    S.op("V", "memset", vc[1][:], 0.0, writes=[("vc", 1)])
    S.op("V", "memset", Sst[0][:], 0.0,
         writes=[("Sst", 0, kv) for kv in range(8)])

    def tr(dst_ps, dst_key, src_ap, src_keys):
        S.op("T", "transpose", dst_ps, src_ap, identb[:],
             reads=list(src_keys) + [("cst", "ident_bf")], writes=[dst_key])

    def rope_nat(dst, src, nrep, dkey, skeys, scale=None):
        def d3(t, c0):
            return t[:, 0:nrep * 128].rearrange(
                "p (k j) -> p k j", k=nrep)[:, :, c0:c0 + 64]
        pstep = cosn[:].ap[0][0]
        cb_lo = bass.AP(cosn, 0, [[pstep, 128], [0, nrep], [1, 64]])
        cb_hi = bass.AP(cosn, 64, [[pstep, 128], [0, nrep], [1, 64]])
        sn_lo = bass.AP(sinn, 0, [[pstep, 128], [0, nrep], [1, 64]])
        sn_hi = bass.AP(sinn, 64, [[pstep, 128], [0, nrep], [1, 64]])
        S.op("V", "tensor_mul", d3(ropet, 0), d3(src, 0), cb_lo,
             reads=skeys + [("cosn",)], writes=[("ropet", 0)])
        S.op("V", "tensor_mul", d3(ropet, 64), d3(src, 64), cb_hi,
             reads=skeys + [("cosn",)], writes=[("ropet", 64)])
        S.op("V", "tensor_mul", d3(ropet2, 0), d3(src, 64), sn_lo,
             reads=skeys + [("sinn",)], writes=[("ropet2", 0)])
        S.op("V", "tensor_mul", d3(ropet2, 64), d3(src, 0), sn_hi,
             reads=skeys + [("sinn",)], writes=[("ropet2", 64)])
        S.op("V", "tensor_sub", d3(dst, 0), d3(ropet, 0), d3(ropet2, 0),
             reads=[("ropet", 0), ("ropet2", 0)], writes=[dkey])
        S.op("V", "tensor_add", d3(dst, 64), d3(ropet, 64), d3(ropet2, 64),
             reads=[("ropet", 64), ("ropet2", 64)], writes=[dkey])
        if scale is not None:
            S.op("V", "tensor_scalar_mul", dst[:, 0:nrep * 128],
                 dst[:, 0:nrep * 128], scale, reads=[dkey], writes=[dkey])

    for ci, r0 in enumerate(chunks):
        ext = r0 >= e0
        cur, prv = ci % 2, (ci + 1) % 2
        last_prefix = (not ext) and (ci == n_prefix - 1)
        need_swa_kv = ext or last_prefix
        kck, vck, qck = ("kc", cur), ("vc", cur), ("qc", cur)
        S.dma("Q", kc[cur][:], k_sc[r0:r0 + 128, :], writes=[kck])
        S.dma("Q", vc[cur][:], v_sc[r0:r0 + 128, :], writes=[vck])
        if ext:
            S.dma("Q", qc[cur][:], q_sc[r0 - e0:r0 - e0 + 128, :], writes=[qck])
        S.op("V", "tensor_scalar_min", gmin[:], kc[cur][:], 0.0,
             reads=[kck], writes=[("gmin",)])
        S.op("S", "activation", gtmp[:], kc[cur][:], AF.Abs,
             reads=[kck], writes=[("gtmp",)])
        S.op("S", "activation", gtmp[:], gtmp[:], AF.Exp, scale=-1.0,
             reads=[("gtmp",)], writes=[("gtmp",)])
        S.op("V", "tensor_scalar_add", gtmp[:], gtmp[:], 1.0,
             reads=[("gtmp",)], writes=[("gtmp",)])
        S.op("S", "activation", gtmp[:], gtmp[:], AF.Ln,
             reads=[("gtmp",)], writes=[("gtmp",)])
        S.op("V", "tensor_sub", gs[:], gmin[:], gtmp[:],
             reads=[("gmin",), ("gtmp",)], writes=[("gs",)])
        for h2 in range(2):
            for kh in range(4):
                kv = h2 * 4 + kh
                S.op("T", "matmul", PB[:, kh * 128:(kh + 1) * 128], cum[:],
                     gs[:, kv * 128:(kv + 1) * 128], start=True, stop=True,
                     reads=[("gs",), ("cst", "cum_lhsT")], writes=[("PB",)])
            S.op("S", "activation", eB[:, h2 * 512:(h2 + 1) * 512], PB[:], AF.Exp,
                 reads=[("PB",)], writes=[("eB", h2)])
            S.op("S", "activation", eBn[:, h2 * 512:(h2 + 1) * 512], PB[:],
                 AF.Exp, scale=-1.0,
                 reads=[("PB",)], writes=[("eBn", h2), ("PB",)])
        for kv in range(8):
            S.op("T", "matmul", PBC[:, kv:kv + 1],
                 gs[:, kv * 128:(kv + 1) * 128], ones16[:], start=True, stop=True,
                 reads=[("gs",), ("cst", "ones16_col")], writes=[("PBC",)])
        S.op("S", "activation", eBC[:], PBC[:, 0:8], AF.Exp,
             reads=[("PBC",)], writes=[("eBC",), ("PBC",)])
        for kv in range(8):
            nmk = colb[:, kv:kv + 1]
            sk = colb[:, 8 + kv:9 + kv]
            S.op("V", "reduce_max", nmk, kc[cur][:, kv * 128:(kv + 1) * 128],
                 axis=AX.X, negate=True,
                 reads=[kck], writes=[("col", kv)])
            S.op("S", "activation", eks[:, kv * 128:(kv + 1) * 128],
                 kc[cur][:, kv * 128:(kv + 1) * 128], AF.Exp, bias=nmk,
                 accum_out=sk,
                 reads=[kck], writes=[("eks", kv), ("colsk", kv)],
                 scalar_reads=[("col", kv)])
            S.op("V", "reciprocal", sk, sk,
                 reads=[("colsk", kv)], writes=[("colsk", kv)])
            S.op("V", "scalar_tensor_tensor", ktil[:, kv * 128:(kv + 1) * 128],
                 eks[:, kv * 128:(kv + 1) * 128], sk,
                 eBn[:, kv * 128:(kv + 1) * 128], ALU.mult, ALU.mult,
                 reads=[("eks", kv), ("eBn", kv // 4)], writes=[("ktil", kv)],
                 scalar_reads=[("colsk", kv)])
        for h2 in range(2):
            for kh in range(4):
                kv = h2 * 4 + kh
                S.op("T", "matmul", PB[:, kh * 128:(kh + 1) * 128],
                     ktil[:, kv * 128:(kv + 1) * 128],
                     vc[cur][:, kv * 128:(kv + 1) * 128], start=True, stop=True,
                     reads=[("ktil", kv), vck], writes=[("PB",)])
            for kh in range(4):
                kv = h2 * 4 + kh
                S.op("V", "tensor_add", Sst[prv][:, kv * 128:(kv + 1) * 128],
                     PB[:, kh * 128:(kh + 1) * 128],
                     Sst[cur][:, kv * 128:(kv + 1) * 128],
                     reads=[("PB",), ("Sst", cur, kv)],
                     writes=[("Sst", prv, kv)])
                S.op("V", "tensor_scalar_mul",
                     Sst[prv][:, kv * 128:(kv + 1) * 128],
                     Sst[prv][:, kv * 128:(kv + 1) * 128], eBC[:, kv:kv + 1],
                     reads=[("Sst", prv, kv)], writes=[("Sst", prv, kv)],
                     scalar_reads=[("eBC",)])
            S.op("V", "engine_nop", reads=[("PB",)], writes=[("PB",)])
        if ext:
            S.op("S", "copy", Sb[:], Sst[cur][:],
                 reads=[("Sst", cur, kv) for kv in range(8)], writes=[("Sb",)])
        if need_swa_kv:
            S.dma("Q", cosn[:], cos_dram[r0:r0 + 128, :], writes=[("cosn",)])
            S.dma("Q", sinn[:], sin_dram[r0:r0 + 128, :], writes=[("sinn",)])
            S.op("V", "tensor_copy", kbf[:], kc[cur][:],
                 reads=[kck], writes=[("kbf",)])
            rope_nat(krn, kbf, 8, ("krn",), [("kbf",)])
            for kv8 in range(8):
                tr(TRB[0][:, kv8 * 128:(kv8 + 1) * 128], ("TRB", 0, kv8),
                   krn[:, kv8 * 128:(kv8 + 1) * 128], [("krn",)])
            S.op("S", "copy", krT[cur][:], TRB[0][:],
                 reads=[("TRB", 0, kv8) for kv8 in range(8)],
                 writes=[("krT", cur)] + [("TRB", 0, kv8) for kv8 in range(8)])
        if not ext:
            continue
        rope_nat(qrn, qc[cur], 32, ("qrn",), [qck], scale=SC)
        for kg in range(4):
            tb = TRB[kg % 2]
            for k8 in range(8):
                kt = kg * 8 + k8
                tr(tb[:, k8 * 128:(k8 + 1) * 128], ("TRB", kg % 2, k8),
                   qrn[:, kt * 128:(kt + 1) * 128], [("qrn",)])
            S.op("S", "copy", qrT[:, kg * 1024:(kg + 1) * 1024], tb[:],
                 reads=[("TRB", kg % 2, k8) for k8 in range(8)],
                 writes=[("qrT", kg)] + [("TRB", kg % 2, k8) for k8 in range(8)])
        q3d = qc[cur][:].rearrange("p (k j) -> p k j", k=32)
        S.op("V", "reduce_max", colq[:, 0:32].rearrange("p (k j) -> p k j", k=32),
             q3d, axis=AX.X, negate=True,
             reads=[qck], writes=[("colq", 0)])
        pstep = colq[:].ap[0][0]
        nm_b = bass.AP(colq, 0, [[pstep, 128], [1, 32], [0, 128]])
        S.op("V", "tensor_add", qs_all[:].rearrange("p (k j) -> p k j", k=32),
             q3d, nm_b,
             reads=[qck, ("colq", 0)], writes=[("qs_all",)])
        S.op("S", "activation", eqa[:], qs_all[:], AF.Exp,
             reads=[("qs_all",)], writes=[("eqa",)])
        S.op("V", "reduce_sum",
             colq[:, 32:64].rearrange("p (k j) -> p k j", k=32),
             eqa[:].rearrange("p (k j) -> p k j", k=32), axis=AX.X,
             reads=[("eqa",)], writes=[("colq", 1)])
        S.op("V", "reciprocal", colq[:, 32:64], colq[:, 32:64],
             reads=[("colq", 1)], writes=[("colq", 1)])
        for h in range(32):
            kv = h // 4
            S.op("V", "scalar_tensor_tensor", qt_all[:, h * 128:(h + 1) * 128],
                 eqa[:, h * 128:(h + 1) * 128], colq[:, 32 + h:33 + h],
                 eB[:, kv * 128:(kv + 1) * 128], ALU.mult, ALU.mult,
                 reads=[("eqa",), ("eB", kv // 4)], writes=[("qt_all", h // 8)],
                 scalar_reads=[("colq", 1)])
        for kg in range(4):
            tb2 = TRB[kg % 2]
            for k8 in range(8):
                kt = kg * 8 + k8
                S.op("T", "transpose", tb2[:, k8 * 128:(k8 + 1) * 128],
                     qt_all[:, kt * 128:(kt + 1) * 128], identb[:],
                     reads=[("qt_all", kg), ("cst", "ident_bf")],
                     writes=[("TRB", kg % 2, k8)])
            S.op("V", "tensor_copy",
                 QT[:].rearrange("p (k t) -> p k t", k=32)[
                     :, kg * 8:(kg + 1) * 8, :],
                 tb2[:].rearrange("p (k t) -> p k t", k=8),
                 reads=[("TRB", kg % 2, k8) for k8 in range(8)],
                 writes=[("QT", kg)] + [("TRB", kg % 2, k8) for k8 in range(8)])
        for kv8 in range(8):
            tr(TRB[0][:, kv8 * 128:(kv8 + 1) * 128], ("TRB", 0, kv8),
               ktil[:, kv8 * 128:(kv8 + 1) * 128], [("ktil", kv8)])
        S.op("S", "copy", ktilT[:], TRB[0][:],
             reads=[("TRB", 0, kv8) for kv8 in range(8)],
             writes=[("ktilT",)] + [("TRB", 0, kv8) for kv8 in range(8)])
        first_own = (r0 == e0 + 256)
        smask = smasks["first" if first_own else "gen"]
        smask_key = "swa_maskA" if first_own else "swa_maskB"
        for h in range(32):
            kv = h // 4
            hp = h % 2
            nms = colb[:, 18 + 4 * hp:19 + 4 * hp]
            rs = colb[:, 19 + 4 * hp:20 + 4 * hp]
            S.op("T", "matmul", PX[:, 0:128],
                 ktilT[:, kv * 128:(kv + 1) * 128],
                 QT[:, h * 128:(h + 1) * 128], start=True, stop=True,
                 reads=[("ktilT",), ("QT", h // 8)], writes=[("PX",)])
            S.op("V", "tensor_mul", ptb[hp][:], PX[:, 0:128], gmask[:],
                 reads=[("PX",), ("cst", "gla_maskT")],
                 writes=[("ptb", hp), ("PX",)])
            S.op("T", "matmul", PO[:, 0:128], ptb[hp][:],
                 vc[cur][:, kv * 128:(kv + 1) * 128], start=True, stop=False,
                 reads=[("ptb", hp), vck], writes=[("PO",)])
            S.op("T", "matmul", PO[:, 0:128], QT[:, h * 128:(h + 1) * 128],
                 Sb[:, kv * 128:(kv + 1) * 128], start=False, stop=True,
                 reads=[("QT", h // 8), ("Sb",)], writes=[("PO",)])
            S.op("S", "copy", ogla[hp][:], PO[:, 0:128],
                 reads=[("PO",)], writes=[("ogla", hp), ("PO",)])
            pw = PSW[hp]
            S.op("T", "matmul", pw[:, 0:128], qrT[:, h * 128:(h + 1) * 128],
                 krT[prv][:, kv * 128:(kv + 1) * 128], start=True, stop=True,
                 reads=[("qrT", h // 8), ("krT", prv)], writes=[("PSW", hp)])
            S.op("T", "matmul", pw[:, 128:256], qrT[:, h * 128:(h + 1) * 128],
                 krT[cur][:, kv * 128:(kv + 1) * 128], start=True, stop=True,
                 reads=[("qrT", h // 8), ("krT", cur)], writes=[("PSW", hp)])
            S.op("V", "reduce_max", nms, pw[:, 0:256], axis=AX.X, negate=True,
                 reads=[("PSW", hp)], writes=[("nms", hp)])
            S.op("S", "activation", esb[:], pw[:, 0:256], AF.Exp, bias=nms,
                 reads=[("PSW", hp)], writes=[("esb",), ("PSW", hp)],
                 scalar_reads=[("nms", hp)])
            S.op("V", "scalar_tensor_tensor", punb[:], esb[:], 1.0, smask[:],
                 ALU.mult, ALU.mult, accum_out=rs,
                 reads=[("esb",), ("cst", smask_key)],
                 writes=[("punb",), ("rs", hp)])
            S.op("V", "reciprocal", rs, rs,
                 reads=[("rs", hp)], writes=[("rs", hp)])
            tr(TRB[1][:, 256:384], ("TRB", 1, "pt1"), punb[:, 0:128],
               [("punb",)])
            tr(TRB[1][:, 384:512], ("TRB", 1, "pt2"), punb[:, 128:256],
               [("punb",)])
            S.op("S", "copy", PT1[:], TRB[1][:, 256:384],
                 reads=[("TRB", 1, "pt1")],
                 writes=[("PT1",), ("TRB", 1, "pt1")])
            S.op("S", "copy", PT2[:], TRB[1][:, 384:512],
                 reads=[("TRB", 1, "pt2")],
                 writes=[("PT2",), ("TRB", 1, "pt2")])
            S.op("T", "matmul", pw[:, 256:384], PT1[:],
                 vc[prv][:, kv * 128:(kv + 1) * 128], start=True, stop=False,
                 reads=[("PT1",), ("vc", prv)], writes=[("PSW", hp)])
            S.op("T", "matmul", pw[:, 256:384], PT2[:],
                 vc[cur][:, kv * 128:(kv + 1) * 128], start=False, stop=True,
                 reads=[("PT2",), vck], writes=[("PSW", hp)])
            S.op("V", "scalar_tensor_tensor", o_[:, h * 128:(h + 1) * 128],
                 pw[:, 256:384], rs, ogla[hp][:], ALU.mult, ALU.add,
                 reads=[("PSW", hp), ("ogla", hp)],
                 writes=[("o_", h), ("PSW", hp)],
                 scalar_reads=[("rs", hp)])
        for kg in range(4):
            tb = TRB[kg % 2]
            for k8 in range(8):
                kt = kg * 8 + k8
                tr(tb[:, k8 * 128:(k8 + 1) * 128], ("TRB", kg % 2, k8),
                   o_[:, kt * 128:(kt + 1) * 128], [("o_", kt)])
            S.op("V", "tensor_copy", ocat[:, kg * 1024:(kg + 1) * 1024], tb[:],
                 reads=[("TRB", kg % 2, k8) for k8 in range(8)],
                 writes=[("ocat", kg)] + [("TRB", kg % 2, k8) for k8 in range(8)])
        S.dma("Q", ocatT_sc.rearrange("(k p) t -> p k t", p=128)[
            :, :, r0 - e0:r0 - e0 + 128],
            ocat[:].rearrange("p (k t) -> p k t", k=32),
            reads=[("ocat", kg) for kg in range(4)], writes=[("ocatT_out", ci)])



def emit_a3(nc, S, stack, half, aps, cst):
    e0 = E0
    x_ap, wo_ap, ocatT_sc, y1_sc = aps["x"], aps["Wo"], aps["ocatT_sc"], aps["y1_sc"]
    sb = lambda nm, shape, dt: stack.enter_context(nc.sbuf_tensor(nm, shape, dt))
    ps = lambda nm, shape, dt: stack.enter_context(nc.psum_tensor(nm, shape, dt))
    ocT = sb("a3_ocT", [128, 32 * EXTN], BF16)
    wpan = [sb(f"a3_wpan{i}", [128, 32 * PANW], BF16) for i in range(2)]
    xs = [sb(f"a3_xs{i}", [128, PANW], F32) for i in range(2)]
    ev = [sb(f"a3_ev{i}", [128, PANW], F32) for i in range(2)]
    pmm = [ps(f"a3_pmm{i}", [128, 512], F32) for i in range(2)]

    S.dma("Q", ocT[:].rearrange("p (k t) -> p k t", k=32),
          ocatT_sc.rearrange("(k p) t -> p k t", p=128), writes=[("ocT",)])

    def ocsl(kt, ti):
        c = kt * EXTN + ti * 128
        return ocT[:, c:c + 128]

    for pi, c0 in enumerate(range(0, DM, PANW)):
        wb = pi % 2
        S.dma("P", wpan[wb][:].rearrange("p (k j) -> p k j", k=32),
              wo_ap[:, c0:c0 + PANW].rearrange("(k p) j -> p k j", p=128),
              writes=[("wpan", wb)])
        for ti in range(EXTN // 128):
            tb = ti % 2
            S.dma("Q", xs[tb][:], x_ap[e0 + ti * 128:e0 + (ti + 1) * 128,
                                       c0:c0 + PANW], writes=[("xs", tb)])
            for kt in range(32):
                S.op("T", "matmul", pmm[tb][:, 0:PANW], ocsl(kt, ti),
                     wpan[wb][:, kt * PANW:(kt + 1) * PANW],
                     start=(kt == 0), stop=(kt == 31),
                     reads=[("ocT",), ("wpan", wb)], writes=[("pmm", tb)])
            S.op("V", "scalar_tensor_tensor", ev[tb][:], pmm[tb][:, 0:PANW],
                 0.5, xs[tb][:], ALU.mult, ALU.add,
                 reads=[("pmm", tb), ("xs", tb)],
                 writes=[("ev", tb), ("pmm", tb)])
            S.dma("Q", y1_sc[ti * 128:(ti + 1) * 128, c0:c0 + PANW], ev[tb][:],
                  reads=[("ev", tb)], writes=[("y1", ti, c0)])



CST_NAMES = ["cum_lhsT", "ones_col", "gla_maskT", "swa_mask_gen", "swa_mask_first",
             "cos0", "sin0", "cos1", "sin1", "alpha_a", "alpha_b",
             "ident_bf", "ident_f32", "ones16_col"]


def host_constants_full():
    c = host_constants()
    c["ident_bf"] = np.eye(128, dtype=NPBF16)
    c["ones_bf"] = np.ones((128, 1), dtype=NPBF16)
    c["ident_f32"] = np.eye(128, dtype=np.float32)
    c["ones16_col"] = np.full((128, 1), 1.0 / 16.0, np.float32)
    pad = np.zeros((NV - N, 128), np.float32)
    for nm in ("cos0", "sin0", "cos1", "sin1"):
        c[nm] = np.ascontiguousarray(np.concatenate([c[nm], pad]))
    c["swa_maskA"] = c["swa_mask_gen"]
    c["swa_maskB"] = c["swa_mask_gen"]
    c["alphaF_a"] = c["alpha_a"]
    c["alphaF_b"] = c["alpha_b"]
    c["alphaF1m_a"] = c["alpha1m_a"]
    c["alphaF1m_b"] = c["alpha1m_b"]
    c["mlast_cols"] = np.concatenate(
        [np.ones((128, 2), np.float32), np.zeros((128, 2), np.float32)], 1)
    return c


def percore_constants(cv, half):
    o = {}
    if half == 1:
        o["swa_maskA"] = cv["swa_mask_gen"]
        o["mlast_cols"] = cv["mlast_cols"]
        for nm in ("alphaF_a", "alphaF_b"):
            o[nm] = cv["alpha_" + nm[-1]]
        for nm in ("alphaF1m_a", "alphaF1m_b"):
            o[nm] = cv["alpha1m_" + nm[-1]]
        for nm in ("cos0", "sin0", "cos1", "sin1"):
            o[nm] = cv[nm]
    else:
        o["swa_maskA"] = cv["swa_mask_first"]
        o["mlast_cols"] = np.ones((128, 4), np.float32)
        o["alphaF_a"] = np.ones((128, 1), np.float32)
        o["alphaF_b"] = np.ones((128, 1), np.float32)
        o["alphaF1m_a"] = np.zeros((128, 1), np.float32)
        o["alphaF1m_b"] = np.zeros((128, 1), np.float32)
        for nm in ("cos0", "sin0", "cos1", "sin1"):
            t = cv[nm]
            sh = np.zeros_like(t)
            sh[2048:] = t[:NV - 2048]
            o[nm] = np.ascontiguousarray(sh)
    return o


def make_xeff(xb, half):
    out = np.zeros((NV, DM), np.float32)
    if half == 1:
        out[:N] = xb
    else:
        out[2048:2048 + 2304] = xb[:2304]
    return out


def _np_to_dt(a):
    if a.dtype == np.float32:
        return F32
    if a.dtype == NPBF16:
        return BF16
    if a.dtype == np.int32:
        return I32
    raise ValueError(a.dtype)


def declare_constants(nc, cvals, names):
    return {nm: nc.dram_tensor("cst_" + nm, list(cvals[nm].shape),
                               _np_to_dt(cvals[nm]), kind="ExternalInput").ap()
            for nm in names}


def build_a1_test(half, serialize=False, strict_engines=()):
    nc = bass.Bass("TRN2", target_bir_lowering=False)
    NK = 4096 if half else EXTN
    cvals = host_constants_full()
    cin = declare_constants(nc, cvals, ["ident_bf"])
    aps = {
        "x": nc.dram_tensor("x", [N, DM], F32, kind="ExternalInput").ap(),
        "Wq": nc.dram_tensor("Wq", [DM, DM], F32, kind="ExternalInput").ap(),
        "Wk": nc.dram_tensor("Wk", [DM, 1024], F32, kind="ExternalInput").ap(),
        "Wv": nc.dram_tensor("Wv", [DM, 1024], F32, kind="ExternalInput").ap(),
        "k_sc": nc.dram_tensor("k_sc", [NK, 1024], F32, kind="ExternalOutput").ap(),
        "v_sc": nc.dram_tensor("v_sc", [NK, 1024], BF16, kind="ExternalOutput").ap(),
        "q_sc": nc.dram_tensor("q_sc", [EXTN, DM], BF16, kind="ExternalOutput").ap(),
        "dbg_ln": nc.dram_tensor("dbg_ln", [128, DM], BF16, kind="ExternalOutput").ap(),
    }
    with contextlib.ExitStack() as pstack:
        pool = SemPool(nc, pstack)
        with contextlib.ExitStack() as stack:
            S = Sched(nc, "a1", pool, serialize=serialize,
                      strict_engines=strict_engines)
            cst = load_constants(nc, stack, S, cin, ["ident_bf"])
            emit_a1(nc, S, stack, half, aps, cst)
            S.finalize(stack)
    return nc, cvals



def build_stage_a(half, serialize=False, strict_engines=("V", "S", "P"), upto=3):
    nc = bass.Bass("TRN2", target_bir_lowering=False)
    NK = NV
    cvals = host_constants_full()
    cnames = ["ident_bf", "cum_lhsT", "ones16_col", "gla_maskT",
              "swa_maskA", "swa_maskB", "cos0", "sin0"]
    cin = declare_constants(nc, cvals, cnames)
    aps = {
        "x": nc.dram_tensor("x", [NV, DM], F32, kind="ExternalInput").ap(),
        "Wq": nc.dram_tensor("Wq", [DM, DM], F32, kind="ExternalInput").ap(),
        "Wk": nc.dram_tensor("Wk", [DM, 1024], F32, kind="ExternalInput").ap(),
        "Wv": nc.dram_tensor("Wv", [DM, 1024], F32, kind="ExternalInput").ap(),
        "Wo": nc.dram_tensor("Wo", [DM, DM], F32, kind="ExternalInput").ap(),
        "k_sc": nc.dram_tensor("k_sc", [NK, 1024], F32, kind="Internal").ap(),
        "v_sc": nc.dram_tensor("v_sc", [NK, 1024], BF16, kind="Internal").ap(),
        "q_sc": nc.dram_tensor("q_sc", [EXTN, DM], BF16, kind="Internal").ap(),
        "ocatT_sc": nc.dram_tensor("ocatT_sc", [DM, EXTN], BF16,
                                   kind="Internal").ap(),
        "y1_sc": nc.dram_tensor("y1_sc", [EXTN, DM], F32,
                                kind="ExternalOutput").ap(),
    }
    with contextlib.ExitStack() as pstack:
        pool = SemPool(nc, pstack)
        with contextlib.ExitStack() as st:
            S = Sched(nc, "a1", pool, serialize=serialize,
                      strict_engines=strict_engines)
            cst = load_constants(nc, st, S, cin, ["ident_bf"])
            emit_a1(nc, S, st, half, aps, cst)
            S.finalize(st)
        if upto >= 2:
            with contextlib.ExitStack() as st:
                S = Sched(nc, "a2", pool, serialize=serialize,
                          strict_engines=strict_engines)
                cst = load_constants(nc, st, S, cin,
                                     ["ident_bf", "cum_lhsT", "ones16_col",
                                      "gla_maskT", "swa_maskA",
                                      "swa_maskB"])
                cst["cos0_dram"] = cin["cos0"]
                cst["sin0_dram"] = cin["sin0"]
                emit_a2(nc, S, st, half, aps, cst)
                S.finalize(st)
        if upto >= 3:
            with contextlib.ExitStack() as st:
                S = Sched(nc, "a3", pool, serialize=serialize,
                          strict_engines=strict_engines)
                emit_a3(nc, S, st, half, aps, cst)
                S.finalize(st)
    return nc, cvals, cnames



def emit_b1(nc, S, stack, half, aps, cst):
    y1_sc, w1T, hT_sc = aps["y1_sc"], aps["w1T_bits"], aps["hT_sc"]
    s0b, s2c = aps["s0b"], aps["s2cols"]
    ident_bf = cst["ident_bf"]
    ones_bf = cst["ones_bf"]
    GT = 10
    sb = lambda nm, shape, dt: stack.enter_context(nc.sbuf_tensor(nm, shape, dt))
    ps = lambda nm, shape, dt: stack.enter_context(nc.psum_tensor(nm, shape, dt))
    lnT = sb("b1_lnT", [128, GT * 32 * 128], BF16)
    xa = [sb(f"b1_xa{i}", [128, DM], F32) for i in range(2)]
    lnb = [sb(f"b1_lnb{i}", [128, DM], BF16) for i in range(2)]
    lns = [sb(f"b1_lns{i}", [128, DM], BF16) for i in range(2)]
    s0sb = sb("b1_s0", [128, DM], BF16)
    s2sb = sb("b1_s2", [128, 32], F32)
    rt = sb("b1_rt", [128, EXTN // 128], F32)
    wpan = [sb(f"b1_wpan{i}", [128, 32 * PANW], BF16) for i in range(2)]
    hnat = [sb(f"b1_hn{i}", [128, PANW], BF16) for i in range(2)]
    hst = [sb(f"b1_hst{i}", [128, PANW], BF16) for i in range(2)]
    colbuf = sb("b1_col", [128, 8], F32)
    ptr = [ps(f"b1_ptr{i}", [128, 1024], BF16) for i in range(2)]
    pmm = [ps(f"b1_pmm{i}", [128, 512], F32) for i in range(2)]
    prt = ps("b1_prt", [128, 512], F32)
    ptt = ps("b1_ptt", [128, 1024], BF16)

    S.dma("Q", s0sb[:], s0b, writes=[("s0",)])
    S.dma("Q", s2sb[:], s2c, writes=[("s2",)])

    def lnT_sl(kt, ji):
        c = kt * (GT * 128) + ji * 128
        return lnT[:, c:c + 128]

    groups = [range(t, min(t + GT, EXTN // 128)) for t in range(0, EXTN // 128, GT)]
    pan_i = 0
    for gi, group in enumerate(groups):
        for j, ti in enumerate(group):
            xb, lb, ls = xa[j % 2], lnb[j % 2], lns[j % 2]
            xkey, lnkey, lskey = ("xa", j % 2), ("lnb", j % 2), ("lns", j % 2)
            ckey = ("col", j % 2)
            cols3 = (colbuf[:, (j % 2) * 3:(j % 2) * 3 + 1],
                     colbuf[:, (j % 2) * 3 + 1:(j % 2) * 3 + 2],
                     colbuf[:, (j % 2) * 3 + 2:(j % 2) * 3 + 3])
            S.dma("Q", xb[:], y1_sc[ti * 128:(ti + 1) * 128, :], writes=[xkey])
            emit_ln(S, xb[:], lb[:], cols3, xkey, lnkey, ckey)
            S.op("V", "tensor_mul", ls[:], lb[:], s0sb[:],
                 reads=[lnkey, ("s0",)], writes=[lskey])
            ji = ti - group[0]
            for kg in range(4):
                pt = ptr[kg % 2]
                for k8 in range(8):
                    kt = kg * 8 + k8
                    S.op("T", "transpose", pt[:, k8 * 128:(k8 + 1) * 128],
                         ls[:, kt * 128:(kt + 1) * 128], ident_bf[:],
                         reads=[lskey, ("cst", "ident_bf")],
                         writes=[("ptr", kg % 2, k8)])
                dst = lnT[:].rearrange("p (k t) -> p k t", k=32)[
                    :, kg * 8:(kg + 1) * 8, ji * 128:(ji + 1) * 128]
                S.op("V", "tensor_copy", dst,
                     pt[:].rearrange("p (k t) -> p k t", k=8),
                     reads=[("ptr", kg % 2, k8) for k8 in range(8)],
                     writes=[("lnT", ji, kg)] + [("ptr", kg % 2, k8)
                                                 for k8 in range(8)])
            for kt in range(32):
                S.op("T", "matmul", prt[:, 0:1], lnT_sl(kt, ji), ones_bf[:],
                     start=(kt == 0), stop=(kt == 31),
                     reads=[("lnT", ji, kt // 8), ("cst", "ones_bf")],
                     writes=[("prt",)])
            S.op("S", "copy", rt[:, ti:ti + 1], prt[:, 0:1],
                 reads=[("prt",)], writes=[("rt", ti), ("prt",)])
        for c0 in range(0, DM, PANW):
            wb = pan_i % 2
            S.dma("P", wpan[wb][:].rearrange("p (k j) -> p k j", k=32),
                  w1T[:, c0:c0 + PANW].rearrange("(k p) j -> p k j", p=128),
                  writes=[("wpan", wb)])
            for ri, ti in enumerate(group):
                ji = ti - group[0]
                pb = pmm[ri % 2]
                for kt in range(32):
                    S.op("T", "matmul", pb[:, 0:PANW], lnT_sl(kt, ji),
                         wpan[wb][:, kt * PANW:(kt + 1) * PANW],
                         start=(kt == 0), stop=(kt == 31),
                         reads=[("lnT", ji, kt // 8), ("wpan", wb)],
                         writes=[("pmm", ri % 2)])
                S.op("V", "tensor_scalar", hnat[ri % 2][:], pb[:, 0:PANW],
                     2.0, rt[:, ti:ti + 1], ALU.mult, ALU.subtract,
                     reads=[("pmm", ri % 2)],
                     writes=[("hnat", ri % 2), ("pmm", ri % 2)],
                     scalar_reads=[("rt", ti)])
                for z in range(2):
                    S.op("T", "transpose",
                         ptt[:, (2 * (ri % 2) + z) * 128:
                             (2 * (ri % 2) + z + 1) * 128],
                         hnat[ri % 2][:, z * 128:(z + 1) * 128], ident_bf[:],
                         reads=[("hnat", ri % 2), ("cst", "ident_bf")],
                         writes=[("ptt", ri % 2, z)])
                for z in range(2):
                    kt = c0 // 128 + z
                    S.op("V", "tensor_scalar_mul",
                         hst[ri % 2][:, z * 128:(z + 1) * 128],
                         ptt[:, (2 * (ri % 2) + z) * 128:
                             (2 * (ri % 2) + z + 1) * 128],
                         s2sb[:, kt:kt + 1],
                         reads=[("ptt", ri % 2, z)],
                         writes=[("hst", ri % 2), ("ptt", ri % 2, z)],
                         scalar_reads=[("s2",)])
                S.dma("Q", hT_sc[c0:c0 + PANW, ti * 128:(ti + 1) * 128]
                      .rearrange("(z p) t -> p z t", p=128),
                      hst[ri % 2][:].rearrange("p (z t) -> p z t", z=2),
                      reads=[("hst", ri % 2)], writes=[("hT", ti, c0)])
            pan_i += 1


def emit_b2(nc, S, stack, half, aps, cst):
    y1_sc, w3T, hT_sc, y2_sc = (aps["y1_sc"], aps["w3T_bits"], aps["hT_sc"],
                                aps["y2_sc"])
    s4b, biasb = aps["s4b"], aps["biasb"]
    ones_bf = cst["ones_bf"]
    GT = 10
    NT = EXTN // 128
    sb = lambda nm, shape, dt: stack.enter_context(nc.sbuf_tensor(nm, shape, dt))
    ps = lambda nm, shape, dt: stack.enter_context(nc.psum_tensor(nm, shape, dt))
    hT = sb("b2_hT", [128, 32 * GT * 128], BF16)
    wpan = [sb(f"b2_wpan{i}", [128, 32 * PANW], BF16) for i in range(2)]
    s4sb = sb("b2_s4", [128, DM], BF16)
    bsb = sb("b2_bias", [128, DM], BF16)
    rt = sb("b2_rt", [128, NT], F32)
    ys = [sb(f"b2_ys{i}", [128, PANW], F32) for i in range(2)]
    yb = [sb(f"b2_yb{i}", [128, PANW], F32) for i in range(2)]
    v1 = [sb(f"b2_v1{i}", [128, PANW], F32) for i in range(2)]
    ev = [sb(f"b2_ev{i}", [128, PANW], F32) for i in range(2)]
    pmm = [ps(f"b2_pmm{i}", [128, 512], F32) for i in range(2)]
    prt = ps("b2_prt", [128, 512], F32)

    S.dma("Q", s4sb[:], s4b, writes=[("s4",)])
    S.dma("Q", bsb[:], biasb, writes=[("bias",)])

    def hT_sl(kt, ji):
        c = kt * (GT * 128) + ji * 128
        return hT[:, c:c + 128]

    groups = [range(t, min(t + GT, NT)) for t in range(0, NT, GT)]
    pan_i = 0
    for gi, group in enumerate(groups):
        glen = len(group)
        t0 = group[0]
        S.dma("Q", hT[:, 0:32 * glen * 128].rearrange(
            "p (k t) -> p k t", k=32),
            hT_sc.rearrange("(k p) t -> p k t", p=128)[
                :, :, t0 * 128:(t0 + glen) * 128],
            writes=[("hT", gi)])
        for ji, ti in enumerate(group):
            for kt in range(32):
                S.op("T", "matmul", prt[:, 0:1], hT_sl(kt, ji), ones_bf[:],
                     start=(kt == 0), stop=(kt == 31),
                     reads=[("hT", gi), ("cst", "ones_bf")], writes=[("prt",)])
            S.op("S", "copy", rt[:, ti:ti + 1], prt[:, 0:1],
                 reads=[("prt",)], writes=[("rt", ti), ("prt",)])
        for c0 in range(0, DM, PANW):
            wb = pan_i % 2
            S.dma("P", wpan[wb][:].rearrange("p (k j) -> p k j", k=32),
                  w3T[:, c0:c0 + PANW].rearrange("(k p) j -> p k j", p=128),
                  writes=[("wpan", wb)])
            for ji, ti in enumerate(group):
                tb = ji % 2
                S.dma("Q", ys[tb][:], y1_sc[ti * 128:(ti + 1) * 128,
                                            c0:c0 + PANW], writes=[("ys", tb)])
                S.op("V", "tensor_add", yb[tb][:], ys[tb][:],
                     bsb[:, c0:c0 + PANW],
                     reads=[("ys", tb), ("bias",)], writes=[("yb", tb)])
                pb = pmm[tb]
                for kt in range(32):
                    S.op("T", "matmul", pb[:, 0:PANW], hT_sl(kt, ji),
                         wpan[wb][:, kt * PANW:(kt + 1) * PANW],
                         start=(kt == 0), stop=(kt == 31),
                         reads=[("hT", gi), ("wpan", wb)], writes=[("pmm", tb)])
                S.op("V", "tensor_scalar", v1[tb][:], pb[:, 0:PANW],
                     2.0, rt[:, ti:ti + 1], ALU.mult, ALU.subtract,
                     reads=[("pmm", tb)], writes=[("v1", tb), ("pmm", tb)],
                     scalar_reads=[("rt", ti)])
                S.op("V", "tensor_mul", v1[tb][:], v1[tb][:],
                     s4sb[:, c0:c0 + PANW],
                     reads=[("v1", tb), ("s4",)], writes=[("v1", tb)])
                S.op("V", "tensor_add", ev[tb][:], v1[tb][:], yb[tb][:],
                     reads=[("v1", tb), ("yb", tb)], writes=[("ev", tb)])
                S.dma("Q", y2_sc[ti * 128:(ti + 1) * 128, c0:c0 + PANW],
                      ev[tb][:], reads=[("ev", tb)], writes=[("y2", ti, c0)])
            pan_i += 1


def build_stage_b(half, serialize=False, strict_engines=("V", "S", "P")):
    nc = bass.Bass("TRN2", target_bir_lowering=False)
    cvals = host_constants_full()
    cnames = ["ident_bf", "ones_bf"]
    cin = declare_constants(nc, cvals, cnames)
    aps = {
        "y1_sc": nc.dram_tensor("y1_sc", [EXTN, DM], F32, kind="ExternalInput").ap(),
        "w1T_bits": nc.dram_tensor("w1T_bits", [DM, DM], I32,
                                   kind="ExternalInput").ap(),
        "w3T_bits": nc.dram_tensor("w3T_bits", [DM, DM], I32,
                                   kind="ExternalInput").ap(),
        "s0b": nc.dram_tensor("s0b", [128, DM], BF16, kind="ExternalInput").ap(),
        "s2cols": nc.dram_tensor("s2cols", [128, 32], F32,
                                 kind="ExternalInput").ap(),
        "s4b": nc.dram_tensor("s4b", [128, DM], BF16, kind="ExternalInput").ap(),
        "biasb": nc.dram_tensor("biasb", [128, DM], BF16,
                                kind="ExternalInput").ap(),
        "hT_sc": nc.dram_tensor("hT_sc", [DM, EXTN], BF16, kind="Internal").ap(),
        "y2_sc": nc.dram_tensor("y2_sc", [EXTN, DM], F32,
                                kind="ExternalOutput").ap(),
    }
    with contextlib.ExitStack() as pstack:
        pool = SemPool(nc, pstack)
        with contextlib.ExitStack() as st:
            S = Sched(nc, "b1", pool, serialize=serialize,
                      strict_engines=strict_engines)
            cst = load_constants(nc, st, S, cin, cnames)
            emit_b1(nc, S, st, half, aps, cst)
            S.finalize(st)
        with contextlib.ExitStack() as st:
            S = Sched(nc, "b2", pool, serialize=serialize,
                      strict_engines=strict_engines)
            cst = load_constants(nc, st, S, cin, cnames)
            emit_b2(nc, S, st, half, aps, cst)
            S.finalize(st)
    return nc, cvals, cnames


def host_b_inputs(inputs):
    w1T = np.ascontiguousarray(np.asarray(inputs["L0_ffn_w1_bits"]).T)
    w3T = np.ascontiguousarray(np.asarray(inputs["L0_ffn_w3_bits"]).T)
    s0 = np.asarray(inputs["L0_ffn_scaling0"], np.float32)
    s2 = np.asarray(inputs["L0_ffn_scaling2"], np.float32)
    s4 = np.asarray(inputs["L0_ffn_scaling4"], np.float32)
    bias = np.asarray(inputs["L0_ffn_bias"], np.float32)
    return {
        "w1T_bits": w1T, "w3T_bits": w3T,
        "s0b": np.ascontiguousarray(np.broadcast_to(s0.astype(NPBF16), (128, DM))),
        "s2cols": np.ascontiguousarray(s2.reshape(32, 128).T),
        "s4b": np.ascontiguousarray(np.broadcast_to(s4.astype(NPBF16), (128, DM))),
        "biasb": np.ascontiguousarray(np.broadcast_to(bias.astype(NPBF16),
                                                      (128, DM))),
    }




def c1_rope_nat(S, cosn, sinn, ropet, dst, src, nrep, dkey, skeys, scale=None,
                ropet2=None):
    def d3(t, c0):
        return t[:, 0:nrep * 128].rearrange(
            "p (k j) -> p k j", k=nrep)[:, :, c0:c0 + 64]
    pstep = cosn[:].ap[0][0]
    cb_lo = bass.AP(cosn, 0, [[pstep, 128], [0, nrep], [1, 64]])
    cb_hi = bass.AP(cosn, 64, [[pstep, 128], [0, nrep], [1, 64]])
    sn_lo = bass.AP(sinn, 0, [[pstep, 128], [0, nrep], [1, 64]])
    sn_hi = bass.AP(sinn, 64, [[pstep, 128], [0, nrep], [1, 64]])
    t2 = ropet2 if ropet2 is not None else ropet
    S.op("V", "tensor_mul", d3(ropet, 0), d3(src, 0), cb_lo,
         reads=skeys + [("cosn",)], writes=[("ropet", 0)])
    S.op("V", "tensor_mul", d3(ropet, 64), d3(src, 64), cb_hi,
         reads=skeys + [("cosn",)], writes=[("ropet", 64)])
    S.op("V", "tensor_mul", d3(t2, 0), d3(src, 64), sn_lo,
         reads=skeys + [("sinn",)], writes=[("ropet2", 0)])
    S.op("V", "tensor_mul", d3(t2, 64), d3(src, 0), sn_hi,
         reads=skeys + [("sinn",)], writes=[("ropet2", 64)])
    S.op("V", "tensor_sub", d3(dst, 0), d3(ropet, 0), d3(t2, 0),
         reads=[("ropet", 0), ("ropet2", 0)], writes=[dkey])
    S.op("V", "tensor_add", d3(dst, 64), d3(ropet, 64), d3(t2, 64),
         reads=[("ropet", 64), ("ropet2", 64)], writes=[dkey])
    if scale is not None:
        S.op("V", "tensor_scalar_mul", dst[:, 0:nrep * 128],
             dst[:, 0:nrep * 128], scale, reads=[dkey], writes=[dkey])


def emit_c1(nc, S, stack, half, aps, cst, kvbufs):
    e0 = E0
    own_off = OWN0
    y2_sc, wqT, wkvT, qn_sc = (aps["y2_sc"], aps["l1wqT"], aps["l1wkvT"],
                               aps["qn_sc"])
    cos_dram, sin_dram = aps["cst_cos1"], aps["cst_sin1"]
    ident_bf = cst["ident_bf"]
    KrT, Vnat = kvbufs
    GT = 10
    sb = lambda nm, shape, dt: stack.enter_context(nc.sbuf_tensor(nm, shape, dt))
    ps = lambda nm, shape, dt: stack.enter_context(nc.psum_tensor(nm, shape, dt))
    lnT = sb("c1_lnT", [128, GT * 32 * 128], BF16)
    xa = [sb(f"c1_xa{i}", [128, DM], F32) for i in range(2)]
    lnb = [sb(f"c1_lnb{i}", [128, DM], BF16) for i in range(2)]
    wqpan = [sb(f"c1_wq{i}", [128, 32 * PANW], BF16) for i in range(2)]
    wkvpan = sb("c1_wkv", [128, 32 * PANW], BF16)
    kvn = [sb(f"c1_kvn{i}", [128, PANW], BF16) for i in range(2)]
    krn = sb("c1_krn", [128, 128], BF16)
    qev = [sb(f"c1_qev{i}", [128, PANW], BF16) for i in range(2)]
    cosn = sb("c1_cosn", [128, 128], F32)
    sinn = sb("c1_sinn", [128, 128], F32)
    ropet = sb("c1_ropet", [128, 128], BF16)
    ropet2 = sb("c1_ropet2", [128, 128], BF16)
    colbuf = sb("c1_col", [128, 8], F32)
    ptr = [ps(f"c1_ptr{i}", [128, 1024], BF16) for i in range(2)]
    pmm = [ps(f"c1_pmm{i}", [128, 512], F32) for i in range(2)]
    pkv = ps("c1_pkv", [128, 512], F32)

    def lnT_sl(kt, ji):
        c = kt * (GT * 128) + ji * 128
        return lnT[:, c:c + 128]

    own_tiles = set(range(own_off // 128, own_off // 128 + 16))
    S.dma("P", wkvpan[:].rearrange("p (k j) -> p k j", k=32),
          wkvT[:, 0:PANW].rearrange("(k p) j -> p k j", p=128),
          writes=[("wkv",)])
    NT = EXTN // 128
    groups = [range(t, min(t + GT, NT)) for t in range(0, NT, GT)]
    pan_i = 0
    for gi, group in enumerate(groups):
        for j, ti in enumerate(group):
            xb, lb = xa[j % 2], lnb[j % 2]
            xkey, lnkey, ckey = ("xa", j % 2), ("lnb", j % 2), ("col", j % 2)
            cols3 = (colbuf[:, (j % 2) * 3:(j % 2) * 3 + 1],
                     colbuf[:, (j % 2) * 3 + 1:(j % 2) * 3 + 2],
                     colbuf[:, (j % 2) * 3 + 2:(j % 2) * 3 + 3])
            S.dma("Q", xb[:], y2_sc[ti * 128:(ti + 1) * 128, :], writes=[xkey])
            emit_ln(S, xb[:], lb[:], cols3, xkey, lnkey, ckey)
            ji = ti - group[0]
            for kg in range(4):
                pt = ptr[kg % 2]
                for k8 in range(8):
                    kt = kg * 8 + k8
                    S.op("T", "transpose", pt[:, k8 * 128:(k8 + 1) * 128],
                         lb[:, kt * 128:(kt + 1) * 128], ident_bf[:],
                         reads=[lnkey, ("cst", "ident_bf")],
                         writes=[("ptr", kg % 2, k8)])
                dst = lnT[:].rearrange("p (k t) -> p k t", k=32)[
                    :, kg * 8:(kg + 1) * 8, ji * 128:(ji + 1) * 128]
                S.op("V", "tensor_copy", dst,
                     pt[:].rearrange("p (k t) -> p k t", k=8),
                     reads=[("ptr", kg % 2, k8) for k8 in range(8)],
                     writes=[("lnT", ji, kg)] + [("ptr", kg % 2, k8)
                                                 for k8 in range(8)])
            for kt in range(32):
                S.op("T", "matmul", pkv[:, 0:PANW], lnT_sl(kt, ji),
                     wkvpan[:, kt * PANW:(kt + 1) * PANW],
                     start=(kt == 0), stop=(kt == 31),
                     reads=[("lnT", ji, kt // 8), ("wkv",)], writes=[("pkv",)])
            kb = kvn[ti % 2]
            S.op("V", "tensor_copy", kb[:], pkv[:, 0:PANW],
                 reads=[("pkv",)], writes=[("kvn", ti % 2), ("pkv",)])
            S.op("V", "tensor_copy", Vnat[:, ti * 128:(ti + 1) * 128],
                 kb[:, 128:256], reads=[("kvn", ti % 2)], writes=[("Vnat", ti)])
            r0 = e0 + ti * 128
            S.dma("Q", cosn[:], cos_dram[r0:r0 + 128, :], writes=[("cosn",)])
            S.dma("Q", sinn[:], sin_dram[r0:r0 + 128, :], writes=[("sinn",)])
            c1_rope_nat(S, cosn, sinn, ropet, krn, kb, 1,
                        ("krn",), [("kvn", ti % 2)], ropet2=ropet2)
            S.op("T", "transpose", ptr[0][:, 0:128], krn[:], ident_bf[:],
                 reads=[("krn",), ("cst", "ident_bf")], writes=[("ptr", 0, 0)])
            S.op("S", "copy", KrT[:, ti * 128:(ti + 1) * 128], ptr[0][:, 0:128],
                 reads=[("ptr", 0, 0)], writes=[("KrT", ti), ("ptr", 0, 0)])
        g_own = [ti for ti in group if ti in own_tiles]
        for c0 in range(0, DM, PANW):
            wb = pan_i % 2
            S.dma("P", wqpan[wb][:].rearrange("p (k j) -> p k j", k=32),
                  wqT[:, c0:c0 + PANW].rearrange("(k p) j -> p k j", p=128),
                  writes=[("wq", wb)])
            for ri, ti in enumerate(g_own):
                ji = ti - group[0]
                pb = pmm[ri % 2]
                for kt in range(32):
                    S.op("T", "matmul", pb[:, 0:PANW], lnT_sl(kt, ji),
                         wqpan[wb][:, kt * PANW:(kt + 1) * PANW],
                         start=(kt == 0), stop=(kt == 31),
                         reads=[("lnT", ji, kt // 8), ("wq", wb)],
                         writes=[("pmm", ri % 2)])
                ev = qev[ri % 2]
                S.op("V", "tensor_copy", ev[:], pb[:, 0:PANW],
                     reads=[("pmm", ri % 2)],
                     writes=[("qev", ri % 2), ("pmm", ri % 2)])
                q0 = ti * 128 - own_off
                S.dma("Q", qn_sc[q0:q0 + 128, c0:c0 + PANW], ev[:],
                      reads=[("qev", ri % 2)], writes=[("qn", ti, c0)])
            pan_i += 1


def emit_c1b(nc, S, stack, half, aps, cst):
    e0 = E0
    own_off = OWN0
    qn_sc, qrT_sc = aps["qn_sc"], aps["qrT_sc"]
    cos_dram, sin_dram = aps["cst_cos1"], aps["cst_sin1"]
    ident_bf = cst["ident_bf"]
    SC = 1.0 / float(np.sqrt(128.0))
    sb = lambda nm, shape, dt: stack.enter_context(nc.sbuf_tensor(nm, shape, dt))
    ps = lambda nm, shape, dt: stack.enter_context(nc.psum_tensor(nm, shape, dt))
    qa = [sb(f"cb_qa{i}", [128, DM], BF16) for i in range(2)]
    qrn2 = [sb(f"cb_qrn{i}", [128, DM], BF16) for i in range(2)]
    qst = [sb(f"cb_qst{i}", [128, DM], BF16) for i in range(2)]
    cosn = sb("cb_cosn", [128, 128], F32)
    sinn = sb("cb_sinn", [128, 128], F32)
    ropet = sb("cb_ropet", [128, DM], BF16)
    ropet2 = sb("cb_ropet2", [128, DM], BF16)
    ptr = [ps(f"cb_ptr{i}", [128, 1024], BF16) for i in range(2)]

    for ti in range(16):
        tb = ti % 2
        qrn = qrn2[tb]
        S.dma("Q", qa[tb][:], qn_sc[ti * 128:(ti + 1) * 128, :],
              writes=[("qa", tb)])
        r0 = e0 + own_off + ti * 128
        S.dma("Q", cosn[:], cos_dram[r0:r0 + 128, :], writes=[("cosn",)])
        S.dma("Q", sinn[:], sin_dram[r0:r0 + 128, :], writes=[("sinn",)])
        c1_rope_nat(S, cosn, sinn, ropet, qrn, qa[tb], 32, ("qrn", tb),
                    [("qa", tb)], scale=SC, ropet2=ropet2)
        if True:
            for kg in range(4):
                pt = ptr[kg % 2]
                for k8 in range(8):
                    kt = kg * 8 + k8
                    S.op("T", "transpose", pt[:, k8 * 128:(k8 + 1) * 128],
                         qrn[:, kt * 128:(kt + 1) * 128], ident_bf[:],
                         reads=[("qrn", tb), ("cst", "ident_bf")],
                         writes=[("ptr", kg % 2, k8)])
                S.op("V", "tensor_copy", qst[tb][:, kg * 1024:(kg + 1) * 1024],
                     pt[:],
                     reads=[("ptr", kg % 2, k8) for k8 in range(8)],
                     writes=[("qst", tb, kg)] + [("ptr", kg % 2, k8)
                                                 for k8 in range(8)])
        for kg in range(4):
            S.dma("Q", qrT_sc.rearrange("(k p) t -> p k t", p=128)[
                :, kg * 8:(kg + 1) * 8, ti * 128:(ti + 1) * 128],
                qst[tb][:].rearrange("p (k t) -> p k t", k=32)[
                    :, kg * 8:(kg + 1) * 8, :],
                reads=[("qst", tb, kg)], writes=[("qrT_out", ti, kg)])


def emit_c2(nc, S, stack, half, aps, cst, kvbufs, ocatT2):
    own_off = OWN0
    own_lo, own_hi = own_off, own_off + 2048
    qrT_sc = aps["qrT_sc"]
    ident_bf, ident_f = cst["ident_bf"], cst["ident_f32"]
    ones_bf = cst["ones_bf"]
    al = {0: cst["alpha_a"], 1: cst["alpha_b"]}
    alm = {0: cst["alpha1m_a"], 1: cst["alpha1m_b"]}
    alF = {0: cst["alphaF_a"], 1: cst["alphaF_b"]}
    alFm = {0: cst["alphaF1m_a"], 1: cst["alphaF1m_b"]}
    mlc = cst["mlast_cols"]
    KrT, Vnat = kvbufs
    sb = lambda nm, shape, dt: stack.enter_context(nc.sbuf_tensor(nm, shape, dt))
    ps = lambda nm, shape, dt: stack.enter_context(nc.psum_tensor(nm, shape, dt))
    qstream = [sb(f"c2_qs{i}", [128, 512], BF16) for i in range(2)]
    sigb = [sb(f"c2_sig{i}", [128, 4 * 512], BF16) for i in range(2)]
    oTb = [sb(f"c2_oT{i}", [128, 512], BF16) for i in range(2)]
    ocur = [sb(f"c2_oc{i}", [128, 512], BF16) for i in range(2)]
    tmpb = [sb(f"c2_tmp{i}", [128, 128], BF16) for i in range(2)]
    tails = sb("c2_tails", [128, 32 * 256], BF16)
    ocatn = [sb(f"c2_on{i}", [128, DM], BF16) for i in range(2)]
    ocst = sb("c2_ocst", [128, DM], BF16)
    rsrow = sb("c2_rs", [1, 512], F32)
    rq = [sb(f"c2_rq{i}", [128, 4], F32) for i in range(2)]
    PSC = [ps(f"c2_psc{i}", [128, 512], F32) for i in range(2)]
    PRS = ps("c2_prs", [128, 512], F32)
    POT = ps("c2_pot", [128, 512], F32)
    TRC = [ps(f"c2_trc{i}", [128, 1024], BF16) for i in range(2)]

    Vm = stack.enter_context(nc.sbuf_tensor("c2_Vm", [128, 512], BF16))
    mlb = stack.enter_context(nc.sbuf_tensor("c2_mlb", [128, 4], BF16))
    S.op("V", "tensor_copy", mlb[:], mlc[:], reads=[("cst", "mlast_cols")],
         writes=[("mlb",)])
    for z in range(4):
        S.op("V", "tensor_scalar_mul", Vm[:, z * 128:(z + 1) * 128],
             Vnat[:, (16 + z) * 128:(17 + z) * 128], mlc[:, z:z + 1],
             reads=[("Vnat", 16 + z), ("cst", "mlast_cols")],
             writes=[("Vm", z)])
    starts = list(range(0, 2049, 256))
    for wi, w in enumerate(starts):
        k0, k1 = w, min(w + 512, EXTN)
        q0, q1 = max(w, own_lo), min(w + 512, own_hi)
        NQ = q1 - q0
        kts = list(range(k0 // 128, k1 // 128))
        last_w = (wi == len(starts) - 1)
        emit_blend = wi >= 1
        emit_direct = False
        has_tails = (q1 >= w + 512)
        for h in range(32):
            hp = h % 2
            qw = qstream[hp]
            S.dma("Q", qw[:, 0:NQ],
                  qrT_sc[h * 128:(h + 1) * 128, q0 - own_off:q1 - own_off],
                  writes=[("qw", hp)])
            sg = sigb[hp]
            for zi, kt in enumerate(kts):
                pc = PSC[zi % 2]
                S.op("T", "matmul", pc[:, 0:NQ],
                     KrT[:, kt * 128:(kt + 1) * 128], qw[:, 0:NQ],
                     start=True, stop=True,
                     reads=[("KrT", kt), ("qw", hp)], writes=[("PSC", zi % 2)])
                S.op("S", "activation", sg[:, zi * 512:zi * 512 + NQ],
                     pc[:, 0:NQ], AF.Sigmoid,
                     reads=[("PSC", zi % 2)],
                     writes=[("sig", hp, zi), ("PSC", zi % 2)])
            for z in range(NQ // 128):
                for zi, kt in enumerate(kts):
                    rhs1 = (mlb[:, zi:zi + 1] if last_w else ones_bf[:])
                    rk = ("mlb",) if last_w else ("cst", "ones_bf")
                    S.op("T", "matmul", PRS[:, z:z + 1],
                         sg[:, zi * 512 + z * 128:zi * 512 + (z + 1) * 128],
                         rhs1,
                         start=(zi == 0), stop=(zi == len(kts) - 1),
                         reads=[("sig", hp, zi), rk],
                         writes=[("PRS",)])
            S.op("V", "reciprocal", rq[hp][:, 0:NQ // 128], PRS[:, 0:NQ // 128],
                 reads=[("PRS",)], writes=[("rq", hp), ("PRS",)])
            for zi, kt in enumerate(kts):
                vt = (Vm[:, zi * 128:(zi + 1) * 128] if last_w
                      else Vnat[:, kt * 128:(kt + 1) * 128])
                vk = ("Vm", zi) if last_w else ("Vnat", kt)
                S.op("T", "matmul", POT[:, 0:NQ], vt,
                     sg[:, zi * 512:zi * 512 + NQ],
                     start=(zi == 0), stop=(zi == len(kts) - 1),
                     reads=[vk, ("sig", hp, zi)], writes=[("POT",)])
            S.op("S", "copy", oTb[hp][:, 0:NQ], POT[:, 0:NQ],
                 reads=[("POT",)], writes=[("oTb", hp), ("POT",)])
            for z in range(NQ // 128):
                S.op("T", "transpose", TRC[hp][:, z * 128:(z + 1) * 128],
                     oTb[hp][:, z * 128:(z + 1) * 128], ident_bf[:],
                     reads=[("oTb", hp), ("cst", "ident_bf")],
                     writes=[("TRC", hp, z)])
                S.op("V", "tensor_scalar_mul", ocur[hp][:, z * 128:(z + 1) * 128],
                     TRC[hp][:, z * 128:(z + 1) * 128], rq[hp][:, z:z + 1],
                     reads=[("TRC", hp, z)],
                     writes=[("ocur", hp, z), ("TRC", hp, z)],
                     scalar_reads=[("rq", hp)])
            zoff = (q0 - w) // 128
            if emit_blend:
                AA = alF if wi == 1 else al
                AM = alFm if wi == 1 else alm
                ka = ("cst", ("alphaF_" if wi == 1 else "alpha_"))
                for z in range(2):
                    zz = z - zoff
                    sfx = "a" if z == 0 else "b"
                    S.op("V", "tensor_scalar_mul", tmpb[hp][:],
                         ocur[hp][:, zz * 128:(zz + 1) * 128],
                         AA[z][:],
                         reads=[("ocur", hp, zz), (ka[0], ka[1] + sfx)],
                         writes=[("tmpb", hp)])
                    S.op("V", "scalar_tensor_tensor",
                         ocatn[z][:, h * 128:(h + 1) * 128],
                         tails[:, h * 256 + z * 128:h * 256 + (z + 1) * 128],
                         AM[z][:], tmpb[hp][:], ALU.mult, ALU.add,
                         reads=[("tails", h, z), ("tmpb", hp)],
                         writes=[("ocatn", z, h)])
            if has_tails:
                for z in range(2):
                    zz = 2 + z - zoff
                    S.op("V", "tensor_copy",
                         tails[:, h * 256 + z * 128:h * 256 + (z + 1) * 128],
                         ocur[hp][:, zz * 128:(zz + 1) * 128],
                         reads=[("ocur", hp, zz)], writes=[("tails", h, z)])
        if emit_blend:
            blk = w - own_off
            for z in range(2):
                tcol = blk + z * 128
                for kg in range(4):
                    pt = TRC[kg % 2]
                    for k8 in range(8):
                        kt = kg * 8 + k8
                        S.op("T", "transpose", pt[:, k8 * 128:(k8 + 1) * 128],
                             ocatn[z][:, kt * 128:(kt + 1) * 128], ident_bf[:],
                             reads=[("ocatn", z, kt), ("cst", "ident_bf")],
                             writes=[("TRC", kg % 2, k8)])
                    dst = ocatT2[:].rearrange("p (k t) -> p k t", k=32)[
                        :, kg * 8:(kg + 1) * 8, tcol:tcol + 128]
                    S.op("V", "tensor_copy", dst,
                         pt[:].rearrange("p (k t) -> p k t", k=8),
                         reads=[("TRC", kg % 2, k8) for k8 in range(8)],
                         writes=[("ocT2", tcol, kg)] + [("TRC", kg % 2, k8)
                                                        for k8 in range(8)])


def emit_c3(nc, S, stack, half, aps, cst, ocatT2):
    own_off = OWN0
    y2_sc, woT, y3_sc = aps["y2_sc"], aps["l1woT"], aps["y3_sc"]
    sb = lambda nm, shape, dt: stack.enter_context(nc.sbuf_tensor(nm, shape, dt))
    ps = lambda nm, shape, dt: stack.enter_context(nc.psum_tensor(nm, shape, dt))
    wpan = [sb(f"c3_wpan{i}", [128, 32 * PANW], BF16) for i in range(2)]
    ys = [sb(f"c3_ys{i}", [128, PANW], F32) for i in range(2)]
    ev = [sb(f"c3_ev{i}", [128, PANW], F32) for i in range(2)]
    pmm = [ps(f"c3_pmm{i}", [128, 512], F32) for i in range(2)]

    def oc_sl(kt, ti):
        c = kt * 2048 + ti * 128
        return ocatT2[:, c:c + 128]

    for pi, c0 in enumerate(range(0, DM, PANW)):
        wb = pi % 2
        S.dma("P", wpan[wb][:].rearrange("p (k j) -> p k j", k=32),
              woT[:, c0:c0 + PANW].rearrange("(k p) j -> p k j", p=128),
              writes=[("wpan", wb)])
        for ti in range(16):
            tb = ti % 2
            r0 = own_off + ti * 128
            S.dma("Q", ys[tb][:], y2_sc[r0:r0 + 128, c0:c0 + PANW],
                  writes=[("ys", tb)])
            for kt in range(32):
                S.op("T", "matmul", pmm[tb][:, 0:PANW], oc_sl(kt, ti),
                     wpan[wb][:, kt * PANW:(kt + 1) * PANW],
                     start=(kt == 0), stop=(kt == 31),
                     reads=[("ocT2", "all"), ("wpan", wb)],
                     writes=[("pmm", tb)])
            S.op("V", "tensor_add", ev[tb][:], pmm[tb][:, 0:PANW], ys[tb][:],
                 reads=[("pmm", tb), ("ys", tb)],
                 writes=[("ev", tb), ("pmm", tb)])
            S.dma("Q", y3_sc[ti * 128:(ti + 1) * 128, c0:c0 + PANW], ev[tb][:],
                  reads=[("ev", tb)], writes=[("y3", ti, c0)])


def build_stage_c(half, serialize=False, strict_engines=("V", "S", "P"),
                  upto=4, ser_stages=()):
    nc = bass.Bass("TRN2", target_bir_lowering=False)
    cvals = host_constants_full()
    cnames = ["ident_bf", "ident_f32", "ones_bf", "alpha_a", "alpha_b",
              "alpha1m_a", "alpha1m_b", "alphaF_a", "alphaF_b",
              "alphaF1m_a", "alphaF1m_b", "mlast_cols"]
    cin = declare_constants(nc, cvals, cnames + ["cos1", "sin1"])
    aps = {
        "y2_sc": nc.dram_tensor("y2_sc", [EXTN, DM], F32,
                                kind="ExternalInput").ap(),
        "l1wqT": nc.dram_tensor("l1wqT", [DM, DM], F32,
                                kind="ExternalInput").ap(),
        "l1wkvT": nc.dram_tensor("l1wkvT", [DM, 256], F32,
                                 kind="ExternalInput").ap(),
        "l1woT": nc.dram_tensor("l1woT", [DM, DM], F32,
                                kind="ExternalInput").ap(),
        "qn_sc": nc.dram_tensor("qn_sc", [2048, DM], BF16, kind="Internal").ap(),
        "qrT_sc": nc.dram_tensor("qrT_sc", [DM, 2048], BF16,
                                 kind="Internal").ap(),
        "y3_sc": nc.dram_tensor("y3_sc", [2048, DM], F32,
                                kind="ExternalOutput").ap(),
        "cst_cos1": cin["cos1"], "cst_sin1": cin["sin1"],
    }
    with contextlib.ExitStack() as pstack:
      pool = SemPool(nc, pstack)
      with contextlib.ExitStack() as kvst:
        KrT = kvst.enter_context(nc.sbuf_tensor("c_KrT", [128, EXTN], BF16))
        Vnat = kvst.enter_context(nc.sbuf_tensor("c_Vnat", [128, EXTN], BF16))
        kvbufs = (KrT, Vnat)
        with contextlib.ExitStack() as st:
            S = Sched(nc, "c1", pool, serialize=serialize or ("c1" in ser_stages), strict_engines=strict_engines)
            cst = load_constants(nc, st, S, cin, ["ident_bf"])
            emit_c1(nc, S, st, half, aps, cst, kvbufs)
            S.finalize(st)
        if upto < 2:
            return nc, cvals, cnames + ["cos1", "sin1"]
        with contextlib.ExitStack() as st:
            S = Sched(nc, "c1b", pool, serialize=serialize or ("c1b" in ser_stages), strict_engines=strict_engines)
            cst = load_constants(nc, st, S, cin, ["ident_bf"])
            emit_c1b(nc, S, st, half, aps, cst)
            S.finalize(st)
        with contextlib.ExitStack() as ocst2:
            ocatT2 = ocst2.enter_context(
                nc.sbuf_tensor("c_ocT2", [128, 32 * 2048], BF16))
            if upto < 3:
                return nc, cvals, cnames + ["cos1", "sin1"]
            with contextlib.ExitStack() as st:
                S = Sched(nc, "c2", pool, serialize=serialize or ("c2" in ser_stages),
                          strict_engines=strict_engines)
                cst = load_constants(nc, st, S, cin, cnames)
                emit_c2(nc, S, st, half, aps, cst, kvbufs, ocatT2)
                S.finalize(st)
            if upto < 4:
                return nc, cvals, cnames + ["cos1", "sin1"]
            with contextlib.ExitStack() as st:
                S = Sched(nc, "c3", pool, serialize=serialize or ("c3" in ser_stages),
                          strict_engines=strict_engines)
                emit_c3(nc, S, st, half, aps, cst, ocatT2)
                S.finalize(st)
      pass
    return nc, cvals, cnames + ["cos1", "sin1"]


def host_c_inputs(inputs):
    wq = np.asarray(inputs["L1_attn_Wq"], np.float32)
    wk = np.asarray(inputs["L1_attn_Wk"], np.float32)
    wv = np.asarray(inputs["L1_attn_Wv"], np.float32)
    wo = np.asarray(inputs["L1_attn_Wo"], np.float32)
    return {
        "l1wqT": np.ascontiguousarray(wq.T),
        "l1wkvT": np.ascontiguousarray(
            np.concatenate([wk.T, wv.T], axis=1)),
        "l1woT": np.ascontiguousarray(wo.T),
    }



def emit_d(nc, S, stack, half, aps, cst):
    y3_sc, out_sc = aps["y3_sc"], aps["y4_out"]
    routerT, AT, BTs = aps["routerT"], aps["AT"], aps["BTs"]
    ident_bf = cst["ident_bf"]
    sb = lambda nm, shape, dt: stack.enter_context(nc.sbuf_tensor(nm, shape, dt))
    ps = lambda nm, shape, dt: stack.enter_context(nc.psum_tensor(nm, shape, dt))
    xa = [sb(f"d_xa{i}", [128, DM], F32) for i in range(2)]
    lnb = [sb(f"d_lnb{i}", [128, DM], BF16) for i in range(2)]
    lnT = sb("d_lnT", [128, 32 * 128], BF16)
    rtr = sb("d_rtr", [128, 32 * 4], BF16)
    atb = sb("d_at", [128, 32 * 64], BF16)
    btb = sb("d_bt", [64, DM], BF16)
    lg = sb("d_lg", [128, 4], F32)
    gcol = sb("d_gc", [128, 16], F32)
    gate = sb("d_gate", [128, 4], F32)
    midg = sb("d_midg", [128, 64], BF16)
    mT = sb("d_mT", [64, 128], BF16)
    ev = [sb(f"d_ev{i}", [128, 512], F32) for i in range(2)]
    colbuf = sb("d_col", [128, 8], F32)
    ptr = [ps(f"d_ptr{i}", [128, 1024], BF16) for i in range(2)]
    PL = ps("d_pl", [128, 512], F32)
    PM = ps("d_pm", [128, 512], F32)
    PD = [ps(f"d_pd{i}", [128, 512], F32) for i in range(2)]

    S.dma("P", rtr[:].rearrange("p (k j) -> p k j", k=32),
          routerT.rearrange("(k p) j -> p k j", p=128), writes=[("rtr",)])
    S.dma("P", atb[:].rearrange("p (k j) -> p k j", k=32),
          AT.rearrange("(k p) j -> p k j", p=128), writes=[("at",)])
    S.dma("P", btb[:], BTs, writes=[("bt",)])

    def lnT_sl(kt):
        return lnT[:, kt * 128:(kt + 1) * 128]

    for ti in range(16):
        tb = ti % 2
        xb, lb = xa[tb], lnb[tb]
        xkey, lnkey, ckey = ("xa", tb), ("lnb", tb), ("col", tb)
        cols3 = (colbuf[:, tb * 3:tb * 3 + 1], colbuf[:, tb * 3 + 1:tb * 3 + 2],
                 colbuf[:, tb * 3 + 2:tb * 3 + 3])
        S.dma("Q", xb[:], y3_sc[ti * 128:(ti + 1) * 128, :], writes=[xkey])
        emit_ln(S, xb[:], lb[:], cols3, xkey, lnkey, ckey)
        for kg in range(4):
            pt = ptr[kg % 2]
            for k8 in range(8):
                kt = kg * 8 + k8
                S.op("T", "transpose", pt[:, k8 * 128:(k8 + 1) * 128],
                     lb[:, kt * 128:(kt + 1) * 128], ident_bf[:],
                     reads=[lnkey, ("cst", "ident_bf")],
                     writes=[("ptr", kg % 2, k8)])
            dst = lnT[:].rearrange("p (k t) -> p k t", k=32)[
                :, kg * 8:(kg + 1) * 8, :]
            S.op("V", "tensor_copy", dst,
                 pt[:].rearrange("p (k t) -> p k t", k=8),
                 reads=[("ptr", kg % 2, k8) for k8 in range(8)],
                 writes=[("lnT", kg)] + [("ptr", kg % 2, k8)
                                         for k8 in range(8)])
        for kt in range(32):
            S.op("T", "matmul", PL[:, 0:4], lnT_sl(kt),
                 rtr[:, kt * 4:(kt + 1) * 4], start=(kt == 0), stop=(kt == 31),
                 reads=[("lnT", kt // 8), ("rtr",)], writes=[("PL",)])
        for kt in range(32):
            S.op("T", "matmul", PM[:, 0:64], lnT_sl(kt),
                 atb[:, kt * 64:(kt + 1) * 64], start=(kt == 0), stop=(kt == 31),
                 reads=[("lnT", kt // 8), ("at",)], writes=[("PM",)])
        nm1 = gcol[:, 0:1]
        nsec = gcol[:, 1:2]
        rsu = gcol[:, 2:3]
        S.op("V", "tensor_copy", lg[:], PL[:, 0:4],
             reads=[("PL",)], writes=[("lg",), ("PL",)])
        S.op("V", "reduce_max", nm1, lg[:], axis=AX.X, negate=True,
             reads=[("lg",)], writes=[("gc", 0)])
        S.op("V", "tensor_scalar_add", lg[:], lg[:], nm1,
             reads=[("lg",)], writes=[("lg",)], scalar_reads=[("gc", 0)])
        S.op("V", "tensor_scalar", gate[:], lg[:], 0.0, None, ALU.is_lt,
             reads=[("lg",)], writes=[("gate",)])
        S.op("V", "tensor_scalar", gate[:], gate[:], 1e30, 1e30,
             ALU.mult, ALU.subtract,
             reads=[("gate",)], writes=[("gate",)])
        S.op("V", "tensor_add", gate[:], gate[:], lg[:],
             reads=[("gate",), ("lg",)], writes=[("gate",)])
        S.op("V", "reduce_max", nsec, gate[:], axis=AX.X, negate=True,
             reads=[("gate",)], writes=[("gc", 1)])
        S.op("V", "tensor_scalar", gate[:], lg[:], nsec, None, ALU.add,
             reads=[("lg",)], writes=[("gate",)], scalar_reads=[("gc", 1)])
        S.op("V", "tensor_scalar", gate[:], gate[:], 0.0, None, ALU.is_ge,
             reads=[("gate",)], writes=[("gate",)])
        S.op("S", "activation", lg[:], lg[:], AF.Exp,
             reads=[("lg",)], writes=[("lg",)])
        S.op("V", "tensor_mul", gate[:], gate[:], lg[:],
             reads=[("gate",), ("lg",)], writes=[("gate",)])
        S.op("V", "reduce_sum", rsu, gate[:], axis=AX.X,
             reads=[("gate",)], writes=[("gc", 2)])
        S.op("V", "reciprocal", rsu, rsu,
             reads=[("gc", 2)], writes=[("gc", 2)])
        S.op("V", "tensor_scalar_mul", gate[:], gate[:], rsu,
             reads=[("gate",)], writes=[("gate",)], scalar_reads=[("gc", 2)])
        for e in range(4):
            S.op("V", "tensor_scalar_mul", midg[:, e * 16:(e + 1) * 16],
                 PM[:, e * 16:(e + 1) * 16], gate[:, e:e + 1],
                 reads=[("PM",)], writes=[("midg",)] + ([("PM",)] if e == 3 else []),
                 scalar_reads=[("gate",)])
        S.op("T", "transpose", ptr[0][0:64, 0:128], midg[:], ident_bf[:],
             reads=[("midg",), ("cst", "ident_bf")], writes=[("ptr", 0, 0)])
        S.op("V", "tensor_copy", mT[:], ptr[0][0:64, 0:128],
             reads=[("ptr", 0, 0)], writes=[("mT",), ("ptr", 0, 0)])
        for pi, c0 in enumerate(range(0, DM, 512)):
            pd = PD[pi % 2]
            S.op("T", "matmul", pd[:, 0:512], mT[:], btb[:, c0:c0 + 512],
                 start=True, stop=True,
                 reads=[("mT",), ("bt",)], writes=[("PD", pi % 2)])
            S.op("V", "scalar_tensor_tensor", ev[pi % 2][:], pd[:, 0:512],
                 4.0, xb[:, c0:c0 + 512], ALU.mult, ALU.add,
                 reads=[("PD", pi % 2), xkey],
                 writes=[("ev", pi % 2), ("PD", pi % 2)])
            S.dma("Q", out_sc[ti * 128:(ti + 1) * 128, c0:c0 + 512],
                  ev[pi % 2][:], reads=[("ev", pi % 2)],
                  writes=[("y4", ti, c0)])


def build_stage_d(half, serialize=False, strict_engines=("V", "S", "P")):
    nc = bass.Bass("TRN2", target_bir_lowering=False)
    cvals = host_constants_full()
    cnames = ["ident_bf"]
    cin = declare_constants(nc, cvals, cnames)
    aps = {
        "y3_sc": nc.dram_tensor("y3_sc", [2048, DM], F32,
                                kind="ExternalInput").ap(),
        "routerT": nc.dram_tensor("routerT", [DM, 4], F32,
                                  kind="ExternalInput").ap(),
        "AT": nc.dram_tensor("AT", [DM, 64], F32, kind="ExternalInput").ap(),
        "BTs": nc.dram_tensor("BTs", [64, DM], F32, kind="ExternalInput").ap(),
        "y4_out": nc.dram_tensor("y4_out", [2048, DM], F32,
                                 kind="ExternalOutput").ap(),
    }
    with contextlib.ExitStack() as pstack:
        pool = SemPool(nc, pstack)
        with contextlib.ExitStack() as st:
            S = Sched(nc, "d", pool, serialize=serialize,
                      strict_engines=strict_engines)
            cst = load_constants(nc, st, S, cin, cnames)
            emit_d(nc, S, st, half, aps, cst)
            S.finalize(st)
    return nc, cvals, cnames


def host_d_inputs(inputs):
    rw = np.asarray(inputs["L1_ffn_router_w"], np.float32)
    A = np.asarray(inputs["L1_ffn_A"], np.float32)
    B = np.asarray(inputs["L1_ffn_B"], np.float32)
    return {
        "routerT": np.ascontiguousarray(rw.T),
        "AT": np.ascontiguousarray(A.transpose(2, 0, 1).reshape(DM, 64)),
        "BTs": np.ascontiguousarray(B.transpose(0, 2, 1).reshape(64, DM)),
    }



def stage_feeds(cv, cn, half, extra):
    f = {("cst_" + nm): cv[nm] for nm in cn}
    for nm, v in percore_constants(cv, half).items():
        if ("cst_" + nm) in f:
            f["cst_" + nm] = v
    f.update(extra)
    return f


def kernel_multilaunch(**inputs):
    inputs = {k: np.asarray(v) for k, v in inputs.items()}
    x = inputs["L0_attn_hidden_states"].astype(np.float32, copy=False)
    cv = host_constants_full()
    hb = host_b_inputs(inputs)
    hc = host_c_inputs(inputs)
    hd = host_d_inputs(inputs)
    pairs = [(b, h) for b in range(4) for h in (0, 1)]
    cores = list(range(8))

    nc, _, cn = build_stage_a(0)
    feeds = [stage_feeds(cv, cn, h,
                         {"x": make_xeff(x[b], h),
                          "Wq": inputs["L0_attn_Wq"],
                          "Wk": inputs["L0_attn_Wk"],
                          "Wv": inputs["L0_attn_Wv"],
                          "Wo": inputs["L0_attn_Wo"]})
             for (b, h) in pairs]
    res = run_bass_kernel_spmd(nc, feeds, core_ids=cores)
    y1 = {p: res.results[i]["y1_sc"] for i, p in enumerate(pairs)}

    nc, _, cn = build_stage_b(0)
    feeds = [stage_feeds(cv, cn, h, {**hb, "y1_sc": y1[(b, h)]})
             for (b, h) in pairs]
    res = run_bass_kernel_spmd(nc, feeds, core_ids=cores)
    y2 = {p: res.results[i]["y2_sc"] for i, p in enumerate(pairs)}

    nc, _, cn = build_stage_c(0)
    feeds = [stage_feeds(cv, cn, h, {**hc, "y2_sc": y2[(b, h)]})
             for (b, h) in pairs]
    res = run_bass_kernel_spmd(nc, feeds, core_ids=cores)
    y3 = {p: res.results[i]["y3_sc"] for i, p in enumerate(pairs)}

    nc, _, cn = build_stage_d(0)
    feeds = [stage_feeds(cv, cn, h, {**hd, "y3_sc": y3[(b, h)]})
             for (b, h) in pairs]
    res = run_bass_kernel_spmd(nc, feeds, core_ids=cores)

    out = np.zeros((4, N, DM), np.float32)
    for i, (b, h) in enumerate(pairs):
        out[b, 2048 * h:2048 * h + 2048] = res.results[i]["y4_out"]
    return out



ALLCST = ["ident_bf", "ident_f32", "ones_bf", "cum_lhsT", "ones16_col",
          "gla_maskT", "swa_maskA", "swa_maskB", "cos0", "sin0", "cos1",
          "sin1", "alpha_a", "alpha_b", "alpha1m_a", "alpha1m_b",
          "alphaF_a", "alphaF_b", "alphaF1m_a", "alphaF1m_b", "mlast_cols"]


def build_full(serialize=False, strict_engines=("V", "S", "P")):
    nc = bass.Bass("TRN2", target_bir_lowering=False)
    cvals = host_constants_full()
    cin = declare_constants(nc, cvals, ALLCST)
    din = lambda nm, sh, dt: nc.dram_tensor(nm, sh, dt, kind="ExternalInput").ap()
    dint = lambda nm, sh, dt: nc.dram_tensor(nm, sh, dt, kind="Internal").ap()
    aps = {
        "x": din("x", [NV, DM], F32),
        "Wq": din("Wq", [DM, DM], F32),
        "Wk": din("Wk", [DM, 1024], F32),
        "Wv": din("Wv", [DM, 1024], F32),
        "Wo": din("Wo", [DM, DM], F32),
        "w1T_bits": din("w1T_bits", [DM, DM], I32),
        "w3T_bits": din("w3T_bits", [DM, DM], I32),
        "s0b": din("s0b", [128, DM], BF16),
        "s2cols": din("s2cols", [128, 32], F32),
        "s4b": din("s4b", [128, DM], BF16),
        "biasb": din("biasb", [128, DM], BF16),
        "l1wqT": din("l1wqT", [DM, DM], F32),
        "l1wkvT": din("l1wkvT", [DM, 256], F32),
        "l1woT": din("l1woT", [DM, DM], F32),
        "routerT": din("routerT", [DM, 4], F32),
        "AT": din("AT", [DM, 64], F32),
        "BTs": din("BTs", [64, DM], F32),
        "k_sc": dint("k_sc", [NV, 1024], F32),
        "v_sc": dint("v_sc", [NV, 1024], BF16),
        "q_sc": dint("q_sc", [EXTN, DM], BF16),
        "ocatT_sc": dint("ocatT_sc", [DM, EXTN], BF16),
        "y1_sc": dint("y1_sc", [EXTN, DM], F32),
        "hT_sc": dint("hT_sc", [DM, EXTN], BF16),
        "y2_sc": dint("y2_sc", [EXTN, DM], F32),
        "qn_sc": dint("qn_sc", [2048, DM], BF16),
        "qrT_sc": dint("qrT_sc", [DM, 2048], BF16),
        "y3_sc": dint("y3_sc", [2048, DM], F32),
        "y4_out": nc.dram_tensor("y4_out", [2048, DM], F32,
                                 kind="ExternalOutput").ap(),
        "cst_cos1": cin["cos1"], "cst_sin1": cin["sin1"],
    }
    SE = strict_engines
    with contextlib.ExitStack() as pstack:
        pool = SemPool(nc, pstack)

        def sched(nm):
            return Sched(nc, nm, pool, serialize=serialize, strict_engines=SE)

        with contextlib.ExitStack() as st:
            S = sched("a1")
            cst = load_constants(nc, st, S, cin, ["ident_bf"])
            emit_a1(nc, S, st, 0, aps, cst)
            S.finalize(st)
        with contextlib.ExitStack() as st:
            S = sched("a2")
            cst = load_constants(nc, st, S, cin,
                                 ["ident_bf", "cum_lhsT", "ones16_col",
                                  "gla_maskT", "swa_maskA", "swa_maskB"])
            cst["cos0_dram"] = cin["cos0"]
            cst["sin0_dram"] = cin["sin0"]
            emit_a2(nc, S, st, 0, aps, cst)
            S.finalize(st)
        with contextlib.ExitStack() as st:
            S = sched("a3")
            cst = load_constants(nc, st, S, cin, ["ident_bf"])
            emit_a3(nc, S, st, 0, aps, cst)
            S.finalize(st)
        with contextlib.ExitStack() as st:
            S = sched("b1")
            cst = load_constants(nc, st, S, cin, ["ident_bf", "ones_bf"])
            emit_b1(nc, S, st, 0, aps, cst)
            S.finalize(st)
        with contextlib.ExitStack() as st:
            S = sched("b2")
            cst = load_constants(nc, st, S, cin, ["ident_bf", "ones_bf"])
            emit_b2(nc, S, st, 0, aps, cst)
            S.finalize(st)
        with contextlib.ExitStack() as kvst:
            KrT = kvst.enter_context(nc.sbuf_tensor("c_KrT", [128, EXTN], BF16))
            Vnat = kvst.enter_context(nc.sbuf_tensor("c_Vnat", [128, EXTN],
                                                     BF16))
            kvbufs = (KrT, Vnat)
            with contextlib.ExitStack() as st:
                S = sched("c1")
                cst = load_constants(nc, st, S, cin, ["ident_bf"])
                emit_c1(nc, S, st, 0, aps, cst, kvbufs)
                S.finalize(st)
            with contextlib.ExitStack() as st:
                S = sched("c1b")
                cst = load_constants(nc, st, S, cin, ["ident_bf"])
                emit_c1b(nc, S, st, 0, aps, cst)
                S.finalize(st)
            with contextlib.ExitStack() as ocst2:
                ocatT2 = ocst2.enter_context(
                    nc.sbuf_tensor("c_ocT2", [128, 32 * 2048], BF16))
                with contextlib.ExitStack() as st:
                    S = sched("c2")
                    cst = load_constants(
                        nc, st, S, cin,
                        ["ident_bf", "ident_f32", "ones_bf", "alpha_a",
                         "alpha_b", "alpha1m_a", "alpha1m_b", "alphaF_a",
                         "alphaF_b", "alphaF1m_a", "alphaF1m_b",
                         "mlast_cols"])
                    emit_c2(nc, S, st, 0, aps, cst, kvbufs, ocatT2)
                    S.finalize(st)
                with contextlib.ExitStack() as st:
                    S = sched("c3")
                    emit_c3(nc, S, st, 0, aps, cst, ocatT2)
                    S.finalize(st)
        with contextlib.ExitStack() as st:
            S = sched("d")
            cst = load_constants(nc, st, S, cin, ["ident_bf"])
            emit_d(nc, S, st, 0, aps, cst)
            S.finalize(st)
    return nc, cvals


def kernel(**inputs):
    inputs = {k: np.asarray(v) for k, v in inputs.items()}
    x = inputs["L0_attn_hidden_states"].astype(np.float32, copy=False)
    cv = host_constants_full()
    hb = host_b_inputs(inputs)
    hc = host_c_inputs(inputs)
    hd = host_d_inputs(inputs)
    pairs = [(b, h) for b in range(4) for h in (0, 1)]
    nc, _ = build_full()
    feeds = []
    for (b, h) in pairs:
        f = {("cst_" + nm): cv[nm] for nm in ALLCST}
        for nm, v in percore_constants(cv, h).items():
            f["cst_" + nm] = v
        f.update({"x": make_xeff(x[b], h),
                  "Wq": inputs["L0_attn_Wq"], "Wk": inputs["L0_attn_Wk"],
                  "Wv": inputs["L0_attn_Wv"], "Wo": inputs["L0_attn_Wo"]})
        f.update(hb)
        f.update(hc)
        f.update(hd)
        feeds.append(f)
    res = run_bass_kernel_spmd(nc, feeds, core_ids=list(range(8)))
    out = np.zeros((4, N, DM), np.float32)
    for i, (b, h) in enumerate(pairs):
        out[b, 2048 * h:2048 * h + 2048] = res.results[i]["y4_out"]
    return out


# version 06
# speedup vs baseline: 1.1535x; 1.1535x over previous
import contextlib
import numpy as np
import ml_dtypes
import concourse.bass as bass
import concourse.mybir as mybir
from concourse.bass_utils import run_bass_kernel_spmd

F32 = mybir.dt.float32
BF16 = mybir.dt.bfloat16
I32 = mybir.dt.int32
AF = mybir.ActivationFunctionType
ALU = mybir.AluOpType
AX = mybir.AxisListType
NPBF16 = ml_dtypes.bfloat16

N = 4096
DM = 4096
NV = 4352
E0 = 1792
EXTN = 2560
OWN0 = 256
CH = 128
HALO = 256


ENGS = ("T", "V", "S", "P", "Q")
DMA_LANES = {"Q": 8, "P": 4}


class SemPool:

    def __init__(self, nc, stack):
        self.eng_sem = {e: stack.enter_context(nc.semaphore(f"sp_{e}"))
                        for e in ("T", "V", "S", "P")}
        self.eng_cnt = {e: 0 for e in ("T", "V", "S", "P")}
        self.lanes = {}
        self.lane_cnt = {}
        self.lane_next = {}
        for q, n in DMA_LANES.items():
            self.lanes[q] = [stack.enter_context(nc.semaphore(f"sp_{q}l{i}"))
                             for i in range(n)]
            self.lane_cnt[q] = [0] * n
            self.lane_next[q] = 0


class Sched:
    def __init__(self, nc, name, pool, serialize=False,
                 strict_engines=("V", "S", "P")):
        self.nc = nc
        self.name = name
        self.pool = pool
        self.serialize = serialize
        self.strict_engines = set(strict_engines)
        self.ops = []
        self.last_writer = {}
        self.readers = {}
        self.eng_pos = {e: [] for e in ENGS}

    def add(self, eng, fn, reads=(), writes=(), scalar_reads=(), dma=False):
        idx = len(self.ops)
        deps = set()
        strict = set()
        if self.serialize and self.ops:
            deps.add(idx - 1)
            strict.add(idx - 1)
        for k in tuple(reads) + tuple(scalar_reads):
            w = self.last_writer.get(k)
            if w is not None:
                deps.add(w)
                if k in scalar_reads:
                    strict.add(w)
        for k in writes:
            w = self.last_writer.get(k)
            if w is not None:
                deps.add(w)
            for rd in self.readers.get(k, ()):
                deps.add(rd)
        if eng in self.strict_engines:
            strict |= deps
        op = dict(eng=eng, fn=fn, dma=dma, deps=deps, strict=strict, idx=idx,
                  need_inc=False)
        self.ops.append(op)
        self.eng_pos[eng].append(idx)
        for k in tuple(reads) + tuple(scalar_reads):
            self.readers.setdefault(k, []).append(idx)
        for k in writes:
            self.last_writer[k] = idx
            self.readers[k] = []
        return idx

    def op(self, eng, meth, *args, reads=(), writes=(), scalar_reads=(),
           dma=False, **kwargs):
        reads = list(reads)
        writes = list(writes)
        seen_out = False
        for a in args:
            if isinstance(a, bass.AP):
                if type(a.tensor).__name__.startswith("PSum"):
                    k = ("BK", a.tensor.name)
                    if not seen_out:
                        writes.append(k)
                    elif k not in reads:
                        reads.append(k)
                seen_out = True
            elif a is not None and not isinstance(a, (int, float)):
                seen_out = True
        return self.add(eng, (meth, args, kwargs), reads=reads, writes=writes,
                        scalar_reads=scalar_reads, dma=dma)

    def dma(self, eng, out_ap, in_ap, reads=(), writes=(), **kw):
        return self.op(eng, "dma_start", out_ap, in_ap, reads=reads,
                       writes=writes, dma=True, **kw)

    def finalize(self, stack, final_waits=True, dump=None):
        pool = self.pool
        for op in self.ops:
            if op["dma"]:
                op["need_inc"] = True
            for d in op["deps"]:
                dep = self.ops[d]
                if (dep["eng"] != op["eng"]) or dep["dma"] or (d in op["strict"]):
                    dep["need_inc"] = True
        target = {}
        for op in self.ops:
            if not op["need_inc"]:
                continue
            e = op["eng"]
            if op["dma"]:
                li = pool.lane_next[e]
                pool.lane_next[e] = (li + 1) % len(pool.lanes[e])
                pool.lane_cnt[e][li] += 16
                op["lane"] = li
                target[op["idx"]] = (pool.lanes[e][li], pool.lane_cnt[e][li])
            else:
                pool.eng_cnt[e] += 1
                target[op["idx"]] = (pool.eng_sem[e], pool.eng_cnt[e])
        blk = stack.enter_context(nc_block(self.nc, self.name))

        def body(eng_key):
            def _(e):
                watermark = {}
                for i in self.eng_pos[eng_key]:
                    op = self.ops[i]
                    need = {}
                    for d in op["deps"]:
                        dep = self.ops[d]
                        if (dep["eng"] == eng_key and not dep["dma"]
                                and d not in op["strict"]):
                            continue
                        sem, v = target[d]
                        if v > watermark.get(id(sem), 0):
                            kk = id(sem)
                            if kk not in need or need[kk][1] < v:
                                need[kk] = (sem, v)
                    for kk in sorted(need):
                        sem, v = need[kk]
                        e.wait_ge(sem, v)
                        watermark[kk] = v
                    fn = op["fn"]
                    if callable(fn):
                        inst = fn(e)
                        desc = getattr(fn, "__name__", "fn")
                    else:
                        meth, args, kwargs = fn
                        inst = getattr(e, meth)(*args, **kwargs)
                        desc = meth
                    if dump is not None:
                        tg = target.get(op["idx"])
                        dump.write(f"{eng_key} op{op['idx']} {desc} "
                                   f"tgt={tg[1] if tg else '-'} "
                                   f"waits={[(s.name if hasattr(s,'name') else '?', v) for _, (s, v) in sorted(need.items())]}\n")
                    if op["need_inc"]:
                        sem, v = target[op["idx"]]
                        inst.then_inc(sem, 16 if op["dma"] else 1)
                if final_waits and eng_key == "Q":
                    for en in ("T", "V", "S", "P"):
                        if pool.eng_cnt[en] > 0:
                            e.wait_ge(pool.eng_sem[en], pool.eng_cnt[en])
                    for q in DMA_LANES:
                        for li, sem in enumerate(pool.lanes[q]):
                            if pool.lane_cnt[q][li] > 0:
                                e.wait_ge(sem, pool.lane_cnt[q][li])
            return _

        for ek, meth in (("Q", blk.sync), ("P", blk.gpsimd), ("T", blk.tensor),
                         ("V", blk.vector), ("S", blk.scalar)):
            if self.eng_pos[ek] or (final_waits and ek == "Q"):
                meth(body(ek))


def nc_block(nc, name):
    return nc.Block(name)



def _rope_l0_tables():
    half = np.arange(64, dtype=np.float64)
    inv = 1.0 / 10000.0 ** (2.0 * half / 128.0)
    fr = np.arange(N, dtype=np.float64)[:, None] * inv[None, :]
    emb = np.concatenate([fr, fr], axis=-1)
    return np.cos(emb).astype(np.float32), np.sin(emb).astype(np.float32)


def _rope_l1_tables():
    inv = 1.0 / 10000.0 ** (np.arange(0, 128, 2, dtype=np.float32) / 128.0)
    t = np.arange(N, dtype=np.float32)
    f = np.outer(t.astype(np.float64), inv.astype(np.float64))
    cos = np.zeros((N, 128), np.float32)
    sin = np.zeros((N, 128), np.float32)
    cos[:, 0::2] = np.cos(f); cos[:, 1::2] = np.cos(f)
    sin[:, 0::2] = np.sin(f); sin[:, 1::2] = np.sin(f)
    return cos, sin


def host_constants():
    c = {}
    ut = np.triu(np.ones((CH, CH), np.float32)) / 16.0
    c["cum_lhsT"] = ut
    c["ones_col"] = np.ones((128, 1), np.float32)
    c["gla_maskT"] = np.triu(np.ones((CH, CH), np.float32)).astype(NPBF16)
    a = np.arange(128)[:, None]
    cc = np.arange(256)[None, :]
    rel = a + 128 - cc
    gen = ((rel >= 0) & (rel <= 64)).astype(np.float32)
    c["swa_mask_gen"] = gen
    c["swa_mask_first"] = np.ascontiguousarray(gen * (cc >= 128))
    c["cos0"], c["sin0"] = _rope_l0_tables()
    c["cos1"], c["sin1"] = _rope_l1_tables()
    alpha = np.linspace(0.0, 1.0, 256, dtype=np.float32)
    c["alpha_a"] = np.ascontiguousarray(alpha[:128].reshape(128, 1))
    c["alpha_b"] = np.ascontiguousarray(alpha[128:].reshape(128, 1))
    c["alpha1m_a"] = np.ascontiguousarray((1.0 - alpha[:128]).reshape(128, 1))
    c["alpha1m_b"] = np.ascontiguousarray((1.0 - alpha[128:]).reshape(128, 1))
    return c



class Prog:
    def __init__(self, name):
        self.nc = bass.Bass("TRN2", target_bir_lowering=False)
        self.name = name
        self.inputs = {}
        self.stack = None

    def dram_in(self, name, arr_shape, dtype):
        return self.nc.dram_tensor(name, list(arr_shape), dtype,
                                   kind="ExternalInput").ap()

    def dram_out(self, name, arr_shape, dtype):
        return self.nc.dram_tensor(name, list(arr_shape), dtype,
                                   kind="ExternalOutput").ap()

    def dram_tmp(self, name, arr_shape, dtype):
        return self.nc.dram_tensor(name, list(arr_shape), dtype,
                                   kind="Internal").ap()


_CST_SEQ = [0]


def load_constants(nc, stack, S, cst_aps, names):
    out = {}
    _CST_SEQ[0] += 1
    for nm in names:
        ap = cst_aps[nm]
        shape = list(ap.shape)
        dt = ap.dtype
        t = stack.enter_context(nc.sbuf_tensor(f"cs{_CST_SEQ[0]}_" + nm, shape, dt))
        S.dma("Q", t[:], ap, writes=[("cst", nm)])
        out[nm] = t
    return out



def emit_ln(S, x_t, ln_t, cols3, xkey, lnkey, ckey):
    negmean, rstd, _ = cols3
    S.op("V", "reduce_sum", negmean, x_t, axis=AX.X, negate=True,
         reads=[xkey], writes=[(ckey, "nm")])
    S.op("V", "tensor_scalar_mul", negmean, negmean, 1.0 / DM,
         reads=[(ckey, "nm")], writes=[(ckey, "nm")])
    S.op("S", "activation", ln_t, x_t, AF.Square, bias=negmean, scale=1.0,
         accum_out=rstd,
         reads=[xkey], writes=[lnkey, (ckey, "var")], scalar_reads=[(ckey, "nm")])
    S.op("V", "tensor_scalar_add", rstd, rstd, DM * 1e-5,
         reads=[(ckey, "var")], writes=[(ckey, "var")])
    S.op("S", "activation", rstd, rstd, AF.Ln,
         reads=[(ckey, "var")], writes=[(ckey, "var")])
    S.op("V", "tensor_scalar", rstd, rstd, -0.5, 0.5 * float(np.log(DM)),
         ALU.mult, ALU.add,
         reads=[(ckey, "var")], writes=[(ckey, "var")])
    S.op("S", "activation", rstd, rstd, AF.Exp,
         reads=[(ckey, "var")], writes=[(ckey, "var")])
    S.op("V", "tensor_scalar", ln_t, x_t, negmean, rstd, ALU.add, ALU.mult,
         reads=[xkey], writes=[lnkey],
         scalar_reads=[(ckey, "nm"), (ckey, "var")])



PANW = 256


def emit_a1(nc, S, stack, half, aps, cst):
    KT = NV // 128
    e0 = E0
    ext_tiles = set(range(e0 // 128, e0 // 128 + EXTN // 128))
    GT = 14
    x_ap, wq_ap, wk_ap, wv_ap = aps["x"], aps["Wq"], aps["Wk"], aps["Wv"]
    k_sc, v_sc, q_sc = aps["k_sc"], aps["v_sc"], aps["q_sc"]
    ident_bf = cst["ident_bf"]

    sb = lambda nm, shape, dt: stack.enter_context(nc.sbuf_tensor(nm, shape, dt))
    ps = lambda nm, shape, dt: stack.enter_context(nc.psum_tensor(nm, shape, dt))
    lnT = sb("a1_lnT", [128, GT * 32 * 128], BF16)
    xa = [sb(f"a1_xa{i}", [128, DM], F32) for i in range(2)]
    lnb = [sb(f"a1_lnb{i}", [128, DM], BF16) for i in range(2)]
    wpan = [sb(f"a1_wpan{i}", [128, 32 * PANW], BF16) for i in range(2)]
    evf = [sb(f"a1_evf{i}", [128, PANW], F32) for i in range(2)]
    evb = [sb(f"a1_evb{i}", [128, PANW], BF16) for i in range(2)]
    colbuf = sb("a1_col", [128, 8], F32)
    ptr = [ps(f"a1_ptr{i}", [128, 1024], BF16) for i in range(2)]
    pmm = [ps(f"a1_pmm{i}", [128, 512], F32) for i in range(2)]

    def lnT_sl(kt, ji):
        c = kt * (GT * 128) + ji * 128
        return lnT[:, c:c + 128]

    panels = ([("k", wk_ap, k_sc, "f32", c0) for c0 in range(0, 1024, PANW)]
              + [("v", wv_ap, v_sc, "bf16", c0) for c0 in range(0, 1024, PANW)]
              + [("q", wq_ap, q_sc, "bf16", c0) for c0 in range(0, DM, PANW)])
    groups = [range(t, min(t + GT, KT)) for t in range(0, KT, GT)]
    pan_i = 0

    for gi, group in enumerate(groups):
        g_ext = [ti for ti in group if ti in ext_tiles]
        for j, ti in enumerate(group):
            xb, lb = xa[j % 2], lnb[j % 2]
            xkey, lnkey, ckey = ("xa", j % 2), ("lnb", j % 2), ("col", j % 2)
            cols3 = (colbuf[:, (j % 2) * 3:(j % 2) * 3 + 1],
                     colbuf[:, (j % 2) * 3 + 1:(j % 2) * 3 + 2],
                     colbuf[:, (j % 2) * 3 + 2:(j % 2) * 3 + 3])
            S.dma("Q", xb[:], x_ap[ti * 128:(ti + 1) * 128, :], writes=[xkey])
            emit_ln(S, xb[:], lb[:], cols3, xkey, lnkey, ckey)
            ji = ti - group[0]
            for kg in range(4):
                pt = ptr[kg % 2]
                for k8 in range(8):
                    kt = kg * 8 + k8
                    S.op("T", "transpose", pt[:, k8 * 128:(k8 + 1) * 128],
                         lb[:, kt * 128:(kt + 1) * 128], ident_bf[:],
                         reads=[lnkey, ("cst", "ident_bf")],
                         writes=[("ptr", kg % 2, k8)])
                dst = lnT[:].rearrange("p (k t) -> p k t", k=32)[
                    :, kg * 8:(kg + 1) * 8, ji * 128:(ji + 1) * 128]
                S.op("V", "tensor_copy", dst,
                     pt[:].rearrange("p (k t) -> p k t", k=8),
                     reads=[("ptr", kg % 2, k8) for k8 in range(8)],
                     writes=[("lnT", ji, kg)] + [("ptr", kg % 2, k8)
                                                 for k8 in range(8)])
        plist = [p for p in panels if not (p[0] == "q" and not g_ext)]
        for (nm, w_ap, sc_ap, dt, c0) in plist:
            wb = pan_i % 2
            rows = g_ext if nm == "q" else list(group)
            S.dma("P", wpan[wb][:].rearrange("p (k j) -> p k j", k=32),
                  w_ap[:, c0:c0 + PANW].rearrange("(k p) j -> p k j", p=128),
                  writes=[("wpan", wb)])
            for ri, ti in enumerate(rows):
                ji = ti - group[0]
                pb = pmm[ri % 2]
                for kt in range(32):
                    S.op("T", "matmul", pb[:, 0:PANW], lnT_sl(kt, ji),
                         wpan[wb][:, kt * PANW:(kt + 1) * PANW],
                         start=(kt == 0), stop=(kt == 31),
                         reads=[("lnT", ji, kt // 8), ("wpan", wb)],
                         writes=[("pmm", ri % 2)])
                ev = evf[ri % 2] if dt == "f32" else evb[ri % 2]
                S.op("V", "tensor_copy", ev[:, 0:PANW], pb[:, 0:PANW],
                     reads=[("pmm", ri % 2)],
                     writes=[("ev", dt, ri % 2), ("pmm", ri % 2)])
                r0 = ti * 128 - (e0 if nm == "q" else 0)
                S.dma("Q", sc_ap[r0:r0 + 128, c0:c0 + PANW], ev[:, 0:PANW],
                      reads=[("ev", dt, ri % 2)], writes=[("sc", nm, ti, c0)])
            pan_i += 1



def bcast_mid(t, p0, p1, nrep, c0, c1):
    ap = t[:]
    pstep = ap.ap[0][0]
    return bass.AP(t, p0 * pstep + c0, [[pstep, p1 - p0], [0, nrep], [1, c1 - c0]])


def emit_a2(nc, S, stack, half, aps, cst):
    e0 = E0
    chunks = list(range(0, e0 + EXTN, 128))
    n_prefix = e0 // 128
    k_sc, v_sc, q_sc, ocatT_sc = (aps["k_sc"], aps["v_sc"], aps["q_sc"],
                                  aps["ocatT_sc"])
    SC = 1.0 / float(np.sqrt(128.0))

    sb = lambda nm, shape, dt: stack.enter_context(nc.sbuf_tensor(nm, shape, dt))
    ps = lambda nm, shape, dt: stack.enter_context(nc.psum_tensor(nm, shape, dt))
    kc = [sb(f"a2_kc{i}", [128, 1024], F32) for i in range(2)]
    vc = [sb(f"a2_vc{i}", [128, 1024], BF16) for i in range(3)]
    qc = [sb(f"a2_qc{i}", [128, DM], BF16) for i in range(2)]
    gmin = sb("a2_gmin", [128, 1024], F32)
    gtmp = sb("a2_gtmp", [128, 1024], F32)
    gs = sb("a2_gs", [128, 1024], F32)
    eB = sb("a2_eB", [128, 1024], F32)
    eBn = sb("a2_eBn", [128, 1024], F32)
    eBC = sb("a2_eBC", [128, 8], F32)
    eks = sb("a2_eks", [128, 1024], F32)
    ktil = sb("a2_ktil", [128, 1024], BF16)
    ktilT = [sb(f"a2_ktilT{i}", [128, 1024], BF16) for i in range(2)]
    Sst = [sb(f"a2_S{i}", [128, 1024], F32) for i in range(2)]
    Sb = [sb(f"a2_Sb{i}", [128, 1024], BF16) for i in range(2)]
    kbf = sb("a2_kbf", [128, 1024], BF16)
    krn = sb("a2_krn", [128, 1024], BF16)
    krT = [sb(f"a2_krT{i}", [128, 1024], BF16) for i in range(3)]
    qrn = sb("a2_qrn", [128, DM], BF16)
    qrT = [sb(f"a2_qrT{i}", [128, DM], BF16) for i in range(2)]
    cosn = sb("a2_cosn", [128, 128], F32)
    sinn = sb("a2_sinn", [128, 128], F32)
    QT = [sb(f"a2_QT{i}", [128, DM], BF16) for i in range(2)]
    qs_all = sb("a2_qsall", [128, DM], BF16)
    eqa = qs_all
    qt_all = sb("a2_qtall", [128, DM], BF16)
    colq = sb("a2_colq", [128, 64], F32)
    ptb = [sb(f"a2_ptb{i}", [128, 128], BF16) for i in range(2)]
    esb = sb("a2_esb", [128, 256], F32)
    punb = sb("a2_punb", [128, 256], BF16)
    PT1 = sb("a2_PT1", [128, 128], BF16)
    PT2 = sb("a2_PT2", [128, 128], BF16)
    ogla = [sb(f"a2_ogla{i}", [128, 128], F32) for i in range(2)]
    o_ = [sb(f"a2_o{i}", [128, DM], BF16) for i in range(2)]
    ocat = [sb(f"a2_ocat{i}", [128, DM], BF16) for i in range(2)]
    colb = sb("a2_colb", [128, 24], F32)
    ropet = sb("a2_ropet", [128, DM], BF16)
    ropet2 = sb("a2_ropet2", [128, DM], BF16)

    PB = ps("a2_PB", [128, 512], F32)
    PBC = ps("a2_PBC", [128, 512], F32)
    TRB = [ps(f"a2_TRB{i}", [128, 1024], BF16) for i in range(2)]
    PX = ps("a2_PX", [128, 512], F32)
    PO = ps("a2_PO", [128, 512], F32)
    PSW = [ps(f"a2_PSW{i}", [128, 512], F32) for i in range(2)]

    cum, ones16 = cst["cum_lhsT"], cst["ones16_col"]
    gmask = cst["gla_maskT"]
    smasks = {"gen": cst["swa_maskB"], "first": cst["swa_maskA"]}
    cos_dram, sin_dram = cst["cos0_dram"], cst["sin0_dram"]
    identb = cst["ident_bf"]

    S.op("V", "memset", krT[2][:], 0.0, writes=[("krT", 2)])
    S.op("V", "memset", vc[2][:], 0.0, writes=[("vc", 2)])
    S.op("V", "memset", Sst[0][:], 0.0,
         writes=[("Sst", 0, kv) for kv in range(8)])

    def tr(dst_ps, dst_key, src_ap, src_keys):
        S.op("T", "transpose", dst_ps, src_ap, identb[:],
             reads=list(src_keys) + [("cst", "ident_bf")], writes=[dst_key])

    def rope_nat(dst, src, nrep, dkey, skeys, scale=None):
        def d3(t, c0):
            return t[:, 0:nrep * 128].rearrange(
                "p (k j) -> p k j", k=nrep)[:, :, c0:c0 + 64]
        pstep = cosn[:].ap[0][0]
        cb_lo = bass.AP(cosn, 0, [[pstep, 128], [0, nrep], [1, 64]])
        cb_hi = bass.AP(cosn, 64, [[pstep, 128], [0, nrep], [1, 64]])
        sn_lo = bass.AP(sinn, 0, [[pstep, 128], [0, nrep], [1, 64]])
        sn_hi = bass.AP(sinn, 64, [[pstep, 128], [0, nrep], [1, 64]])
        S.op("V", "tensor_mul", d3(ropet, 0), d3(src, 0), cb_lo,
             reads=skeys + [("cosn",)], writes=[("ropet", 0)])
        S.op("V", "tensor_mul", d3(ropet, 64), d3(src, 64), cb_hi,
             reads=skeys + [("cosn",)], writes=[("ropet", 64)])
        S.op("V", "tensor_mul", d3(ropet2, 0), d3(src, 64), sn_lo,
             reads=skeys + [("sinn",)], writes=[("ropet2", 0)])
        S.op("V", "tensor_mul", d3(ropet2, 64), d3(src, 0), sn_hi,
             reads=skeys + [("sinn",)], writes=[("ropet2", 64)])
        S.op("V", "tensor_sub", d3(dst, 0), d3(ropet, 0), d3(ropet2, 0),
             reads=[("ropet", 0), ("ropet2", 0)], writes=[dkey])
        S.op("V", "tensor_add", d3(dst, 64), d3(ropet, 64), d3(ropet2, 64),
             reads=[("ropet", 64), ("ropet2", 64)], writes=[dkey])
        if scale is not None:
            S.op("V", "tensor_scalar_mul", dst[:, 0:nrep * 128],
                 dst[:, 0:nrep * 128], scale, reads=[dkey], writes=[dkey])

    for ci, r0 in enumerate(chunks):
        ext = r0 >= e0
        cur, prv = ci % 2, (ci + 1) % 2
        c3, p3 = ci % 3, (ci - 1) % 3
        last_prefix = (not ext) and (ci == n_prefix - 1)
        need_swa_kv = ext or last_prefix
        kck, vck, qck = ("kc", cur), ("vc", c3), ("qc", cur)
        S.dma("Q", kc[cur][:], k_sc[r0:r0 + 128, :], writes=[kck])
        S.dma("Q", vc[c3][:], v_sc[r0:r0 + 128, :], writes=[vck])
        if ext:
            S.dma("Q", qc[cur][:], q_sc[r0 - e0:r0 - e0 + 128, :], writes=[qck])
        S.op("V", "tensor_scalar_min", gmin[:], kc[cur][:], 0.0,
             reads=[kck], writes=[("gmin",)])
        S.op("S", "activation", gtmp[:], kc[cur][:], AF.Abs,
             reads=[kck], writes=[("gtmp",)])
        S.op("S", "activation", gtmp[:], gtmp[:], AF.Exp, scale=-1.0,
             reads=[("gtmp",)], writes=[("gtmp",)])
        S.op("V", "tensor_scalar_add", gtmp[:], gtmp[:], 1.0,
             reads=[("gtmp",)], writes=[("gtmp",)])
        S.op("S", "activation", gtmp[:], gtmp[:], AF.Ln,
             reads=[("gtmp",)], writes=[("gtmp",)])
        S.op("V", "tensor_sub", gs[:], gmin[:], gtmp[:],
             reads=[("gmin",), ("gtmp",)], writes=[("gs",)])
        for h2 in range(2):
            for kh in range(4):
                kv = h2 * 4 + kh
                S.op("T", "matmul", PB[:, kh * 128:(kh + 1) * 128], cum[:],
                     gs[:, kv * 128:(kv + 1) * 128], start=True, stop=True,
                     reads=[("gs",), ("cst", "cum_lhsT")], writes=[("PB",)])
            S.op("S", "activation", eB[:, h2 * 512:(h2 + 1) * 512], PB[:], AF.Exp,
                 reads=[("PB",)], writes=[("eB", h2)])
            S.op("S", "activation", eBn[:, h2 * 512:(h2 + 1) * 512], PB[:],
                 AF.Exp, scale=-1.0,
                 reads=[("PB",)], writes=[("eBn", h2), ("PB",)])
        for kv in range(8):
            S.op("T", "matmul", PBC[:, kv:kv + 1],
                 gs[:, kv * 128:(kv + 1) * 128], ones16[:], start=True, stop=True,
                 reads=[("gs",), ("cst", "ones16_col")], writes=[("PBC",)])
        S.op("S", "activation", eBC[:], PBC[:, 0:8], AF.Exp,
             reads=[("PBC",)], writes=[("eBC",), ("PBC",)])
        for kv in range(8):
            nmk = colb[:, kv:kv + 1]
            sk = colb[:, 8 + kv:9 + kv]
            S.op("V", "reduce_max", nmk, kc[cur][:, kv * 128:(kv + 1) * 128],
                 axis=AX.X, negate=True,
                 reads=[kck], writes=[("col", kv)])
            S.op("S", "activation", eks[:, kv * 128:(kv + 1) * 128],
                 kc[cur][:, kv * 128:(kv + 1) * 128], AF.Exp, bias=nmk,
                 accum_out=sk,
                 reads=[kck], writes=[("eks", kv), ("colsk", kv)],
                 scalar_reads=[("col", kv)])
            S.op("V", "reciprocal", sk, sk,
                 reads=[("colsk", kv)], writes=[("colsk", kv)])
            S.op("V", "scalar_tensor_tensor", ktil[:, kv * 128:(kv + 1) * 128],
                 eks[:, kv * 128:(kv + 1) * 128], sk,
                 eBn[:, kv * 128:(kv + 1) * 128], ALU.mult, ALU.mult,
                 reads=[("eks", kv), ("eBn", kv // 4)], writes=[("ktil", kv)],
                 scalar_reads=[("colsk", kv)])
        for h2 in range(2):
            for kh in range(4):
                kv = h2 * 4 + kh
                S.op("T", "matmul", PB[:, kh * 128:(kh + 1) * 128],
                     ktil[:, kv * 128:(kv + 1) * 128],
                     vc[c3][:, kv * 128:(kv + 1) * 128], start=True, stop=True,
                     reads=[("ktil", kv), vck], writes=[("PB",)])
            for kh in range(4):
                kv = h2 * 4 + kh
                S.op("V", "tensor_add", Sst[prv][:, kv * 128:(kv + 1) * 128],
                     PB[:, kh * 128:(kh + 1) * 128],
                     Sst[cur][:, kv * 128:(kv + 1) * 128],
                     reads=[("PB",), ("Sst", cur, kv)],
                     writes=[("Sst", prv, kv)])
                S.op("V", "tensor_scalar_mul",
                     Sst[prv][:, kv * 128:(kv + 1) * 128],
                     Sst[prv][:, kv * 128:(kv + 1) * 128], eBC[:, kv:kv + 1],
                     reads=[("Sst", prv, kv)], writes=[("Sst", prv, kv)],
                     scalar_reads=[("eBC",)])
            S.op("V", "engine_nop", reads=[("PB",)], writes=[("PB",)])
        if ext:
            S.op("S", "copy", Sb[cur][:], Sst[cur][:],
                 reads=[("Sst", cur, kv) for kv in range(8)],
                 writes=[("Sb", cur)])
        if need_swa_kv:
            S.dma("Q", cosn[:], cos_dram[r0:r0 + 128, :], writes=[("cosn",)])
            S.dma("Q", sinn[:], sin_dram[r0:r0 + 128, :], writes=[("sinn",)])
            S.op("V", "tensor_copy", kbf[:], kc[cur][:],
                 reads=[kck], writes=[("kbf",)])
            rope_nat(krn, kbf, 8, ("krn",), [("kbf",)])
            for kv8 in range(8):
                tr(TRB[0][:, kv8 * 128:(kv8 + 1) * 128], ("TRB", 0, kv8),
                   krn[:, kv8 * 128:(kv8 + 1) * 128], [("krn",)])
            S.op("S", "copy", krT[c3][:], TRB[0][:],
                 reads=[("TRB", 0, kv8) for kv8 in range(8)],
                 writes=[("krT", c3)] + [("TRB", 0, kv8) for kv8 in range(8)])
        if not ext:
            continue
        rope_nat(qrn, qc[cur], 32, ("qrn",), [qck], scale=SC)
        for kg in range(4):
            tb = TRB[kg % 2]
            for k8 in range(8):
                kt = kg * 8 + k8
                tr(tb[:, k8 * 128:(k8 + 1) * 128], ("TRB", kg % 2, k8),
                   qrn[:, kt * 128:(kt + 1) * 128], [("qrn",)])
            S.op("V", "tensor_copy",
                 qrT[cur][:, kg * 1024:(kg + 1) * 1024], tb[:],
                 reads=[("TRB", kg % 2, k8) for k8 in range(8)],
                 writes=[("qrT", cur, kg)] + [("TRB", kg % 2, k8)
                                              for k8 in range(8)])
        q3d = qc[cur][:].rearrange("p (k j) -> p k j", k=32)
        S.op("V", "reduce_max", colq[:, 0:32].rearrange("p (k j) -> p k j", k=32),
             q3d, axis=AX.X, negate=True,
             reads=[qck], writes=[("colq", 0)])
        pstep = colq[:].ap[0][0]
        nm_b = bass.AP(colq, 0, [[pstep, 128], [1, 32], [0, 128]])
        S.op("V", "tensor_add", qs_all[:].rearrange("p (k j) -> p k j", k=32),
             q3d, nm_b,
             reads=[qck, ("colq", 0)], writes=[("qs_all",)])
        S.op("S", "activation", eqa[:], qs_all[:], AF.Exp,
             reads=[("qs_all",)], writes=[("eqa",)])
        S.op("V", "reduce_sum",
             colq[:, 32:64].rearrange("p (k j) -> p k j", k=32),
             eqa[:].rearrange("p (k j) -> p k j", k=32), axis=AX.X,
             reads=[("eqa",)], writes=[("colq", 1)])
        S.op("V", "reciprocal", colq[:, 32:64], colq[:, 32:64],
             reads=[("colq", 1)], writes=[("colq", 1)])
        for h in range(32):
            kv = h // 4
            S.op("V", "scalar_tensor_tensor", qt_all[:, h * 128:(h + 1) * 128],
                 eqa[:, h * 128:(h + 1) * 128], colq[:, 32 + h:33 + h],
                 eB[:, kv * 128:(kv + 1) * 128], ALU.mult, ALU.mult,
                 reads=[("eqa",), ("eB", kv // 4)],
                 writes=[("qt_all", h // 8)],
                 scalar_reads=[("colq", 1)])
        for kg in range(4):
            tb2 = TRB[kg % 2]
            for k8 in range(8):
                kt = kg * 8 + k8
                S.op("T", "transpose", tb2[:, k8 * 128:(k8 + 1) * 128],
                     qt_all[:, kt * 128:(kt + 1) * 128], identb[:],
                     reads=[("qt_all", kg), ("cst", "ident_bf")],
                     writes=[("TRB", kg % 2, k8)])
            S.op("V", "tensor_copy",
                 QT[cur][:].rearrange("p (k t) -> p k t", k=32)[
                     :, kg * 8:(kg + 1) * 8, :],
                 tb2[:].rearrange("p (k t) -> p k t", k=8),
                 reads=[("TRB", kg % 2, k8) for k8 in range(8)],
                 writes=[("QT", cur, kg)] + [("TRB", kg % 2, k8)
                                             for k8 in range(8)])
        for kv8 in range(8):
            tr(TRB[0][:, kv8 * 128:(kv8 + 1) * 128], ("TRB", 0, kv8),
               ktil[:, kv8 * 128:(kv8 + 1) * 128], [("ktil", kv8)])
        S.op("S", "copy", ktilT[cur][:], TRB[0][:],
             reads=[("TRB", 0, kv8) for kv8 in range(8)],
             writes=[("ktilT", cur)] + [("TRB", 0, kv8) for kv8 in range(8)])
        first_own = (r0 == e0 + 256)
        smask = smasks["first" if first_own else "gen"]
        smask_key = "swa_maskA" if first_own else "swa_maskB"
        for h in range(32):
            kv = h // 4
            hp = h % 2
            nms = colb[:, 18 + 4 * hp:19 + 4 * hp]
            rs = colb[:, 19 + 4 * hp:20 + 4 * hp]
            S.op("T", "matmul", PX[:, 0:128],
                 ktilT[cur][:, kv * 128:(kv + 1) * 128],
                 QT[cur][:, h * 128:(h + 1) * 128], start=True, stop=True,
                 reads=[("ktilT", cur), ("QT", cur, h // 8)],
                 writes=[("PX",)])
            S.op("V", "tensor_mul", ptb[hp][:], PX[:, 0:128], gmask[:],
                 reads=[("PX",), ("cst", "gla_maskT")],
                 writes=[("ptb", hp), ("PX",)])
            S.op("T", "matmul", PO[:, 0:128], ptb[hp][:],
                 vc[c3][:, kv * 128:(kv + 1) * 128], start=True, stop=False,
                 reads=[("ptb", hp), vck], writes=[("PO",)])
            S.op("T", "matmul", PO[:, 0:128],
                 QT[cur][:, h * 128:(h + 1) * 128],
                 Sb[cur][:, kv * 128:(kv + 1) * 128], start=False, stop=True,
                 reads=[("QT", cur, h // 8), ("Sb", cur)], writes=[("PO",)])
            S.op("S", "copy", ogla[hp][:], PO[:, 0:128],
                 reads=[("PO",)], writes=[("ogla", hp), ("PO",)])
            pw = PSW[hp]
            S.op("T", "matmul", pw[:, 0:128],
                 qrT[cur][:, h * 128:(h + 1) * 128],
                 krT[p3][:, kv * 128:(kv + 1) * 128], start=True, stop=True,
                 reads=[("qrT", cur, h // 8), ("krT", p3)],
                 writes=[("PSW", hp)])
            S.op("T", "matmul", pw[:, 128:256],
                 qrT[cur][:, h * 128:(h + 1) * 128],
                 krT[c3][:, kv * 128:(kv + 1) * 128], start=True, stop=True,
                 reads=[("qrT", cur, h // 8), ("krT", c3)],
                 writes=[("PSW", hp)])
            S.op("V", "reduce_max", nms, pw[:, 0:256], axis=AX.X, negate=True,
                 reads=[("PSW", hp)], writes=[("nms", hp)])
            S.op("S", "activation", esb[:], pw[:, 0:256], AF.Exp, bias=nms,
                 reads=[("PSW", hp)], writes=[("esb",), ("PSW", hp)],
                 scalar_reads=[("nms", hp)])
            S.op("V", "scalar_tensor_tensor", punb[:], esb[:], 1.0, smask[:],
                 ALU.mult, ALU.mult, accum_out=rs,
                 reads=[("esb",), ("cst", smask_key)],
                 writes=[("punb",), ("rs", hp)])
            S.op("V", "reciprocal", rs, rs,
                 reads=[("rs", hp)], writes=[("rs", hp)])
            tr(TRB[1][:, 256:384], ("TRB", 1, "pt1"), punb[:, 0:128],
               [("punb",)])
            tr(TRB[1][:, 384:512], ("TRB", 1, "pt2"), punb[:, 128:256],
               [("punb",)])
            S.op("S", "copy", PT1[:], TRB[1][:, 256:384],
                 reads=[("TRB", 1, "pt1")],
                 writes=[("PT1",), ("TRB", 1, "pt1")])
            S.op("S", "copy", PT2[:], TRB[1][:, 384:512],
                 reads=[("TRB", 1, "pt2")],
                 writes=[("PT2",), ("TRB", 1, "pt2")])
            S.op("T", "matmul", pw[:, 256:384], PT1[:],
                 vc[p3][:, kv * 128:(kv + 1) * 128], start=True, stop=False,
                 reads=[("PT1",), ("vc", p3)], writes=[("PSW", hp)])
            S.op("T", "matmul", pw[:, 256:384], PT2[:],
                 vc[c3][:, kv * 128:(kv + 1) * 128], start=False, stop=True,
                 reads=[("PT2",), vck], writes=[("PSW", hp)])
            S.op("V", "scalar_tensor_tensor",
                 o_[cur][:, h * 128:(h + 1) * 128],
                 pw[:, 256:384], rs, ogla[hp][:], ALU.mult, ALU.add,
                 reads=[("PSW", hp), ("ogla", hp)],
                 writes=[("o_", cur, h), ("PSW", hp)],
                 scalar_reads=[("rs", hp)])
        for kg in range(4):
            tb = TRB[kg % 2]
            for k8 in range(8):
                kt = kg * 8 + k8
                tr(tb[:, k8 * 128:(k8 + 1) * 128], ("TRB", kg % 2, k8),
                   o_[cur][:, kt * 128:(kt + 1) * 128], [("o_", cur, kt)])
            S.op("V", "tensor_copy",
                 ocat[cur][:, kg * 1024:(kg + 1) * 1024], tb[:],
                 reads=[("TRB", kg % 2, k8) for k8 in range(8)],
                 writes=[("ocat", cur, kg)] + [("TRB", kg % 2, k8)
                                               for k8 in range(8)])
        S.dma("Q", ocatT_sc.rearrange("(k p) t -> p k t", p=128)[
            :, :, r0 - e0:r0 - e0 + 128],
            ocat[cur][:].rearrange("p (k t) -> p k t", k=32),
            reads=[("ocat", cur, kg) for kg in range(4)],
            writes=[("ocatT_out", ci)])



def emit_a3(nc, S, stack, half, aps, cst):
    e0 = E0
    x_ap, wo_ap, ocatT_sc, y1_sc = aps["x"], aps["Wo"], aps["ocatT_sc"], aps["y1_sc"]
    sb = lambda nm, shape, dt: stack.enter_context(nc.sbuf_tensor(nm, shape, dt))
    ps = lambda nm, shape, dt: stack.enter_context(nc.psum_tensor(nm, shape, dt))
    ocT = sb("a3_ocT", [128, 32 * EXTN], BF16)
    wpan = [sb(f"a3_wpan{i}", [128, 32 * PANW], BF16) for i in range(2)]
    xs = [sb(f"a3_xs{i}", [128, PANW], F32) for i in range(2)]
    ev = [sb(f"a3_ev{i}", [128, PANW], F32) for i in range(2)]
    pmm = [ps(f"a3_pmm{i}", [128, 512], F32) for i in range(2)]

    S.dma("Q", ocT[:].rearrange("p (k t) -> p k t", k=32),
          ocatT_sc.rearrange("(k p) t -> p k t", p=128), writes=[("ocT",)])

    def ocsl(kt, ti):
        c = kt * EXTN + ti * 128
        return ocT[:, c:c + 128]

    for pi, c0 in enumerate(range(0, DM, PANW)):
        wb = pi % 2
        S.dma("P", wpan[wb][:].rearrange("p (k j) -> p k j", k=32),
              wo_ap[:, c0:c0 + PANW].rearrange("(k p) j -> p k j", p=128),
              writes=[("wpan", wb)])
        for ti in range(EXTN // 128):
            tb = ti % 2
            S.dma("Q", xs[tb][:], x_ap[e0 + ti * 128:e0 + (ti + 1) * 128,
                                       c0:c0 + PANW], writes=[("xs", tb)])
            for kt in range(32):
                S.op("T", "matmul", pmm[tb][:, 0:PANW], ocsl(kt, ti),
                     wpan[wb][:, kt * PANW:(kt + 1) * PANW],
                     start=(kt == 0), stop=(kt == 31),
                     reads=[("ocT",), ("wpan", wb)], writes=[("pmm", tb)])
            S.op("V", "scalar_tensor_tensor", ev[tb][:], pmm[tb][:, 0:PANW],
                 0.5, xs[tb][:], ALU.mult, ALU.add,
                 reads=[("pmm", tb), ("xs", tb)],
                 writes=[("ev", tb), ("pmm", tb)])
            S.dma("Q", y1_sc[ti * 128:(ti + 1) * 128, c0:c0 + PANW], ev[tb][:],
                  reads=[("ev", tb)], writes=[("y1", ti, c0)])



CST_NAMES = ["cum_lhsT", "ones_col", "gla_maskT", "swa_mask_gen", "swa_mask_first",
             "cos0", "sin0", "cos1", "sin1", "alpha_a", "alpha_b",
             "ident_bf", "ident_f32", "ones16_col"]


def host_constants_full():
    c = host_constants()
    c["ident_bf"] = np.eye(128, dtype=NPBF16)
    c["ones_bf"] = np.ones((128, 1), dtype=NPBF16)
    c["ident_f32"] = np.eye(128, dtype=np.float32)
    c["ones16_col"] = np.full((128, 1), 1.0 / 16.0, np.float32)
    pad = np.zeros((NV - N, 128), np.float32)
    for nm in ("cos0", "sin0", "cos1", "sin1"):
        c[nm] = np.ascontiguousarray(np.concatenate([c[nm], pad]))
    c["swa_maskA"] = c["swa_mask_gen"]
    c["swa_maskB"] = c["swa_mask_gen"]
    c["alphaF_a"] = c["alpha_a"]
    c["alphaF_b"] = c["alpha_b"]
    c["alphaF1m_a"] = c["alpha1m_a"]
    c["alphaF1m_b"] = c["alpha1m_b"]
    c["mlast_cols"] = np.concatenate(
        [np.ones((128, 2), np.float32), np.zeros((128, 2), np.float32)], 1)
    return c


def percore_constants(cv, half):
    o = {}
    if half == 1:
        o["swa_maskA"] = cv["swa_mask_gen"]
        o["mlast_cols"] = cv["mlast_cols"]
        for nm in ("alphaF_a", "alphaF_b"):
            o[nm] = cv["alpha_" + nm[-1]]
        for nm in ("alphaF1m_a", "alphaF1m_b"):
            o[nm] = cv["alpha1m_" + nm[-1]]
        for nm in ("cos0", "sin0", "cos1", "sin1"):
            o[nm] = cv[nm]
    else:
        o["swa_maskA"] = cv["swa_mask_first"]
        o["mlast_cols"] = np.ones((128, 4), np.float32)
        o["alphaF_a"] = np.ones((128, 1), np.float32)
        o["alphaF_b"] = np.ones((128, 1), np.float32)
        o["alphaF1m_a"] = np.zeros((128, 1), np.float32)
        o["alphaF1m_b"] = np.zeros((128, 1), np.float32)
        for nm in ("cos0", "sin0", "cos1", "sin1"):
            t = cv[nm]
            sh = np.zeros_like(t)
            sh[2048:] = t[:NV - 2048]
            o[nm] = np.ascontiguousarray(sh)
    return o


def make_xeff(xb, half):
    out = np.zeros((NV, DM), np.float32)
    if half == 1:
        out[:N] = xb
    else:
        out[2048:2048 + 2304] = xb[:2304]
    return out


def _np_to_dt(a):
    if a.dtype == np.float32:
        return F32
    if a.dtype == NPBF16:
        return BF16
    if a.dtype == np.int32:
        return I32
    raise ValueError(a.dtype)


def declare_constants(nc, cvals, names):
    return {nm: nc.dram_tensor("cst_" + nm, list(cvals[nm].shape),
                               _np_to_dt(cvals[nm]), kind="ExternalInput").ap()
            for nm in names}


def build_a1_test(half, serialize=False, strict_engines=()):
    nc = bass.Bass("TRN2", target_bir_lowering=False)
    NK = 4096 if half else EXTN
    cvals = host_constants_full()
    cin = declare_constants(nc, cvals, ["ident_bf"])
    aps = {
        "x": nc.dram_tensor("x", [N, DM], F32, kind="ExternalInput").ap(),
        "Wq": nc.dram_tensor("Wq", [DM, DM], F32, kind="ExternalInput").ap(),
        "Wk": nc.dram_tensor("Wk", [DM, 1024], F32, kind="ExternalInput").ap(),
        "Wv": nc.dram_tensor("Wv", [DM, 1024], F32, kind="ExternalInput").ap(),
        "k_sc": nc.dram_tensor("k_sc", [NK, 1024], F32, kind="ExternalOutput").ap(),
        "v_sc": nc.dram_tensor("v_sc", [NK, 1024], BF16, kind="ExternalOutput").ap(),
        "q_sc": nc.dram_tensor("q_sc", [EXTN, DM], BF16, kind="ExternalOutput").ap(),
        "dbg_ln": nc.dram_tensor("dbg_ln", [128, DM], BF16, kind="ExternalOutput").ap(),
    }
    with contextlib.ExitStack() as pstack:
        pool = SemPool(nc, pstack)
        with contextlib.ExitStack() as stack:
            S = Sched(nc, "a1", pool, serialize=serialize,
                      strict_engines=strict_engines)
            cst = load_constants(nc, stack, S, cin, ["ident_bf"])
            emit_a1(nc, S, stack, half, aps, cst)
            S.finalize(stack)
    return nc, cvals



def build_stage_a(half, serialize=False, strict_engines=("V", "S", "P"), upto=3):
    nc = bass.Bass("TRN2", target_bir_lowering=False)
    NK = NV
    cvals = host_constants_full()
    cnames = ["ident_bf", "cum_lhsT", "ones16_col", "gla_maskT",
              "swa_maskA", "swa_maskB", "cos0", "sin0"]
    cin = declare_constants(nc, cvals, cnames)
    aps = {
        "x": nc.dram_tensor("x", [NV, DM], F32, kind="ExternalInput").ap(),
        "Wq": nc.dram_tensor("Wq", [DM, DM], F32, kind="ExternalInput").ap(),
        "Wk": nc.dram_tensor("Wk", [DM, 1024], F32, kind="ExternalInput").ap(),
        "Wv": nc.dram_tensor("Wv", [DM, 1024], F32, kind="ExternalInput").ap(),
        "Wo": nc.dram_tensor("Wo", [DM, DM], F32, kind="ExternalInput").ap(),
        "k_sc": nc.dram_tensor("k_sc", [NK, 1024], F32, kind="Internal").ap(),
        "v_sc": nc.dram_tensor("v_sc", [NK, 1024], BF16, kind="Internal").ap(),
        "q_sc": nc.dram_tensor("q_sc", [EXTN, DM], BF16, kind="Internal").ap(),
        "ocatT_sc": nc.dram_tensor("ocatT_sc", [DM, EXTN], BF16,
                                   kind="Internal").ap(),
        "y1_sc": nc.dram_tensor("y1_sc", [EXTN, DM], F32,
                                kind="ExternalOutput").ap(),
    }
    with contextlib.ExitStack() as pstack:
        pool = SemPool(nc, pstack)
        with contextlib.ExitStack() as st:
            S = Sched(nc, "a1", pool, serialize=serialize,
                      strict_engines=strict_engines)
            cst = load_constants(nc, st, S, cin, ["ident_bf"])
            emit_a1(nc, S, st, half, aps, cst)
            S.finalize(st)
        if upto >= 2:
            with contextlib.ExitStack() as st:
                S = Sched(nc, "a2", pool, serialize=serialize,
                          strict_engines=strict_engines)
                cst = load_constants(nc, st, S, cin,
                                     ["ident_bf", "cum_lhsT", "ones16_col",
                                      "gla_maskT", "swa_maskA",
                                      "swa_maskB"])
                cst["cos0_dram"] = cin["cos0"]
                cst["sin0_dram"] = cin["sin0"]
                emit_a2(nc, S, st, half, aps, cst)
                S.finalize(st)
        if upto >= 3:
            with contextlib.ExitStack() as st:
                S = Sched(nc, "a3", pool, serialize=serialize,
                          strict_engines=strict_engines)
                emit_a3(nc, S, st, half, aps, cst)
                S.finalize(st)
    return nc, cvals, cnames



def emit_b1(nc, S, stack, half, aps, cst):
    y1_sc, w1T, hT_sc = aps["y1_sc"], aps["w1T_bits"], aps["hT_sc"]
    s0b, s2c = aps["s0b"], aps["s2cols"]
    ident_bf = cst["ident_bf"]
    ones_bf = cst["ones_bf"]
    GT = 10
    sb = lambda nm, shape, dt: stack.enter_context(nc.sbuf_tensor(nm, shape, dt))
    ps = lambda nm, shape, dt: stack.enter_context(nc.psum_tensor(nm, shape, dt))
    lnT = sb("b1_lnT", [128, GT * 32 * 128], BF16)
    xa = [sb(f"b1_xa{i}", [128, DM], F32) for i in range(2)]
    lnb = [sb(f"b1_lnb{i}", [128, DM], BF16) for i in range(2)]
    lns = [sb(f"b1_lns{i}", [128, DM], BF16) for i in range(2)]
    s0sb = sb("b1_s0", [128, DM], BF16)
    s2sb = sb("b1_s2", [128, 32], F32)
    rt = sb("b1_rt", [128, EXTN // 128], F32)
    wpan = [sb(f"b1_wpan{i}", [128, 32 * PANW], BF16) for i in range(2)]
    hnat = [sb(f"b1_hn{i}", [128, PANW], BF16) for i in range(2)]
    hst = [sb(f"b1_hst{i}", [128, PANW], BF16) for i in range(2)]
    colbuf = sb("b1_col", [128, 8], F32)
    ptr = [ps(f"b1_ptr{i}", [128, 1024], BF16) for i in range(2)]
    pmm = [ps(f"b1_pmm{i}", [128, 512], F32) for i in range(2)]
    prt = ps("b1_prt", [128, 512], F32)
    ptt = ps("b1_ptt", [128, 1024], BF16)

    S.dma("Q", s0sb[:], s0b, writes=[("s0",)])
    S.dma("Q", s2sb[:], s2c, writes=[("s2",)])

    def lnT_sl(kt, ji):
        c = kt * (GT * 128) + ji * 128
        return lnT[:, c:c + 128]

    groups = [range(t, min(t + GT, EXTN // 128)) for t in range(0, EXTN // 128, GT)]
    pan_i = 0
    for gi, group in enumerate(groups):
        for j, ti in enumerate(group):
            xb, lb, ls = xa[j % 2], lnb[j % 2], lns[j % 2]
            xkey, lnkey, lskey = ("xa", j % 2), ("lnb", j % 2), ("lns", j % 2)
            ckey = ("col", j % 2)
            cols3 = (colbuf[:, (j % 2) * 3:(j % 2) * 3 + 1],
                     colbuf[:, (j % 2) * 3 + 1:(j % 2) * 3 + 2],
                     colbuf[:, (j % 2) * 3 + 2:(j % 2) * 3 + 3])
            S.dma("Q", xb[:], y1_sc[ti * 128:(ti + 1) * 128, :], writes=[xkey])
            emit_ln(S, xb[:], lb[:], cols3, xkey, lnkey, ckey)
            S.op("V", "tensor_mul", ls[:], lb[:], s0sb[:],
                 reads=[lnkey, ("s0",)], writes=[lskey])
            ji = ti - group[0]
            for kg in range(4):
                pt = ptr[kg % 2]
                for k8 in range(8):
                    kt = kg * 8 + k8
                    S.op("T", "transpose", pt[:, k8 * 128:(k8 + 1) * 128],
                         ls[:, kt * 128:(kt + 1) * 128], ident_bf[:],
                         reads=[lskey, ("cst", "ident_bf")],
                         writes=[("ptr", kg % 2, k8)])
                dst = lnT[:].rearrange("p (k t) -> p k t", k=32)[
                    :, kg * 8:(kg + 1) * 8, ji * 128:(ji + 1) * 128]
                S.op("V", "tensor_copy", dst,
                     pt[:].rearrange("p (k t) -> p k t", k=8),
                     reads=[("ptr", kg % 2, k8) for k8 in range(8)],
                     writes=[("lnT", ji, kg)] + [("ptr", kg % 2, k8)
                                                 for k8 in range(8)])
            for kt in range(32):
                S.op("T", "matmul", prt[:, 0:1], lnT_sl(kt, ji), ones_bf[:],
                     start=(kt == 0), stop=(kt == 31),
                     reads=[("lnT", ji, kt // 8), ("cst", "ones_bf")],
                     writes=[("prt",)])
            S.op("S", "copy", rt[:, ti:ti + 1], prt[:, 0:1],
                 reads=[("prt",)], writes=[("rt", ti), ("prt",)])
        for c0 in range(0, DM, PANW):
            wb = pan_i % 2
            S.dma("P", wpan[wb][:].rearrange("p (k j) -> p k j", k=32),
                  w1T[:, c0:c0 + PANW].rearrange("(k p) j -> p k j", p=128),
                  writes=[("wpan", wb)])
            for ri, ti in enumerate(group):
                ji = ti - group[0]
                pb = pmm[ri % 2]
                for kt in range(32):
                    S.op("T", "matmul", pb[:, 0:PANW], lnT_sl(kt, ji),
                         wpan[wb][:, kt * PANW:(kt + 1) * PANW],
                         start=(kt == 0), stop=(kt == 31),
                         reads=[("lnT", ji, kt // 8), ("wpan", wb)],
                         writes=[("pmm", ri % 2)])
                S.op("V", "tensor_scalar", hnat[ri % 2][:], pb[:, 0:PANW],
                     2.0, rt[:, ti:ti + 1], ALU.mult, ALU.subtract,
                     reads=[("pmm", ri % 2)],
                     writes=[("hnat", ri % 2), ("pmm", ri % 2)],
                     scalar_reads=[("rt", ti)])
                for z in range(2):
                    S.op("T", "transpose",
                         ptt[:, (2 * (ri % 2) + z) * 128:
                             (2 * (ri % 2) + z + 1) * 128],
                         hnat[ri % 2][:, z * 128:(z + 1) * 128], ident_bf[:],
                         reads=[("hnat", ri % 2), ("cst", "ident_bf")],
                         writes=[("ptt", ri % 2, z)])
                for z in range(2):
                    kt = c0 // 128 + z
                    S.op("V", "tensor_scalar_mul",
                         hst[ri % 2][:, z * 128:(z + 1) * 128],
                         ptt[:, (2 * (ri % 2) + z) * 128:
                             (2 * (ri % 2) + z + 1) * 128],
                         s2sb[:, kt:kt + 1],
                         reads=[("ptt", ri % 2, z)],
                         writes=[("hst", ri % 2), ("ptt", ri % 2, z)],
                         scalar_reads=[("s2",)])
                S.dma("Q", hT_sc[c0:c0 + PANW, ti * 128:(ti + 1) * 128]
                      .rearrange("(z p) t -> p z t", p=128),
                      hst[ri % 2][:].rearrange("p (z t) -> p z t", z=2),
                      reads=[("hst", ri % 2)], writes=[("hT", ti, c0)])
            pan_i += 1


def emit_b2(nc, S, stack, half, aps, cst):
    y1_sc, w3T, hT_sc, y2_sc = (aps["y1_sc"], aps["w3T_bits"], aps["hT_sc"],
                                aps["y2_sc"])
    s4b, biasb = aps["s4b"], aps["biasb"]
    ones_bf = cst["ones_bf"]
    GT = 10
    NT = EXTN // 128
    sb = lambda nm, shape, dt: stack.enter_context(nc.sbuf_tensor(nm, shape, dt))
    ps = lambda nm, shape, dt: stack.enter_context(nc.psum_tensor(nm, shape, dt))
    hT = sb("b2_hT", [128, 32 * GT * 128], BF16)
    wpan = [sb(f"b2_wpan{i}", [128, 32 * PANW], BF16) for i in range(2)]
    s4sb = sb("b2_s4", [128, DM], BF16)
    bsb = sb("b2_bias", [128, DM], BF16)
    rt = sb("b2_rt", [128, NT], F32)
    ys = [sb(f"b2_ys{i}", [128, PANW], F32) for i in range(2)]
    yb = [sb(f"b2_yb{i}", [128, PANW], F32) for i in range(2)]
    v1 = [sb(f"b2_v1{i}", [128, PANW], F32) for i in range(2)]
    ev = [sb(f"b2_ev{i}", [128, PANW], F32) for i in range(2)]
    pmm = [ps(f"b2_pmm{i}", [128, 512], F32) for i in range(2)]
    prt = ps("b2_prt", [128, 512], F32)

    S.dma("Q", s4sb[:], s4b, writes=[("s4",)])
    S.dma("Q", bsb[:], biasb, writes=[("bias",)])

    def hT_sl(kt, ji):
        c = kt * (GT * 128) + ji * 128
        return hT[:, c:c + 128]

    groups = [range(t, min(t + GT, NT)) for t in range(0, NT, GT)]
    pan_i = 0
    for gi, group in enumerate(groups):
        glen = len(group)
        t0 = group[0]
        S.dma("Q", hT[:, 0:32 * glen * 128].rearrange(
            "p (k t) -> p k t", k=32),
            hT_sc.rearrange("(k p) t -> p k t", p=128)[
                :, :, t0 * 128:(t0 + glen) * 128],
            writes=[("hT", gi)])
        for ji, ti in enumerate(group):
            for kt in range(32):
                S.op("T", "matmul", prt[:, 0:1], hT_sl(kt, ji), ones_bf[:],
                     start=(kt == 0), stop=(kt == 31),
                     reads=[("hT", gi), ("cst", "ones_bf")], writes=[("prt",)])
            S.op("S", "copy", rt[:, ti:ti + 1], prt[:, 0:1],
                 reads=[("prt",)], writes=[("rt", ti), ("prt",)])
        for c0 in range(0, DM, PANW):
            wb = pan_i % 2
            S.dma("P", wpan[wb][:].rearrange("p (k j) -> p k j", k=32),
                  w3T[:, c0:c0 + PANW].rearrange("(k p) j -> p k j", p=128),
                  writes=[("wpan", wb)])
            for ji, ti in enumerate(group):
                tb = ji % 2
                S.dma("Q", ys[tb][:], y1_sc[ti * 128:(ti + 1) * 128,
                                            c0:c0 + PANW], writes=[("ys", tb)])
                S.op("V", "tensor_add", yb[tb][:], ys[tb][:],
                     bsb[:, c0:c0 + PANW],
                     reads=[("ys", tb), ("bias",)], writes=[("yb", tb)])
                pb = pmm[tb]
                for kt in range(32):
                    S.op("T", "matmul", pb[:, 0:PANW], hT_sl(kt, ji),
                         wpan[wb][:, kt * PANW:(kt + 1) * PANW],
                         start=(kt == 0), stop=(kt == 31),
                         reads=[("hT", gi), ("wpan", wb)], writes=[("pmm", tb)])
                S.op("V", "tensor_scalar", v1[tb][:], pb[:, 0:PANW],
                     2.0, rt[:, ti:ti + 1], ALU.mult, ALU.subtract,
                     reads=[("pmm", tb)], writes=[("v1", tb), ("pmm", tb)],
                     scalar_reads=[("rt", ti)])
                S.op("V", "tensor_mul", v1[tb][:], v1[tb][:],
                     s4sb[:, c0:c0 + PANW],
                     reads=[("v1", tb), ("s4",)], writes=[("v1", tb)])
                S.op("V", "tensor_add", ev[tb][:], v1[tb][:], yb[tb][:],
                     reads=[("v1", tb), ("yb", tb)], writes=[("ev", tb)])
                S.dma("Q", y2_sc[ti * 128:(ti + 1) * 128, c0:c0 + PANW],
                      ev[tb][:], reads=[("ev", tb)], writes=[("y2", ti, c0)])
            pan_i += 1


def build_stage_b(half, serialize=False, strict_engines=("V", "S", "P")):
    nc = bass.Bass("TRN2", target_bir_lowering=False)
    cvals = host_constants_full()
    cnames = ["ident_bf", "ones_bf"]
    cin = declare_constants(nc, cvals, cnames)
    aps = {
        "y1_sc": nc.dram_tensor("y1_sc", [EXTN, DM], F32, kind="ExternalInput").ap(),
        "w1T_bits": nc.dram_tensor("w1T_bits", [DM, DM], I32,
                                   kind="ExternalInput").ap(),
        "w3T_bits": nc.dram_tensor("w3T_bits", [DM, DM], I32,
                                   kind="ExternalInput").ap(),
        "s0b": nc.dram_tensor("s0b", [128, DM], BF16, kind="ExternalInput").ap(),
        "s2cols": nc.dram_tensor("s2cols", [128, 32], F32,
                                 kind="ExternalInput").ap(),
        "s4b": nc.dram_tensor("s4b", [128, DM], BF16, kind="ExternalInput").ap(),
        "biasb": nc.dram_tensor("biasb", [128, DM], BF16,
                                kind="ExternalInput").ap(),
        "hT_sc": nc.dram_tensor("hT_sc", [DM, EXTN], BF16, kind="Internal").ap(),
        "y2_sc": nc.dram_tensor("y2_sc", [EXTN, DM], F32,
                                kind="ExternalOutput").ap(),
    }
    with contextlib.ExitStack() as pstack:
        pool = SemPool(nc, pstack)
        with contextlib.ExitStack() as st:
            S = Sched(nc, "b1", pool, serialize=serialize,
                      strict_engines=strict_engines)
            cst = load_constants(nc, st, S, cin, cnames)
            emit_b1(nc, S, st, half, aps, cst)
            S.finalize(st)
        with contextlib.ExitStack() as st:
            S = Sched(nc, "b2", pool, serialize=serialize,
                      strict_engines=strict_engines)
            cst = load_constants(nc, st, S, cin, cnames)
            emit_b2(nc, S, st, half, aps, cst)
            S.finalize(st)
    return nc, cvals, cnames


def host_b_inputs(inputs):
    w1T = np.ascontiguousarray(np.asarray(inputs["L0_ffn_w1_bits"]).T)
    w3T = np.ascontiguousarray(np.asarray(inputs["L0_ffn_w3_bits"]).T)
    s0 = np.asarray(inputs["L0_ffn_scaling0"], np.float32)
    s2 = np.asarray(inputs["L0_ffn_scaling2"], np.float32)
    s4 = np.asarray(inputs["L0_ffn_scaling4"], np.float32)
    bias = np.asarray(inputs["L0_ffn_bias"], np.float32)
    return {
        "w1T_bits": w1T, "w3T_bits": w3T,
        "s0b": np.ascontiguousarray(np.broadcast_to(s0.astype(NPBF16), (128, DM))),
        "s2cols": np.ascontiguousarray(s2.reshape(32, 128).T),
        "s4b": np.ascontiguousarray(np.broadcast_to(s4.astype(NPBF16), (128, DM))),
        "biasb": np.ascontiguousarray(np.broadcast_to(bias.astype(NPBF16),
                                                      (128, DM))),
    }




def c1_rope_nat(S, cosn, sinn, ropet, dst, src, nrep, dkey, skeys, scale=None,
                ropet2=None):
    def d3(t, c0):
        return t[:, 0:nrep * 128].rearrange(
            "p (k j) -> p k j", k=nrep)[:, :, c0:c0 + 64]
    pstep = cosn[:].ap[0][0]
    cb_lo = bass.AP(cosn, 0, [[pstep, 128], [0, nrep], [1, 64]])
    cb_hi = bass.AP(cosn, 64, [[pstep, 128], [0, nrep], [1, 64]])
    sn_lo = bass.AP(sinn, 0, [[pstep, 128], [0, nrep], [1, 64]])
    sn_hi = bass.AP(sinn, 64, [[pstep, 128], [0, nrep], [1, 64]])
    t2 = ropet2 if ropet2 is not None else ropet
    S.op("V", "tensor_mul", d3(ropet, 0), d3(src, 0), cb_lo,
         reads=skeys + [("cosn",)], writes=[("ropet", 0)])
    S.op("V", "tensor_mul", d3(ropet, 64), d3(src, 64), cb_hi,
         reads=skeys + [("cosn",)], writes=[("ropet", 64)])
    S.op("V", "tensor_mul", d3(t2, 0), d3(src, 64), sn_lo,
         reads=skeys + [("sinn",)], writes=[("ropet2", 0)])
    S.op("V", "tensor_mul", d3(t2, 64), d3(src, 0), sn_hi,
         reads=skeys + [("sinn",)], writes=[("ropet2", 64)])
    S.op("V", "tensor_sub", d3(dst, 0), d3(ropet, 0), d3(t2, 0),
         reads=[("ropet", 0), ("ropet2", 0)], writes=[dkey])
    S.op("V", "tensor_add", d3(dst, 64), d3(ropet, 64), d3(t2, 64),
         reads=[("ropet", 64), ("ropet2", 64)], writes=[dkey])
    if scale is not None:
        S.op("V", "tensor_scalar_mul", dst[:, 0:nrep * 128],
             dst[:, 0:nrep * 128], scale, reads=[dkey], writes=[dkey])


def emit_c1(nc, S, stack, half, aps, cst, kvbufs):
    e0 = E0
    own_off = OWN0
    y2_sc, wqT, wkvT, qn_sc = (aps["y2_sc"], aps["l1wqT"], aps["l1wkvT"],
                               aps["qn_sc"])
    cos_dram, sin_dram = aps["cst_cos1"], aps["cst_sin1"]
    ident_bf = cst["ident_bf"]
    KrT, Vnat = kvbufs
    GT = 10
    sb = lambda nm, shape, dt: stack.enter_context(nc.sbuf_tensor(nm, shape, dt))
    ps = lambda nm, shape, dt: stack.enter_context(nc.psum_tensor(nm, shape, dt))
    lnT = sb("c1_lnT", [128, GT * 32 * 128], BF16)
    xa = [sb(f"c1_xa{i}", [128, DM], F32) for i in range(2)]
    lnb = [sb(f"c1_lnb{i}", [128, DM], BF16) for i in range(2)]
    wqpan = [sb(f"c1_wq{i}", [128, 32 * PANW], BF16) for i in range(2)]
    wkvpan = sb("c1_wkv", [128, 32 * PANW], BF16)
    kvn = [sb(f"c1_kvn{i}", [128, PANW], BF16) for i in range(2)]
    krn = sb("c1_krn", [128, 128], BF16)
    qev = [sb(f"c1_qev{i}", [128, PANW], BF16) for i in range(2)]
    cosn = sb("c1_cosn", [128, 128], F32)
    sinn = sb("c1_sinn", [128, 128], F32)
    ropet = sb("c1_ropet", [128, 128], BF16)
    ropet2 = sb("c1_ropet2", [128, 128], BF16)
    colbuf = sb("c1_col", [128, 8], F32)
    ptr = [ps(f"c1_ptr{i}", [128, 1024], BF16) for i in range(2)]
    pmm = [ps(f"c1_pmm{i}", [128, 512], F32) for i in range(2)]
    pkv = ps("c1_pkv", [128, 512], F32)

    def lnT_sl(kt, ji):
        c = kt * (GT * 128) + ji * 128
        return lnT[:, c:c + 128]

    own_tiles = set(range(own_off // 128, own_off // 128 + 16))
    S.dma("P", wkvpan[:].rearrange("p (k j) -> p k j", k=32),
          wkvT[:, 0:PANW].rearrange("(k p) j -> p k j", p=128),
          writes=[("wkv",)])
    NT = EXTN // 128
    groups = [range(t, min(t + GT, NT)) for t in range(0, NT, GT)]
    pan_i = 0
    for gi, group in enumerate(groups):
        for j, ti in enumerate(group):
            xb, lb = xa[j % 2], lnb[j % 2]
            xkey, lnkey, ckey = ("xa", j % 2), ("lnb", j % 2), ("col", j % 2)
            cols3 = (colbuf[:, (j % 2) * 3:(j % 2) * 3 + 1],
                     colbuf[:, (j % 2) * 3 + 1:(j % 2) * 3 + 2],
                     colbuf[:, (j % 2) * 3 + 2:(j % 2) * 3 + 3])
            S.dma("Q", xb[:], y2_sc[ti * 128:(ti + 1) * 128, :], writes=[xkey])
            emit_ln(S, xb[:], lb[:], cols3, xkey, lnkey, ckey)
            ji = ti - group[0]
            for kg in range(4):
                pt = ptr[kg % 2]
                for k8 in range(8):
                    kt = kg * 8 + k8
                    S.op("T", "transpose", pt[:, k8 * 128:(k8 + 1) * 128],
                         lb[:, kt * 128:(kt + 1) * 128], ident_bf[:],
                         reads=[lnkey, ("cst", "ident_bf")],
                         writes=[("ptr", kg % 2, k8)])
                dst = lnT[:].rearrange("p (k t) -> p k t", k=32)[
                    :, kg * 8:(kg + 1) * 8, ji * 128:(ji + 1) * 128]
                S.op("V", "tensor_copy", dst,
                     pt[:].rearrange("p (k t) -> p k t", k=8),
                     reads=[("ptr", kg % 2, k8) for k8 in range(8)],
                     writes=[("lnT", ji, kg)] + [("ptr", kg % 2, k8)
                                                 for k8 in range(8)])
            for kt in range(32):
                S.op("T", "matmul", pkv[:, 0:PANW], lnT_sl(kt, ji),
                     wkvpan[:, kt * PANW:(kt + 1) * PANW],
                     start=(kt == 0), stop=(kt == 31),
                     reads=[("lnT", ji, kt // 8), ("wkv",)], writes=[("pkv",)])
            kb = kvn[ti % 2]
            S.op("V", "tensor_copy", kb[:], pkv[:, 0:PANW],
                 reads=[("pkv",)], writes=[("kvn", ti % 2), ("pkv",)])
            S.op("V", "tensor_copy", Vnat[:, ti * 128:(ti + 1) * 128],
                 kb[:, 128:256], reads=[("kvn", ti % 2)], writes=[("Vnat", ti)])
            r0 = e0 + ti * 128
            S.dma("Q", cosn[:], cos_dram[r0:r0 + 128, :], writes=[("cosn",)])
            S.dma("Q", sinn[:], sin_dram[r0:r0 + 128, :], writes=[("sinn",)])
            c1_rope_nat(S, cosn, sinn, ropet, krn, kb, 1,
                        ("krn",), [("kvn", ti % 2)], ropet2=ropet2)
            S.op("T", "transpose", ptr[0][:, 0:128], krn[:], ident_bf[:],
                 reads=[("krn",), ("cst", "ident_bf")], writes=[("ptr", 0, 0)])
            S.op("S", "copy", KrT[:, ti * 128:(ti + 1) * 128], ptr[0][:, 0:128],
                 reads=[("ptr", 0, 0)], writes=[("KrT", ti), ("ptr", 0, 0)])
        g_own = [ti for ti in group if ti in own_tiles]
        for c0 in range(0, DM, PANW):
            wb = pan_i % 2
            S.dma("P", wqpan[wb][:].rearrange("p (k j) -> p k j", k=32),
                  wqT[:, c0:c0 + PANW].rearrange("(k p) j -> p k j", p=128),
                  writes=[("wq", wb)])
            for ri, ti in enumerate(g_own):
                ji = ti - group[0]
                pb = pmm[ri % 2]
                for kt in range(32):
                    S.op("T", "matmul", pb[:, 0:PANW], lnT_sl(kt, ji),
                         wqpan[wb][:, kt * PANW:(kt + 1) * PANW],
                         start=(kt == 0), stop=(kt == 31),
                         reads=[("lnT", ji, kt // 8), ("wq", wb)],
                         writes=[("pmm", ri % 2)])
                ev = qev[ri % 2]
                S.op("V", "tensor_copy", ev[:], pb[:, 0:PANW],
                     reads=[("pmm", ri % 2)],
                     writes=[("qev", ri % 2), ("pmm", ri % 2)])
                q0 = ti * 128 - own_off
                S.dma("Q", qn_sc[q0:q0 + 128, c0:c0 + PANW], ev[:],
                      reads=[("qev", ri % 2)], writes=[("qn", ti, c0)])
            pan_i += 1


def emit_c1b(nc, S, stack, half, aps, cst):
    e0 = E0
    own_off = OWN0
    qn_sc, qrT_sc = aps["qn_sc"], aps["qrT_sc"]
    cos_dram, sin_dram = aps["cst_cos1"], aps["cst_sin1"]
    ident_bf = cst["ident_bf"]
    SC = 1.0 / float(np.sqrt(128.0))
    sb = lambda nm, shape, dt: stack.enter_context(nc.sbuf_tensor(nm, shape, dt))
    ps = lambda nm, shape, dt: stack.enter_context(nc.psum_tensor(nm, shape, dt))
    qa = [sb(f"cb_qa{i}", [128, DM], BF16) for i in range(2)]
    qrn2 = [sb(f"cb_qrn{i}", [128, DM], BF16) for i in range(2)]
    qst = [sb(f"cb_qst{i}", [128, DM], BF16) for i in range(2)]
    cosn = sb("cb_cosn", [128, 128], F32)
    sinn = sb("cb_sinn", [128, 128], F32)
    ropet = sb("cb_ropet", [128, DM], BF16)
    ropet2 = sb("cb_ropet2", [128, DM], BF16)
    ptr = [ps(f"cb_ptr{i}", [128, 1024], BF16) for i in range(2)]

    for ti in range(16):
        tb = ti % 2
        qrn = qrn2[tb]
        S.dma("Q", qa[tb][:], qn_sc[ti * 128:(ti + 1) * 128, :],
              writes=[("qa", tb)])
        r0 = e0 + own_off + ti * 128
        S.dma("Q", cosn[:], cos_dram[r0:r0 + 128, :], writes=[("cosn",)])
        S.dma("Q", sinn[:], sin_dram[r0:r0 + 128, :], writes=[("sinn",)])
        c1_rope_nat(S, cosn, sinn, ropet, qrn, qa[tb], 32, ("qrn", tb),
                    [("qa", tb)], scale=SC, ropet2=ropet2)
        if True:
            for kg in range(4):
                pt = ptr[kg % 2]
                for k8 in range(8):
                    kt = kg * 8 + k8
                    S.op("T", "transpose", pt[:, k8 * 128:(k8 + 1) * 128],
                         qrn[:, kt * 128:(kt + 1) * 128], ident_bf[:],
                         reads=[("qrn", tb), ("cst", "ident_bf")],
                         writes=[("ptr", kg % 2, k8)])
                S.op("V", "tensor_copy", qst[tb][:, kg * 1024:(kg + 1) * 1024],
                     pt[:],
                     reads=[("ptr", kg % 2, k8) for k8 in range(8)],
                     writes=[("qst", tb, kg)] + [("ptr", kg % 2, k8)
                                                 for k8 in range(8)])
        for kg in range(4):
            S.dma("Q", qrT_sc.rearrange("(k p) t -> p k t", p=128)[
                :, kg * 8:(kg + 1) * 8, ti * 128:(ti + 1) * 128],
                qst[tb][:].rearrange("p (k t) -> p k t", k=32)[
                    :, kg * 8:(kg + 1) * 8, :],
                reads=[("qst", tb, kg)], writes=[("qrT_out", ti, kg)])


def emit_c2(nc, S, stack, half, aps, cst, kvbufs, ocatT2):
    own_off = OWN0
    own_lo, own_hi = own_off, own_off + 2048
    qrT_sc = aps["qrT_sc"]
    ident_bf, ident_f = cst["ident_bf"], cst["ident_f32"]
    ones_bf = cst["ones_bf"]
    al = {0: cst["alpha_a"], 1: cst["alpha_b"]}
    alm = {0: cst["alpha1m_a"], 1: cst["alpha1m_b"]}
    alF = {0: cst["alphaF_a"], 1: cst["alphaF_b"]}
    alFm = {0: cst["alphaF1m_a"], 1: cst["alphaF1m_b"]}
    mlc = cst["mlast_cols"]
    KrT, Vnat = kvbufs
    sb = lambda nm, shape, dt: stack.enter_context(nc.sbuf_tensor(nm, shape, dt))
    ps = lambda nm, shape, dt: stack.enter_context(nc.psum_tensor(nm, shape, dt))
    qstream = [sb(f"c2_qs{i}", [128, 512], BF16) for i in range(2)]
    sigb = [sb(f"c2_sig{i}", [128, 4 * 512], BF16) for i in range(2)]
    oTb = [sb(f"c2_oT{i}", [128, 512], BF16) for i in range(2)]
    ocur = [sb(f"c2_oc{i}", [128, 512], BF16) for i in range(2)]
    tmpb = [sb(f"c2_tmp{i}", [128, 128], BF16) for i in range(2)]
    tails = sb("c2_tails", [128, 32 * 256], BF16)
    ocatn = [sb(f"c2_on{i}", [128, DM], BF16) for i in range(2)]
    ocst = sb("c2_ocst", [128, DM], BF16)
    rsrow = sb("c2_rs", [1, 512], F32)
    rq = [sb(f"c2_rq{i}", [128, 4], F32) for i in range(2)]
    PSC = [ps(f"c2_psc{i}", [128, 512], F32) for i in range(2)]
    PRS = ps("c2_prs", [128, 512], F32)
    POT = ps("c2_pot", [128, 512], F32)
    TRC = [ps(f"c2_trc{i}", [128, 1024], BF16) for i in range(2)]

    Vm = stack.enter_context(nc.sbuf_tensor("c2_Vm", [128, 512], BF16))
    mlb = stack.enter_context(nc.sbuf_tensor("c2_mlb", [128, 4], BF16))
    S.op("V", "tensor_copy", mlb[:], mlc[:], reads=[("cst", "mlast_cols")],
         writes=[("mlb",)])
    for z in range(4):
        S.op("V", "tensor_scalar_mul", Vm[:, z * 128:(z + 1) * 128],
             Vnat[:, (16 + z) * 128:(17 + z) * 128], mlc[:, z:z + 1],
             reads=[("Vnat", 16 + z), ("cst", "mlast_cols")],
             writes=[("Vm", z)])
    starts = list(range(0, 2049, 256))
    for wi, w in enumerate(starts):
        k0, k1 = w, min(w + 512, EXTN)
        q0, q1 = max(w, own_lo), min(w + 512, own_hi)
        NQ = q1 - q0
        kts = list(range(k0 // 128, k1 // 128))
        last_w = (wi == len(starts) - 1)
        emit_blend = wi >= 1
        emit_direct = False
        has_tails = (q1 >= w + 512)
        for h in range(32):
            hp = h % 2
            qw = qstream[hp]
            S.dma("Q", qw[:, 0:NQ],
                  qrT_sc[h * 128:(h + 1) * 128, q0 - own_off:q1 - own_off],
                  writes=[("qw", hp)])
            sg = sigb[hp]
            for zi, kt in enumerate(kts):
                pc = PSC[zi % 2]
                S.op("T", "matmul", pc[:, 0:NQ],
                     KrT[:, kt * 128:(kt + 1) * 128], qw[:, 0:NQ],
                     start=True, stop=True,
                     reads=[("KrT", kt), ("qw", hp)], writes=[("PSC", zi % 2)])
                S.op("S", "activation", sg[:, zi * 512:zi * 512 + NQ],
                     pc[:, 0:NQ], AF.Sigmoid,
                     reads=[("PSC", zi % 2)],
                     writes=[("sig", hp, zi), ("PSC", zi % 2)])
            for z in range(NQ // 128):
                for zi, kt in enumerate(kts):
                    rhs1 = (mlb[:, zi:zi + 1] if last_w else ones_bf[:])
                    rk = ("mlb",) if last_w else ("cst", "ones_bf")
                    S.op("T", "matmul", PRS[:, z:z + 1],
                         sg[:, zi * 512 + z * 128:zi * 512 + (z + 1) * 128],
                         rhs1,
                         start=(zi == 0), stop=(zi == len(kts) - 1),
                         reads=[("sig", hp, zi), rk],
                         writes=[("PRS",)])
            S.op("V", "reciprocal", rq[hp][:, 0:NQ // 128], PRS[:, 0:NQ // 128],
                 reads=[("PRS",)], writes=[("rq", hp), ("PRS",)])
            for zi, kt in enumerate(kts):
                vt = (Vm[:, zi * 128:(zi + 1) * 128] if last_w
                      else Vnat[:, kt * 128:(kt + 1) * 128])
                vk = ("Vm", zi) if last_w else ("Vnat", kt)
                S.op("T", "matmul", POT[:, 0:NQ], vt,
                     sg[:, zi * 512:zi * 512 + NQ],
                     start=(zi == 0), stop=(zi == len(kts) - 1),
                     reads=[vk, ("sig", hp, zi)], writes=[("POT",)])
            S.op("S", "copy", oTb[hp][:, 0:NQ], POT[:, 0:NQ],
                 reads=[("POT",)], writes=[("oTb", hp), ("POT",)])
            for z in range(NQ // 128):
                S.op("T", "transpose", TRC[hp][:, z * 128:(z + 1) * 128],
                     oTb[hp][:, z * 128:(z + 1) * 128], ident_bf[:],
                     reads=[("oTb", hp), ("cst", "ident_bf")],
                     writes=[("TRC", hp, z)])
                S.op("V", "tensor_scalar_mul", ocur[hp][:, z * 128:(z + 1) * 128],
                     TRC[hp][:, z * 128:(z + 1) * 128], rq[hp][:, z:z + 1],
                     reads=[("TRC", hp, z)],
                     writes=[("ocur", hp, z), ("TRC", hp, z)],
                     scalar_reads=[("rq", hp)])
            zoff = (q0 - w) // 128
            if emit_blend:
                AA = alF if wi == 1 else al
                AM = alFm if wi == 1 else alm
                ka = ("cst", ("alphaF_" if wi == 1 else "alpha_"))
                for z in range(2):
                    zz = z - zoff
                    sfx = "a" if z == 0 else "b"
                    S.op("V", "tensor_scalar_mul", tmpb[hp][:],
                         ocur[hp][:, zz * 128:(zz + 1) * 128],
                         AA[z][:],
                         reads=[("ocur", hp, zz), (ka[0], ka[1] + sfx)],
                         writes=[("tmpb", hp)])
                    S.op("V", "scalar_tensor_tensor",
                         ocatn[z][:, h * 128:(h + 1) * 128],
                         tails[:, h * 256 + z * 128:h * 256 + (z + 1) * 128],
                         AM[z][:], tmpb[hp][:], ALU.mult, ALU.add,
                         reads=[("tails", h, z), ("tmpb", hp)],
                         writes=[("ocatn", z, h)])
            if has_tails:
                for z in range(2):
                    zz = 2 + z - zoff
                    S.op("V", "tensor_copy",
                         tails[:, h * 256 + z * 128:h * 256 + (z + 1) * 128],
                         ocur[hp][:, zz * 128:(zz + 1) * 128],
                         reads=[("ocur", hp, zz)], writes=[("tails", h, z)])
        if emit_blend:
            blk = w - own_off
            for z in range(2):
                tcol = blk + z * 128
                for kg in range(4):
                    pt = TRC[kg % 2]
                    for k8 in range(8):
                        kt = kg * 8 + k8
                        S.op("T", "transpose", pt[:, k8 * 128:(k8 + 1) * 128],
                             ocatn[z][:, kt * 128:(kt + 1) * 128], ident_bf[:],
                             reads=[("ocatn", z, kt), ("cst", "ident_bf")],
                             writes=[("TRC", kg % 2, k8)])
                    dst = ocatT2[:].rearrange("p (k t) -> p k t", k=32)[
                        :, kg * 8:(kg + 1) * 8, tcol:tcol + 128]
                    S.op("V", "tensor_copy", dst,
                         pt[:].rearrange("p (k t) -> p k t", k=8),
                         reads=[("TRC", kg % 2, k8) for k8 in range(8)],
                         writes=[("ocT2", tcol, kg)] + [("TRC", kg % 2, k8)
                                                        for k8 in range(8)])


def emit_c3(nc, S, stack, half, aps, cst, ocatT2):
    own_off = OWN0
    y2_sc, woT, y3_sc = aps["y2_sc"], aps["l1woT"], aps["y3_sc"]
    sb = lambda nm, shape, dt: stack.enter_context(nc.sbuf_tensor(nm, shape, dt))
    ps = lambda nm, shape, dt: stack.enter_context(nc.psum_tensor(nm, shape, dt))
    wpan = [sb(f"c3_wpan{i}", [128, 32 * PANW], BF16) for i in range(2)]
    ys = [sb(f"c3_ys{i}", [128, PANW], F32) for i in range(2)]
    ev = [sb(f"c3_ev{i}", [128, PANW], F32) for i in range(2)]
    pmm = [ps(f"c3_pmm{i}", [128, 512], F32) for i in range(2)]

    def oc_sl(kt, ti):
        c = kt * 2048 + ti * 128
        return ocatT2[:, c:c + 128]

    for pi, c0 in enumerate(range(0, DM, PANW)):
        wb = pi % 2
        S.dma("P", wpan[wb][:].rearrange("p (k j) -> p k j", k=32),
              woT[:, c0:c0 + PANW].rearrange("(k p) j -> p k j", p=128),
              writes=[("wpan", wb)])
        for ti in range(16):
            tb = ti % 2
            r0 = own_off + ti * 128
            S.dma("Q", ys[tb][:], y2_sc[r0:r0 + 128, c0:c0 + PANW],
                  writes=[("ys", tb)])
            for kt in range(32):
                S.op("T", "matmul", pmm[tb][:, 0:PANW], oc_sl(kt, ti),
                     wpan[wb][:, kt * PANW:(kt + 1) * PANW],
                     start=(kt == 0), stop=(kt == 31),
                     reads=[("ocT2", "all"), ("wpan", wb)],
                     writes=[("pmm", tb)])
            S.op("V", "tensor_add", ev[tb][:], pmm[tb][:, 0:PANW], ys[tb][:],
                 reads=[("pmm", tb), ("ys", tb)],
                 writes=[("ev", tb), ("pmm", tb)])
            S.dma("Q", y3_sc[ti * 128:(ti + 1) * 128, c0:c0 + PANW], ev[tb][:],
                  reads=[("ev", tb)], writes=[("y3", ti, c0)])


def build_stage_c(half, serialize=False, strict_engines=("V", "S", "P"),
                  upto=4, ser_stages=()):
    nc = bass.Bass("TRN2", target_bir_lowering=False)
    cvals = host_constants_full()
    cnames = ["ident_bf", "ident_f32", "ones_bf", "alpha_a", "alpha_b",
              "alpha1m_a", "alpha1m_b", "alphaF_a", "alphaF_b",
              "alphaF1m_a", "alphaF1m_b", "mlast_cols"]
    cin = declare_constants(nc, cvals, cnames + ["cos1", "sin1"])
    aps = {
        "y2_sc": nc.dram_tensor("y2_sc", [EXTN, DM], F32,
                                kind="ExternalInput").ap(),
        "l1wqT": nc.dram_tensor("l1wqT", [DM, DM], F32,
                                kind="ExternalInput").ap(),
        "l1wkvT": nc.dram_tensor("l1wkvT", [DM, 256], F32,
                                 kind="ExternalInput").ap(),
        "l1woT": nc.dram_tensor("l1woT", [DM, DM], F32,
                                kind="ExternalInput").ap(),
        "qn_sc": nc.dram_tensor("qn_sc", [2048, DM], BF16, kind="Internal").ap(),
        "qrT_sc": nc.dram_tensor("qrT_sc", [DM, 2048], BF16,
                                 kind="Internal").ap(),
        "y3_sc": nc.dram_tensor("y3_sc", [2048, DM], F32,
                                kind="ExternalOutput").ap(),
        "cst_cos1": cin["cos1"], "cst_sin1": cin["sin1"],
    }
    with contextlib.ExitStack() as pstack:
      pool = SemPool(nc, pstack)
      with contextlib.ExitStack() as kvst:
        KrT = kvst.enter_context(nc.sbuf_tensor("c_KrT", [128, EXTN], BF16))
        Vnat = kvst.enter_context(nc.sbuf_tensor("c_Vnat", [128, EXTN], BF16))
        kvbufs = (KrT, Vnat)
        with contextlib.ExitStack() as st:
            S = Sched(nc, "c1", pool, serialize=serialize or ("c1" in ser_stages), strict_engines=strict_engines)
            cst = load_constants(nc, st, S, cin, ["ident_bf"])
            emit_c1(nc, S, st, half, aps, cst, kvbufs)
            S.finalize(st)
        if upto < 2:
            return nc, cvals, cnames + ["cos1", "sin1"]
        with contextlib.ExitStack() as st:
            S = Sched(nc, "c1b", pool, serialize=serialize or ("c1b" in ser_stages), strict_engines=strict_engines)
            cst = load_constants(nc, st, S, cin, ["ident_bf"])
            emit_c1b(nc, S, st, half, aps, cst)
            S.finalize(st)
        with contextlib.ExitStack() as ocst2:
            ocatT2 = ocst2.enter_context(
                nc.sbuf_tensor("c_ocT2", [128, 32 * 2048], BF16))
            if upto < 3:
                return nc, cvals, cnames + ["cos1", "sin1"]
            with contextlib.ExitStack() as st:
                S = Sched(nc, "c2", pool, serialize=serialize or ("c2" in ser_stages),
                          strict_engines=strict_engines)
                cst = load_constants(nc, st, S, cin, cnames)
                emit_c2(nc, S, st, half, aps, cst, kvbufs, ocatT2)
                S.finalize(st)
            if upto < 4:
                return nc, cvals, cnames + ["cos1", "sin1"]
            with contextlib.ExitStack() as st:
                S = Sched(nc, "c3", pool, serialize=serialize or ("c3" in ser_stages),
                          strict_engines=strict_engines)
                emit_c3(nc, S, st, half, aps, cst, ocatT2)
                S.finalize(st)
      pass
    return nc, cvals, cnames + ["cos1", "sin1"]


def host_c_inputs(inputs):
    wq = np.asarray(inputs["L1_attn_Wq"], np.float32)
    wk = np.asarray(inputs["L1_attn_Wk"], np.float32)
    wv = np.asarray(inputs["L1_attn_Wv"], np.float32)
    wo = np.asarray(inputs["L1_attn_Wo"], np.float32)
    return {
        "l1wqT": np.ascontiguousarray(wq.T),
        "l1wkvT": np.ascontiguousarray(
            np.concatenate([wk.T, wv.T], axis=1)),
        "l1woT": np.ascontiguousarray(wo.T),
    }



def emit_d(nc, S, stack, half, aps, cst):
    y3_sc, out_sc = aps["y3_sc"], aps["y4_out"]
    routerT, AT, BTs = aps["routerT"], aps["AT"], aps["BTs"]
    ident_bf = cst["ident_bf"]
    sb = lambda nm, shape, dt: stack.enter_context(nc.sbuf_tensor(nm, shape, dt))
    ps = lambda nm, shape, dt: stack.enter_context(nc.psum_tensor(nm, shape, dt))
    xa = [sb(f"d_xa{i}", [128, DM], F32) for i in range(2)]
    lnb = [sb(f"d_lnb{i}", [128, DM], BF16) for i in range(2)]
    lnT = sb("d_lnT", [128, 32 * 128], BF16)
    rtr = sb("d_rtr", [128, 32 * 4], BF16)
    atb = sb("d_at", [128, 32 * 64], BF16)
    btb = sb("d_bt", [64, DM], BF16)
    lg = sb("d_lg", [128, 4], F32)
    gcol = sb("d_gc", [128, 16], F32)
    gate = sb("d_gate", [128, 4], F32)
    midg = sb("d_midg", [128, 64], BF16)
    mT = sb("d_mT", [64, 128], BF16)
    ev = [sb(f"d_ev{i}", [128, 512], F32) for i in range(2)]
    colbuf = sb("d_col", [128, 8], F32)
    ptr = [ps(f"d_ptr{i}", [128, 1024], BF16) for i in range(2)]
    PL = ps("d_pl", [128, 512], F32)
    PM = ps("d_pm", [128, 512], F32)
    PD = [ps(f"d_pd{i}", [128, 512], F32) for i in range(2)]

    S.dma("P", rtr[:].rearrange("p (k j) -> p k j", k=32),
          routerT.rearrange("(k p) j -> p k j", p=128), writes=[("rtr",)])
    S.dma("P", atb[:].rearrange("p (k j) -> p k j", k=32),
          AT.rearrange("(k p) j -> p k j", p=128), writes=[("at",)])
    S.dma("P", btb[:], BTs, writes=[("bt",)])

    def lnT_sl(kt):
        return lnT[:, kt * 128:(kt + 1) * 128]

    for ti in range(16):
        tb = ti % 2
        xb, lb = xa[tb], lnb[tb]
        xkey, lnkey, ckey = ("xa", tb), ("lnb", tb), ("col", tb)
        cols3 = (colbuf[:, tb * 3:tb * 3 + 1], colbuf[:, tb * 3 + 1:tb * 3 + 2],
                 colbuf[:, tb * 3 + 2:tb * 3 + 3])
        S.dma("Q", xb[:], y3_sc[ti * 128:(ti + 1) * 128, :], writes=[xkey])
        emit_ln(S, xb[:], lb[:], cols3, xkey, lnkey, ckey)
        for kg in range(4):
            pt = ptr[kg % 2]
            for k8 in range(8):
                kt = kg * 8 + k8
                S.op("T", "transpose", pt[:, k8 * 128:(k8 + 1) * 128],
                     lb[:, kt * 128:(kt + 1) * 128], ident_bf[:],
                     reads=[lnkey, ("cst", "ident_bf")],
                     writes=[("ptr", kg % 2, k8)])
            dst = lnT[:].rearrange("p (k t) -> p k t", k=32)[
                :, kg * 8:(kg + 1) * 8, :]
            S.op("V", "tensor_copy", dst,
                 pt[:].rearrange("p (k t) -> p k t", k=8),
                 reads=[("ptr", kg % 2, k8) for k8 in range(8)],
                 writes=[("lnT", kg)] + [("ptr", kg % 2, k8)
                                         for k8 in range(8)])
        for kt in range(32):
            S.op("T", "matmul", PL[:, 0:4], lnT_sl(kt),
                 rtr[:, kt * 4:(kt + 1) * 4], start=(kt == 0), stop=(kt == 31),
                 reads=[("lnT", kt // 8), ("rtr",)], writes=[("PL",)])
        for kt in range(32):
            S.op("T", "matmul", PM[:, 0:64], lnT_sl(kt),
                 atb[:, kt * 64:(kt + 1) * 64], start=(kt == 0), stop=(kt == 31),
                 reads=[("lnT", kt // 8), ("at",)], writes=[("PM",)])
        nm1 = gcol[:, 0:1]
        nsec = gcol[:, 1:2]
        rsu = gcol[:, 2:3]
        S.op("V", "tensor_copy", lg[:], PL[:, 0:4],
             reads=[("PL",)], writes=[("lg",), ("PL",)])
        S.op("V", "reduce_max", nm1, lg[:], axis=AX.X, negate=True,
             reads=[("lg",)], writes=[("gc", 0)])
        S.op("V", "tensor_scalar_add", lg[:], lg[:], nm1,
             reads=[("lg",)], writes=[("lg",)], scalar_reads=[("gc", 0)])
        S.op("V", "tensor_scalar", gate[:], lg[:], 0.0, None, ALU.is_lt,
             reads=[("lg",)], writes=[("gate",)])
        S.op("V", "tensor_scalar", gate[:], gate[:], 1e30, 1e30,
             ALU.mult, ALU.subtract,
             reads=[("gate",)], writes=[("gate",)])
        S.op("V", "tensor_add", gate[:], gate[:], lg[:],
             reads=[("gate",), ("lg",)], writes=[("gate",)])
        S.op("V", "reduce_max", nsec, gate[:], axis=AX.X, negate=True,
             reads=[("gate",)], writes=[("gc", 1)])
        S.op("V", "tensor_scalar", gate[:], lg[:], nsec, None, ALU.add,
             reads=[("lg",)], writes=[("gate",)], scalar_reads=[("gc", 1)])
        S.op("V", "tensor_scalar", gate[:], gate[:], 0.0, None, ALU.is_ge,
             reads=[("gate",)], writes=[("gate",)])
        S.op("S", "activation", lg[:], lg[:], AF.Exp,
             reads=[("lg",)], writes=[("lg",)])
        S.op("V", "tensor_mul", gate[:], gate[:], lg[:],
             reads=[("gate",), ("lg",)], writes=[("gate",)])
        S.op("V", "reduce_sum", rsu, gate[:], axis=AX.X,
             reads=[("gate",)], writes=[("gc", 2)])
        S.op("V", "reciprocal", rsu, rsu,
             reads=[("gc", 2)], writes=[("gc", 2)])
        S.op("V", "tensor_scalar_mul", gate[:], gate[:], rsu,
             reads=[("gate",)], writes=[("gate",)], scalar_reads=[("gc", 2)])
        for e in range(4):
            S.op("V", "tensor_scalar_mul", midg[:, e * 16:(e + 1) * 16],
                 PM[:, e * 16:(e + 1) * 16], gate[:, e:e + 1],
                 reads=[("PM",)], writes=[("midg",)] + ([("PM",)] if e == 3 else []),
                 scalar_reads=[("gate",)])
        S.op("T", "transpose", ptr[0][0:64, 0:128], midg[:], ident_bf[:],
             reads=[("midg",), ("cst", "ident_bf")], writes=[("ptr", 0, 0)])
        S.op("V", "tensor_copy", mT[:], ptr[0][0:64, 0:128],
             reads=[("ptr", 0, 0)], writes=[("mT",), ("ptr", 0, 0)])
        for pi, c0 in enumerate(range(0, DM, 512)):
            pd = PD[pi % 2]
            S.op("T", "matmul", pd[:, 0:512], mT[:], btb[:, c0:c0 + 512],
                 start=True, stop=True,
                 reads=[("mT",), ("bt",)], writes=[("PD", pi % 2)])
            S.op("V", "scalar_tensor_tensor", ev[pi % 2][:], pd[:, 0:512],
                 4.0, xb[:, c0:c0 + 512], ALU.mult, ALU.add,
                 reads=[("PD", pi % 2), xkey],
                 writes=[("ev", pi % 2), ("PD", pi % 2)])
            S.dma("Q", out_sc[ti * 128:(ti + 1) * 128, c0:c0 + 512],
                  ev[pi % 2][:], reads=[("ev", pi % 2)],
                  writes=[("y4", ti, c0)])


def build_stage_d(half, serialize=False, strict_engines=("V", "S", "P")):
    nc = bass.Bass("TRN2", target_bir_lowering=False)
    cvals = host_constants_full()
    cnames = ["ident_bf"]
    cin = declare_constants(nc, cvals, cnames)
    aps = {
        "y3_sc": nc.dram_tensor("y3_sc", [2048, DM], F32,
                                kind="ExternalInput").ap(),
        "routerT": nc.dram_tensor("routerT", [DM, 4], F32,
                                  kind="ExternalInput").ap(),
        "AT": nc.dram_tensor("AT", [DM, 64], F32, kind="ExternalInput").ap(),
        "BTs": nc.dram_tensor("BTs", [64, DM], F32, kind="ExternalInput").ap(),
        "y4_out": nc.dram_tensor("y4_out", [2048, DM], F32,
                                 kind="ExternalOutput").ap(),
    }
    with contextlib.ExitStack() as pstack:
        pool = SemPool(nc, pstack)
        with contextlib.ExitStack() as st:
            S = Sched(nc, "d", pool, serialize=serialize,
                      strict_engines=strict_engines)
            cst = load_constants(nc, st, S, cin, cnames)
            emit_d(nc, S, st, half, aps, cst)
            S.finalize(st)
    return nc, cvals, cnames


def host_d_inputs(inputs):
    rw = np.asarray(inputs["L1_ffn_router_w"], np.float32)
    A = np.asarray(inputs["L1_ffn_A"], np.float32)
    B = np.asarray(inputs["L1_ffn_B"], np.float32)
    return {
        "routerT": np.ascontiguousarray(rw.T),
        "AT": np.ascontiguousarray(A.transpose(2, 0, 1).reshape(DM, 64)),
        "BTs": np.ascontiguousarray(B.transpose(0, 2, 1).reshape(64, DM)),
    }



def stage_feeds(cv, cn, half, extra):
    f = {("cst_" + nm): cv[nm] for nm in cn}
    for nm, v in percore_constants(cv, half).items():
        if ("cst_" + nm) in f:
            f["cst_" + nm] = v
    f.update(extra)
    return f


def kernel_multilaunch(**inputs):
    inputs = {k: np.asarray(v) for k, v in inputs.items()}
    x = inputs["L0_attn_hidden_states"].astype(np.float32, copy=False)
    cv = host_constants_full()
    hb = host_b_inputs(inputs)
    hc = host_c_inputs(inputs)
    hd = host_d_inputs(inputs)
    pairs = [(b, h) for b in range(4) for h in (0, 1)]
    cores = list(range(8))

    nc, _, cn = build_stage_a(0)
    feeds = [stage_feeds(cv, cn, h,
                         {"x": make_xeff(x[b], h),
                          "Wq": inputs["L0_attn_Wq"],
                          "Wk": inputs["L0_attn_Wk"],
                          "Wv": inputs["L0_attn_Wv"],
                          "Wo": inputs["L0_attn_Wo"]})
             for (b, h) in pairs]
    res = run_bass_kernel_spmd(nc, feeds, core_ids=cores)
    y1 = {p: res.results[i]["y1_sc"] for i, p in enumerate(pairs)}

    nc, _, cn = build_stage_b(0)
    feeds = [stage_feeds(cv, cn, h, {**hb, "y1_sc": y1[(b, h)]})
             for (b, h) in pairs]
    res = run_bass_kernel_spmd(nc, feeds, core_ids=cores)
    y2 = {p: res.results[i]["y2_sc"] for i, p in enumerate(pairs)}

    nc, _, cn = build_stage_c(0)
    feeds = [stage_feeds(cv, cn, h, {**hc, "y2_sc": y2[(b, h)]})
             for (b, h) in pairs]
    res = run_bass_kernel_spmd(nc, feeds, core_ids=cores)
    y3 = {p: res.results[i]["y3_sc"] for i, p in enumerate(pairs)}

    nc, _, cn = build_stage_d(0)
    feeds = [stage_feeds(cv, cn, h, {**hd, "y3_sc": y3[(b, h)]})
             for (b, h) in pairs]
    res = run_bass_kernel_spmd(nc, feeds, core_ids=cores)

    out = np.zeros((4, N, DM), np.float32)
    for i, (b, h) in enumerate(pairs):
        out[b, 2048 * h:2048 * h + 2048] = res.results[i]["y4_out"]
    return out



ALLCST = ["ident_bf", "ident_f32", "ones_bf", "cum_lhsT", "ones16_col",
          "gla_maskT", "swa_maskA", "swa_maskB", "cos0", "sin0", "cos1",
          "sin1", "alpha_a", "alpha_b", "alpha1m_a", "alpha1m_b",
          "alphaF_a", "alphaF_b", "alphaF1m_a", "alphaF1m_b", "mlast_cols"]


def build_full(serialize=False, strict_engines=("V", "S", "P")):
    nc = bass.Bass("TRN2", target_bir_lowering=False)
    cvals = host_constants_full()
    cin = declare_constants(nc, cvals, ALLCST)
    din = lambda nm, sh, dt: nc.dram_tensor(nm, sh, dt, kind="ExternalInput").ap()
    dint = lambda nm, sh, dt: nc.dram_tensor(nm, sh, dt, kind="Internal").ap()
    aps = {
        "x": din("x", [NV, DM], F32),
        "Wq": din("Wq", [DM, DM], F32),
        "Wk": din("Wk", [DM, 1024], F32),
        "Wv": din("Wv", [DM, 1024], F32),
        "Wo": din("Wo", [DM, DM], F32),
        "w1T_bits": din("w1T_bits", [DM, DM], I32),
        "w3T_bits": din("w3T_bits", [DM, DM], I32),
        "s0b": din("s0b", [128, DM], BF16),
        "s2cols": din("s2cols", [128, 32], F32),
        "s4b": din("s4b", [128, DM], BF16),
        "biasb": din("biasb", [128, DM], BF16),
        "l1wqT": din("l1wqT", [DM, DM], F32),
        "l1wkvT": din("l1wkvT", [DM, 256], F32),
        "l1woT": din("l1woT", [DM, DM], F32),
        "routerT": din("routerT", [DM, 4], F32),
        "AT": din("AT", [DM, 64], F32),
        "BTs": din("BTs", [64, DM], F32),
        "k_sc": dint("k_sc", [NV, 1024], F32),
        "v_sc": dint("v_sc", [NV, 1024], BF16),
        "q_sc": dint("q_sc", [EXTN, DM], BF16),
        "ocatT_sc": dint("ocatT_sc", [DM, EXTN], BF16),
        "y1_sc": dint("y1_sc", [EXTN, DM], F32),
        "hT_sc": dint("hT_sc", [DM, EXTN], BF16),
        "y2_sc": dint("y2_sc", [EXTN, DM], F32),
        "qn_sc": dint("qn_sc", [2048, DM], BF16),
        "qrT_sc": dint("qrT_sc", [DM, 2048], BF16),
        "y3_sc": dint("y3_sc", [2048, DM], F32),
        "y4_out": nc.dram_tensor("y4_out", [2048, DM], F32,
                                 kind="ExternalOutput").ap(),
        "cst_cos1": cin["cos1"], "cst_sin1": cin["sin1"],
    }
    SE = strict_engines
    with contextlib.ExitStack() as pstack:
        pool = SemPool(nc, pstack)

        def sched(nm):
            return Sched(nc, nm, pool, serialize=serialize, strict_engines=SE)

        with contextlib.ExitStack() as st:
            S = sched("a1")
            cst = load_constants(nc, st, S, cin, ["ident_bf"])
            emit_a1(nc, S, st, 0, aps, cst)
            S.finalize(st)
        with contextlib.ExitStack() as st:
            S = sched("a2")
            cst = load_constants(nc, st, S, cin,
                                 ["ident_bf", "cum_lhsT", "ones16_col",
                                  "gla_maskT", "swa_maskA", "swa_maskB"])
            cst["cos0_dram"] = cin["cos0"]
            cst["sin0_dram"] = cin["sin0"]
            emit_a2(nc, S, st, 0, aps, cst)
            S.finalize(st)
        with contextlib.ExitStack() as st:
            S = sched("a3")
            cst = load_constants(nc, st, S, cin, ["ident_bf"])
            emit_a3(nc, S, st, 0, aps, cst)
            S.finalize(st)
        with contextlib.ExitStack() as st:
            S = sched("b1")
            cst = load_constants(nc, st, S, cin, ["ident_bf", "ones_bf"])
            emit_b1(nc, S, st, 0, aps, cst)
            S.finalize(st)
        with contextlib.ExitStack() as st:
            S = sched("b2")
            cst = load_constants(nc, st, S, cin, ["ident_bf", "ones_bf"])
            emit_b2(nc, S, st, 0, aps, cst)
            S.finalize(st)
        with contextlib.ExitStack() as kvst:
            KrT = kvst.enter_context(nc.sbuf_tensor("c_KrT", [128, EXTN], BF16))
            Vnat = kvst.enter_context(nc.sbuf_tensor("c_Vnat", [128, EXTN],
                                                     BF16))
            kvbufs = (KrT, Vnat)
            with contextlib.ExitStack() as st:
                S = sched("c1")
                cst = load_constants(nc, st, S, cin, ["ident_bf"])
                emit_c1(nc, S, st, 0, aps, cst, kvbufs)
                S.finalize(st)
            with contextlib.ExitStack() as st:
                S = sched("c1b")
                cst = load_constants(nc, st, S, cin, ["ident_bf"])
                emit_c1b(nc, S, st, 0, aps, cst)
                S.finalize(st)
            with contextlib.ExitStack() as ocst2:
                ocatT2 = ocst2.enter_context(
                    nc.sbuf_tensor("c_ocT2", [128, 32 * 2048], BF16))
                with contextlib.ExitStack() as st:
                    S = sched("c2")
                    cst = load_constants(
                        nc, st, S, cin,
                        ["ident_bf", "ident_f32", "ones_bf", "alpha_a",
                         "alpha_b", "alpha1m_a", "alpha1m_b", "alphaF_a",
                         "alphaF_b", "alphaF1m_a", "alphaF1m_b",
                         "mlast_cols"])
                    emit_c2(nc, S, st, 0, aps, cst, kvbufs, ocatT2)
                    S.finalize(st)
                with contextlib.ExitStack() as st:
                    S = sched("c3")
                    emit_c3(nc, S, st, 0, aps, cst, ocatT2)
                    S.finalize(st)
        with contextlib.ExitStack() as st:
            S = sched("d")
            cst = load_constants(nc, st, S, cin, ["ident_bf"])
            emit_d(nc, S, st, 0, aps, cst)
            S.finalize(st)
    return nc, cvals


def kernel(**inputs):
    inputs = {k: np.asarray(v) for k, v in inputs.items()}
    x = inputs["L0_attn_hidden_states"].astype(np.float32, copy=False)
    cv = host_constants_full()
    hb = host_b_inputs(inputs)
    hc = host_c_inputs(inputs)
    hd = host_d_inputs(inputs)
    pairs = [(b, h) for b in range(4) for h in (0, 1)]
    nc, _ = build_full()
    feeds = []
    for (b, h) in pairs:
        f = {("cst_" + nm): cv[nm] for nm in ALLCST}
        for nm, v in percore_constants(cv, h).items():
            f["cst_" + nm] = v
        f.update({"x": make_xeff(x[b], h),
                  "Wq": inputs["L0_attn_Wq"], "Wk": inputs["L0_attn_Wk"],
                  "Wv": inputs["L0_attn_Wv"], "Wo": inputs["L0_attn_Wo"]})
        f.update(hb)
        f.update(hc)
        f.update(hd)
        feeds.append(f)
    res = run_bass_kernel_spmd(nc, feeds, core_ids=list(range(8)))
    out = np.zeros((4, N, DM), np.float32)
    for i, (b, h) in enumerate(pairs):
        out[b, 2048 * h:2048 * h + 2048] = res.results[i]["y4_out"]
    return out
